# version 05
# baseline (speedup 1.0000x reference)
import math
from contextlib import ExitStack

import numpy as np
import concourse.bass as bass
import concourse.mybir as mybir
from concourse.bass_utils import run_bass_kernel_spmd

F32 = mybir.dt.float32
BF16 = mybir.dt.bfloat16
AF = mybir.ActivationFunctionType
ALU = mybir.AluOpType
AX = mybir.AxisListType

HID = 1024
H = 8
DH = 64
VH = 128
E = 8
GAMMA = 1.0 - 2.0 ** (-5.0 - np.arange(H, dtype=np.float64))
C1G = 1.5957691216057308
C2G = 0.044715


def ctab_layout(T):
    NT = T // 128
    NDIFF = NT + 3
    o = {}
    c = 0
    o["ESC"] = c; c += H * NDIFF
    o["GNW"] = c; c += 8
    o["GNB"] = c; c += 8
    o["GB1"] = c; c += 8
    o["GB2"] = c; c += 4
    o["GB3"] = c; c += 1
    o["E0B1"] = c; c += 96
    o["E1B1"] = c; c += 48
    o["E1B2"] = c; c += 48
    o["E2B1"] = c; c += 48
    o["LNS"] = c; c += 8
    o["LNO"] = c; c += 8
    o["AFFB"] = c; c += 8
    o["W"] = c
    return o


class _PhaseDone(Exception):
    pass


class Res:
    __slots__ = ("w", "r")

    def __init__(self):
        self.w = None
        self.r = []


class Sched:
    def __init__(self):
        self.q = {}

    def add(self, name, sems, inc):
        self.q[name] = dict(sems=list(sems), inc=inc, idx=0,
                            uses=[0] * len(sems), hw={}, prog=[])

    def emit(self, qn, method, *args, reads=(), writes=(), **kw):
        q = self.q[qn]
        need = {}

        def _n(tag):
            if tag is None:
                return
            tq, slot, tc = tag
            if tq == qn == "pe":
                return
            k = (tq, slot)
            if need.get(k, 0) < tc:
                need[k] = tc

        for r in reads:
            _n(r.w)
        for w in writes:
            _n(w.w)
            for tag in w.r:
                _n(tag)
        alt = kw.pop("alt_q", None)
        qs = self.q[alt] if alt else q
        an = alt if alt else qn
        nsl = len(qs["sems"])
        slot = qs["idx"] % nsl
        if qs["uses"][slot] > 0:
            k = (an, slot)
            tc = qs["inc"] * qs["uses"][slot]
            if need.get(k, 0) < tc:
                need[k] = tc
        for k in sorted(need):
            tc = need[k]
            tq, tslot = k
            if q["hw"].get(k, 0) >= tc:
                continue
            q["prog"].append(("w", self.q[tq]["sems"][tslot], tc, None))
            q["hw"][k] = tc
        qs["idx"] += 1
        qs["uses"][slot] += 1
        q["prog"].append(("i", method, (args, kw), (an, slot, qs["inc"])))
        tag = (an, slot, qs["inc"] * qs["uses"][slot])
        for w in writes:
            w.w = tag
            w.r = []
        for r in reads:
            r.r.append(tag)

    def rawwait(self, qn, sem, val):
        self.q[qn]["prog"].append(("w", sem, val, None))

    def qdone(self, qn):
        q = self.q[qn]
        return [(q["sems"][s], q["inc"] * q["uses"][s])
                for s in range(len(q["sems"])) if q["uses"][s] > 0]

    def replay(self, qn, eng):
        q = self.q[qn]
        for kind, a, b, inc in q["prog"]:
            if kind == "w":
                eng.wait_ge(a, b)
            else:
                args, kw = b
                an, slot, v = inc
                getattr(eng, a)(*args, **kw).then_inc(self.q[an]["sems"][slot], v)


def build_nc(T, NCORES, dbg=False, phases=99, mode="full"):
    NT = T // 128
    NCH = T // 512
    ETOK = 1024 if T >= 1024 else T
    NG = T // ETOK
    ENC = ETOK // 512
    NTT = ETOK // 128
    NDIFF = NT + 3
    CO = ctab_layout(T)
    CTW = CO["W"]
    NROWS = NCORES * NT
    assert NROWS <= 128
    NTOT = NCORES * T
    KPOS = math.floor(0.9 * (NTOT - 1))
    FRAC = 0.9 * (NTOT - 1) - KPOS

    nc = bass.Bass("TRN2", target_bir_lowering=False)
    dt = nc.dram_tensor
    x_in = dt("x", [T, HID], F32, kind="ExternalInput").ap()
    wqT = dt("wqT", [HID, H * DH], F32, kind="ExternalInput").ap()
    wkT = dt("wkT", [HID, H * DH], F32, kind="ExternalInput").ap()
    wvT = dt("wvT", [HID, H * VH], F32, kind="ExternalInput").ap()
    wgT = dt("wgT", [HID, H * VH], F32, kind="ExternalInput").ap()
    woT = dt("woT", [H * VH, HID], F32, kind="ExternalInput").ap()
    gw1 = dt("gw1", [HID, HID], F32, kind="ExternalInput").ap()
    gw2 = dt("gw2", [HID, 512], F32, kind="ExternalInput").ap()
    gw3 = dt("gw3", [512, E], F32, kind="ExternalInput").ap()
    affwT = dt("affwT", [128, 64], F32, kind="ExternalInput").ap()
    e0w1 = dt("e0w1", [3, HID, 4096], F32, kind="ExternalInput").ap()
    e0w2 = dt("e0w2", [3, 4096, HID], F32, kind="ExternalInput").ap()
    e1w1 = dt("e1w1", [3, HID, 2048], F32, kind="ExternalInput").ap()
    e1w2 = dt("e1w2", [3, 2048, 2048], F32, kind="ExternalInput").ap()
    e1w3 = dt("e1w3", [3, 2048, HID], F32, kind="ExternalInput").ap()
    e2w1 = dt("e2w1", [2, HID, 3072], F32, kind="ExternalInput").ap()
    e2w2 = dt("e2w2", [2, 3072, HID], F32, kind="ExternalInput").ap()
    biasmat = dt("biasmat", [E, HID], F32, kind="ExternalInput").ap()
    ctab_d = dt("ctab", [128, CTW], F32, kind="ExternalInput").ap()
    trig_d = dt("trig", [64, 2 * T], F32, kind="ExternalInput").ap()
    masks_d = dt("masks", [128, 4 * 512], F32, kind="ExternalInput").ap()
    idn_d = dt("idn", [128, 128], F32, kind="ExternalInput").ap()
    noise_d = dt("noiseT", [E, T], F32, kind="ExternalInput").ap()
    eu_d = dt("eu", [1, E], F32, kind="ExternalInput").ap()
    epsr_d = dt("epsrow", [H, T], F32, kind="ExternalInput").ap()
    out_d = dt("out", [T, HID], F32, kind="ExternalOutput").ap()
    x1sp = dt("x1spill", [T, HID], F32).ap()
    if mode == "attn":
        x1out = dt("x1out", [T, HID], F32, kind="ExternalOutput").ap()
        impout = dt("impout", [NT, 128], F32, kind="ExternalOutput").ap()
    if mode == "ffn":
        x1in = dt("x1in", [T, HID], F32, kind="ExternalInput").ap()
        gimp = dt("gimp", [NROWS, 128], F32, kind="ExternalInput").ap()
    if dbg:
        d_ln0T = dt("d_ln0T", [128, 8 * T], F32).ap()
        d_q = dt("d_q", [128, 4 * T], F32).ap()
        d_k = dt("d_k", [128, 4 * T], F32).ap()
        d_v = dt("d_v", [128, 8 * T], F32).ap()
        d_g = dt("d_g", [128, 8 * T], F32).ap()
        d_of = dt("d_of", [128, 8 * T], F32).ap()
        d_ao = dt("d_ao", [128, 8 * T], F32).ap()
        d_sm = dt("d_sm", [128, 18], F32, kind="ExternalOutput").ap()
        d_xb = dt("d_xb", [128, 16], F32, kind="ExternalOutput").ap()
        d_s1 = dt("d_s1", [128, 1024], F32).ap()
    cc_in = dt("cc_in", [NT, 128], F32)
    cc_out = dt("cc_out", [NROWS, 128], F32)

    es = ExitStack()
    sb = lambda n, shp, d: es.enter_context(nc.sbuf_tensor(n, shp, d))
    mksem = lambda n: es.enter_context(nc.semaphore(n))

    AWW = max(8 * T, NT * 512 + 8192, NT * 1024)
    AW = sb("AW", [128, AWW], BF16)
    AQ = sb("AQ", [128, 4 * T], BF16)
    AK = sb("AK", [128, 4 * T], BF16)
    AVW = max(8 * T, 16 * ETOK)
    AV = sb("AV", [128, AVW], BF16)
    AG = sb("AG", [128, AVW], BF16)
    WB = sb("WB", [128, 4096], BF16)
    S1 = sb("S1", [128, 2 * T], BF16)
    S2 = sb("S2", [128, T], F32)
    S3 = sb("S3", [128, T], F32)
    V1 = sb("V1", [1, T], F32)
    V2 = sb("V2", [1, T], F32)
    SM = sb("SM", [128, 112], F32)
    PRB = sb("PRB", [128, NT * E], F32)
    GI = sb("GI", [128, 128], F32)
    LGS = sb("LGS", [E, 512], F32)
    CT = sb("CT", [128, CTW], F32)
    TRG = sb("TRG", [64, 2 * T], BF16)
    MSK = sb("MSK", [128, 4 * 512], BF16)
    IDB = sb("IDB", [128, 128], BF16)
    IDF = sb("IDF", [128, 128], F32)
    ONEB = sb("ONEB", [128, 1], BF16)
    ONEF = sb("ONEF", [128, 1], F32)
    ONRF = sb("ONRF", [1, 128], F32)
    BM = sb("BM", [E, HID], BF16)
    W3B = sb("W3B", [128, 4 * E], BF16)
    AFW = sb("AFW", [128, 64], F32)
    EUS = sb("EUS", [1, 16], F32)
    SC1 = sb("SC1", [1, 160], F32)
    SCB = sb("SCB", [128, 8], F32)
    NZE = sb("NZE", [128, 8], F32)
    NEGT = sb("NEGT", [128, 8], F32)
    PRT = sb("PRT", [E, T], BF16)

    ps_es = ExitStack()
    P = [ps_es.enter_context(nc.psum_tensor(f"p{i}", [128, 512], F32)) for i in range(6)]
    PB = ps_es.enter_context(nc.psum_tensor("pb", [128, 1024], BF16))
    PS = ps_es.enter_context(nc.psum_tensor("pst", [1, 512], F32))

    sd = Sched()
    R = {}

    def rs(*names):
        out = []
        for n in names:
            if n not in R:
                R[n] = Res()
            out.append(R[n])
        return out

    (rAW, rAQ, rAK, rAV, rAG, rWB0, rWB1, rS1, rS2, rS3a, rS3b, rV1, rV2, rSM,
     rPRB, rGI, rLGS, rCT, rTRG, rMSK, rIDB, rIDF, rONEB, rONEF, rONRF, rBM,
     rW3B, rAFW, rEUS, rSC1, rSCB, rNZE, rNEGT, rPRT, rX1SP, rCCI, rCCO, rOUT, rPBp,
     rPSp) = rs("AW", "AQ", "AK", "AV", "AG", "WB0", "WB1", "S1", "S2", "S3a",
                "S3b", "V1", "V2", "SM", "PRB", "GI", "LGS", "CT", "TRG", "MSK",
                "IDB", "IDF", "ONEB", "ONEF", "ONRF", "BM", "W3B", "AFW", "EUS",
                "SC1", "SCB", "NZE", "NEGT", "PRT", "X1SP", "CCI", "CCO", "OUT", "PBp",
                "PSp")
    rP = rs(*[f"P{i}" for i in range(6)])
    (res_impout,) = rs("IMPOUT")
    rPO = rs(*[f"PO{i}" for i in range(8)])
    rWB = [rWB0, rWB1]
    S3_HALF = [rS3a, rS3b]

    try:
     if True:
        for n in ("pe", "dve", "act"):
            sd.add(n, [mksem("s_" + n)], 1)
        sd.add("gd", [mksem(f"s_gd{i}") for i in range(8)], 16)
        sd.add("sy", [mksem(f"s_sy{i}") for i in range(8)], 16)
        sd.add("cc", [mksem("s_cc")], 16)
        E_ = sd.emit

        def mm(out_ap, lhsT_ap, rhs_ap, rd, wr, start, stop):
            E_("pe", "matmul", reads=tuple(rd), writes=tuple(wr),
               out=out_ap, lhsT=lhsT_ap, rhs=rhs_ap, start=start, stop=stop)

        E_("sy", "dma_start", reads=(), writes=(rCT,), out=CT[:, :], in_=ctab_d[:, :])
        E_("gd", "dma_start", reads=(), writes=(rTRG,), out=TRG[:, :], in_=trig_d[:, :])
        E_("gd", "dma_start", reads=(), writes=(rMSK,), out=MSK[:, :], in_=masks_d[:, :])
        E_("gd", "dma_start", reads=(), writes=(rIDB,), out=IDB[:, :], in_=idn_d[:, :])
        E_("sy", "dma_start", reads=(), writes=(rIDF,), out=IDF[:, :], in_=idn_d[:, :])
        E_("gd", "dma_start", reads=(), writes=(rBM,), out=BM[:, :], in_=biasmat[:, :])
        E_("gd", "dma_start", reads=(), writes=(rW3B,), out=W3B[:, :],
           in_=gw3.rearrange("(kt p) e -> p kt e", p=128))
        E_("sy", "dma_start", reads=(), writes=(rAFW,), out=AFW[:, :], in_=affwT[:, :])
        E_("sy", "dma_start", reads=(), writes=(rEUS,), out=EUS[:, 0:E], in_=eu_d[:, :])
        E_("dve", "memset", reads=(), writes=(rONEB,), ap=ONEB[:, :], constant=1.0)
        E_("dve", "memset", reads=(), writes=(rONEF,), ap=ONEF[:, :], constant=1.0)
        E_("dve", "memset", reads=(), writes=(rONRF,), ap=ONRF[:, :], constant=1.0)
        E_("dve", "memset", reads=(), writes=(rNZE,), ap=NZE[:, :], constant=0.0)
        E_("dve", "memset", reads=(rNZE,), writes=(rNZE,), ap=NZE[:, 0:2], constant=0.125)
        E_("dve", "memset", reads=(), writes=(rNEGT,), ap=NEGT[:, :], constant=-1e9)

        def ln_stats(src_ap, src_res):
            E_("dve", "bn_stats", reads=(src_res,), writes=(rSM,),
               out=SM[:, 80:86], in_=src_ap[:, 0:512])
            E_("dve", "bn_stats", reads=(src_res,), writes=(rSM,),
               out=SM[:, 86:92], in_=src_ap[:, 512:1024])
            E_("dve", "bn_aggr", reads=(rSM,), writes=(rSM,),
               out=SM[:, 92:94], in_=SM[:, 80:92])
            E_("dve", "tensor_scalar", reads=(rSM,), writes=(rSM,),
               out=SM[:, 94:95], in0=SM[:, 93:94], scalar1=1e-5, scalar2=None, op0=ALU.add)
            E_("act", "activation", reads=(rSM,), writes=(rSM,),
               out=SM[:, 95:96], in_=SM[:, 94:95], func=AF.Ln)
            E_("act", "activation", reads=(rSM,), writes=(rSM,),
               out=SM[:, 96:97], in_=SM[:, 95:96], func=AF.Exp, scale=-0.5)
            E_("dve", "scalar_tensor_tensor", reads=(rSM,), writes=(rSM,),
               out=SM[:, 97:98], in0=SM[:, 92:93], scalar=-1.0, in1=SM[:, 96:97],
               op0=ALU.mult, op1=ALU.mult)

        skip_attn = (mode == "ffn")
        if not skip_attn:
            for i in range(NT):
                hb = i % 2
                xbuf = S3[:, hb * 1024:hb * 1024 + 1024]
                E_("sy", "dma_start", reads=(), writes=(S3_HALF[hb],),
                   out=xbuf, in_=x_in[i * 128:(i + 1) * 128, :])
                ln_stats(xbuf, S3_HALF[hb])
                E_("act", "activation", reads=(S3_HALF[hb], rSM), writes=(rS1,),
                   out=S1[:, 0:1024], in_=xbuf, func=AF.Identity, scale=SM[:, 96:97],
                   bias=SM[:, 97:98])
                if dbg and i == 0:
                    E_("gd", "dma_start", reads=(rSM,), writes=(), out=d_sm[:, :],
                       in_=SM[:, 80:98])
                    E_("gd", "dma_start", reads=(S3_HALF[0],), writes=(), out=d_xb[:, :],
                       in_=S3[:, 0:16])
                    E_("gd", "dma_start", reads=(rS1,), writes=(), out=d_s1[:, :],
                       in_=S1[:, 0:1024])
                for db in range(8):
                    E_("pe", "transpose", reads=(rS1, rIDB), writes=(rPBp,),
                       out=PB[:, 0:128], in_=S1[:, db * 128:(db + 1) * 128], identity=IDB[:, :])
                    E_("dve", "tensor_copy", reads=(rPBp,), writes=(rAW,),
                       out=AW[:, db * T + i * 128: db * T + (i + 1) * 128], in_=PB[:, 0:128])

            if dbg:
                E_("gd", "dma_start", reads=(rAW,), writes=(), out=d_ln0T[:, :],
                   in_=AW[:, 0:8 * T])
            if phases < 2:
                raise _PhaseDone()
            def rotary(dest, rdest, mt, src32_res):
                cosF = TRG[0:64, 0:T]
                sinF = TRG[0:64, T:2 * T]
                for par in range(2):
                    if par == 1:
                        E_("dve", "tensor_copy", reads=(src32_res,), writes=(src32_res,),
                           out=S2[0:64, :], in_=S2[64:128, :])
                    E_("dve", "tensor_scalar", reads=(src32_res,), writes=(rS3a, rS3b),
                       out=S3[0:32, :], in0=S2[32:64, :], scalar1=-1.0, scalar2=None,
                       op0=ALU.mult)
                    E_("dve", "tensor_copy", reads=(src32_res,), writes=(rS3a, rS3b),
                       out=S3[32:64, :], in_=S2[0:32, :])
                    E_("dve", "tensor_tensor", reads=(rS3a, rS3b, rTRG), writes=(rS3a, rS3b),
                       out=S3[0:64, :], in0=S3[0:64, :], in1=sinF, op=ALU.mult)
                    E_("dve", "tensor_tensor", reads=(src32_res, rTRG), writes=(src32_res,),
                       out=S2[0:64, :], in0=S2[0:64, :], in1=cosF, op=ALU.mult)
                    E_("dve", "tensor_tensor", reads=(src32_res, rS3a, rS3b), writes=(rdest,),
                       out=dest[64 * par:64 * par + 64, mt * T:(mt + 1) * T],
                       in0=S2[0:64, :], in1=S3[0:64, :], op=ALU.add)

            for wap, dest, rdest in ((wqT, AQ, rAQ), (wkT, AK, rAK)):
                E_("gd", "dma_start", reads=(), writes=(rWB0, rWB1),
                   out=WB[:, 0:4096], in_=wap.rearrange("(kt p) m -> p kt m", p=128))
                for mt in range(4):
                    for c in range(NCH):
                        pi = c % 2
                        for kt in range(8):
                            mm(P[pi][:, :], WB[:, kt * 512 + mt * 128: kt * 512 + (mt + 1) * 128],
                               AW[:, kt * T + c * 512: kt * T + (c + 1) * 512],
                               rd=[rWB0, rWB1, rAW], wr=[rP[pi]], start=(kt == 0), stop=(kt == 7))
                        E_("act", "activation", reads=(rP[pi],), writes=(rS2,),
                           out=S2[:, c * 512:(c + 1) * 512], in_=P[pi][:, :], func=AF.Identity)
                    rotary(dest, rdest, mt, rS2)
            for half in range(2):
                E_("gd", "dma_start", reads=(), writes=(rWB0, rWB1),
                   out=WB[:, 0:4096],
                   in_=wvT[:, half * 512:(half + 1) * 512].rearrange("(kt p) m -> p kt m", p=128))
                for tt in range(NT):
                    pi = tt % 2
                    for kt in range(8):
                        mm(P[pi][:, :], AW[:, kt * T + tt * 128: kt * T + (tt + 1) * 128],
                           WB[:, kt * 512:(kt + 1) * 512],
                           rd=[rWB0, rWB1, rAW], wr=[rP[pi]], start=(kt == 0), stop=(kt == 7))
                    E_("act", "activation", reads=(rP[pi],), writes=(rAV,),
                       out=AV[:, tt * 1024 + half * 512: tt * 1024 + (half + 1) * 512],
                       in_=P[pi][:, :], func=AF.Identity)
            for half in range(2):
                E_("gd", "dma_start", reads=(), writes=(rWB0, rWB1),
                   out=WB[:, 0:4096],
                   in_=wgT[:, half * 512:(half + 1) * 512].rearrange("(kt p) m -> p kt m", p=128))
                for mt in range(4):
                    gmt = half * 4 + mt
                    for c in range(NCH):
                        pi = 2 + c % 2
                        for kt in range(8):
                            mm(P[pi][:, :],
                               WB[:, kt * 512 + mt * 128: kt * 512 + (mt + 1) * 128],
                               AW[:, kt * T + c * 512: kt * T + (c + 1) * 512],
                               rd=[rWB0, rWB1, rAW], wr=[rP[pi]], start=(kt == 0), stop=(kt == 7))
                        E_("act", "activation", reads=(rP[pi],), writes=(rAG,),
                           out=AG[:, gmt * T + c * 512: gmt * T + (c + 1) * 512],
                           in_=P[pi][:, :], func=AF.Identity)
            for gmt in range(8):
                E_("act", "activation", reads=(rAG,), writes=(rAG,),
                   out=AG[:, gmt * T:(gmt + 1) * T], in_=AG[:, gmt * T:(gmt + 1) * T],
                   func=AF.Silu)

            if dbg:
                E_("gd", "dma_start", reads=(rAQ,), writes=(), out=d_q[:, :], in_=AQ[:, :])
                E_("gd", "dma_start", reads=(rAK,), writes=(), out=d_k[:, :], in_=AK[:, :])
                E_("gd", "dma_start", reads=(rAV,), writes=(), out=d_v[:, :], in_=AV[:, 0:8 * T])
                E_("gd", "dma_start", reads=(rAG,), writes=(), out=d_g[:, :], in_=AG[:, 0:8 * T])
            if phases < 3:
                raise _PhaseDone()
            WOC = AWW - 8192
            E_("gd", "dma_start", reads=(rAW,), writes=(rAW,),
               out=AW[:, WOC:WOC + 8192], in_=woT.rearrange("(ct p) o -> p ct o", p=128))
            for h in range(H):
                mt, par = h // 2, h % 2
                base = 64 * par
                E_("sy", "dma_start", reads=(), writes=(rS3a, rS3b),
                   out=S3[0:1, 0:T], in_=epsr_d[h:h + 1, :])
                for c in range(NCH):
                    nst = 4 * c + 4
                    for st in range(nst):
                        pi = st % 2
                        di = 4 * c - st + 3
                        mm(P[pi][:, :], AK[base:base + 64, mt * T + st * 128: mt * T + (st + 1) * 128],
                           AQ[base:base + 64, mt * T + c * 512: mt * T + (c + 1) * 512],
                           rd=[rAK, rAQ], wr=[rP[pi]], start=True, stop=True)
                        E_("act", "activation", reads=(rP[pi], rCT), writes=(rAW,),
                           out=AW[:, st * 512:(st + 1) * 512], in_=P[pi][:, :], func=AF.Identity,
                           scale=CT[:, CO["ESC"] + h * NDIFF + di: CO["ESC"] + h * NDIFF + di + 1])
                        if st >= 4 * c:
                            k = st - 4 * c
                            E_("dve", "tensor_tensor", reads=(rAW, rMSK), writes=(rAW,),
                               out=AW[:, st * 512:(st + 1) * 512],
                               in0=AW[:, st * 512:(st + 1) * 512],
                               in1=MSK[:, k * 512:(k + 1) * 512], op=ALU.mult)
                    po = 4 + (c % 2)
                    for st in range(nst):
                        mm(P[po][:, :], AV[:, st * 1024 + h * 128: st * 1024 + h * 128 + 128],
                           AW[:, st * 512:(st + 1) * 512],
                           rd=[rAV, rAW], wr=[rP[po]], start=(st == 0), stop=(st == nst - 1))
                    E_("act", "activation", reads=(rP[po],), writes=(rS1,),
                       out=S1[:, c * 512:(c + 1) * 512], in_=P[po][:, :], func=AF.Identity)
                E_("act", "activation", reads=(rS1,), writes=(rS1,),
                   out=S1[:, T:2 * T], in_=S1[:, 0:T], func=AF.Square)
                for c in range(NCH):
                    mm(PS[:, :], ONEB[:, :], S1[:, c * 512:(c + 1) * 512],
                       rd=[rONEB, rS1], wr=[rPSp], start=True, stop=True)
                    E_("act", "activation", reads=(rPSp,), writes=(rV1,),
                       out=V1[:, c * 512:(c + 1) * 512], in_=PS[:, :], func=AF.Identity)
                    mm(PS[:, :], ONEB[:, :], S1[:, T + c * 512: T + (c + 1) * 512],
                       rd=[rONEB, rS1], wr=[rPSp], start=True, stop=True)
                    E_("act", "activation", reads=(rPSp,), writes=(rV2,),
                       out=V2[:, c * 512:(c + 1) * 512], in_=PS[:, :], func=AF.Identity)
                V3 = S2[0:1, 0:T]
                E_("dve", "tensor_scalar", reads=(rV1,), writes=(rV1,),
                   out=V1[:, :], in0=V1[:, :], scalar1=1.0 / VH, scalar2=None, op0=ALU.mult)
                E_("dve", "tensor_scalar", reads=(rV2,), writes=(rV2,),
                   out=V2[:, :], in0=V2[:, :], scalar1=1.0 / VH, scalar2=None, op0=ALU.mult)
                E_("dve", "tensor_tensor", reads=(rV1,), writes=(rS2,),
                   out=V3, in0=V1[:, :], in1=V1[:, :], op=ALU.mult)
                E_("dve", "tensor_tensor", reads=(rV2, rS2), writes=(rV2,),
                   out=V2[:, :], in0=V2[:, :], in1=V3, op=ALU.subtract)
                E_("dve", "tensor_tensor", reads=(rV2, rS3a, rS3b), writes=(rV2,),
                   out=V2[:, :], in0=V2[:, :], in1=S3[0:1, 0:T], op=ALU.add)
                E_("act", "activation", reads=(rV2,), writes=(rV2,),
                   out=V2[:, :], in_=V2[:, :], func=AF.Ln)
                E_("act", "activation", reads=(rV2,), writes=(rV2,),
                   out=V2[:, :], in_=V2[:, :], func=AF.Exp, scale=-0.5)
                E_("dve", "scalar_tensor_tensor", reads=(rV1, rV2), writes=(rS2,),
                   out=V3, in0=V1[:, :], scalar=-1.0, in1=V2[:, :],
                   op0=ALU.mult, op1=ALU.mult)
                for c in range(NCH):
                    mm(P[0][:, :], ONRF[:, :], V2[:, c * 512:(c + 1) * 512],
                       rd=[rONRF, rV2], wr=[rP[0]], start=True, stop=True)
                    mm(P[1][:, :], ONRF[:, :], S2[0:1, c * 512:(c + 1) * 512],
                       rd=[rONRF, rS2], wr=[rP[1]], start=True, stop=True)
                    tmp = S1[:, T + c * 512: T + (c + 1) * 512]
                    E_("dve", "tensor_tensor", reads=(rS1, rP[0]), writes=(rS1,),
                       out=tmp, in0=S1[:, c * 512:(c + 1) * 512], in1=P[0][:, :], op=ALU.mult)
                    E_("dve", "tensor_tensor", reads=(rS1, rP[1]), writes=(rS1,),
                       out=tmp, in0=tmp, in1=P[1][:, :], op=ALU.add)
                    E_("act", "activation", reads=(rS1, rCT), writes=(rS1,),
                       out=tmp, in_=tmp, func=AF.Identity,
                       scale=CT[:, CO["GNW"] + h:CO["GNW"] + h + 1],
                       bias=CT[:, CO["GNB"] + h:CO["GNB"] + h + 1])
                    E_("dve", "tensor_tensor", reads=(rS1, rAG), writes=(rAG,),
                       out=AG[:, h * T + c * 512: h * T + (c + 1) * 512],
                       in0=tmp, in1=AG[:, h * T + c * 512: h * T + (c + 1) * 512],
                       op=ALU.mult)

            if dbg:
                E_("gd", "dma_start", reads=(rAG,), writes=(), out=d_of[:, :], in_=AG[:, 0:8 * T])
            if phases < 4:
                raise _PhaseDone()
            for m in range(8):
                dest, dres = (AQ, rAQ) if m < 4 else (AK, rAK)
                for c in range(NCH):
                    pi = c % 2
                    for h in range(H):
                        mm(P[pi][:, :],
                           AW[:, WOC + h * 1024 + m * 128: WOC + h * 1024 + (m + 1) * 128],
                           AG[:, h * T + c * 512: h * T + (c + 1) * 512],
                           rd=[rAW, rAG], wr=[rP[pi]], start=(h == 0), stop=(h == 7))
                    E_("act", "activation", reads=(rP[pi],), writes=(dres,),
                       out=dest[:, (m % 4) * T + c * 512:(m % 4) * T + (c + 1) * 512],
                       in_=P[pi][:, :], func=AF.Identity)
            if dbg:
                E_("gd", "dma_start", reads=(rAQ,), writes=(), out=d_ao[:, 0:4 * T], in_=AQ[:, :])
                E_("gd", "dma_start", reads=(rAK,), writes=(), out=d_ao[:, 4 * T:8 * T], in_=AK[:, :])
            for c in range(NCH):
                for m in range(8):
                    src_t, sres = (AQ, rAQ) if m < 4 else (AK, rAK)
                    sl = src_t[:, (m % 4) * T + c * 512:(m % 4) * T + (c + 1) * 512]
                    mm(PS[:, :], ONEB[:, :], sl, rd=[rONEB, sres], wr=[rPSp],
                       start=(m == 0), stop=(m == 7))
                E_("act", "activation", reads=(rPSp,), writes=(rV1,),
                   out=V1[:, c * 512:(c + 1) * 512], in_=PS[:, :], func=AF.Identity)
                for m in range(8):
                    src_t, sres = (AQ, rAQ) if m < 4 else (AK, rAK)
                    sl = src_t[:, (m % 4) * T + c * 512:(m % 4) * T + (c + 1) * 512]
                    E_("act", "activation", reads=(sres,), writes=(rS1,),
                       out=S1[:, 0:512], in_=sl, func=AF.Square)
                    mm(PS[:, :], ONEB[:, :], S1[:, 0:512], rd=[rONEB, rS1], wr=[rPSp],
                       start=(m == 0), stop=(m == 7))
                E_("act", "activation", reads=(rPSp,), writes=(rV2,),
                   out=V2[:, c * 512:(c + 1) * 512], in_=PS[:, :], func=AF.Identity)
            V3 = S2[0:1, 0:T]
            E_("dve", "tensor_scalar", reads=(rV1,), writes=(rV1,),
               out=V1[:, :], in0=V1[:, :], scalar1=1.0 / HID, scalar2=None, op0=ALU.mult)
            E_("dve", "tensor_scalar", reads=(rV2,), writes=(rV2,),
               out=V2[:, :], in0=V2[:, :], scalar1=1.0 / HID, scalar2=None, op0=ALU.mult)
            E_("dve", "tensor_tensor", reads=(rV1,), writes=(rS2,),
               out=V3, in0=V1[:, :], in1=V1[:, :], op=ALU.mult)
            E_("dve", "tensor_tensor", reads=(rV2, rS2), writes=(rV2,),
               out=V2[:, :], in0=V2[:, :], in1=V3, op=ALU.subtract)
            E_("dve", "tensor_scalar", reads=(rV2,), writes=(rV2,),
               out=V2[:, :], in0=V2[:, :], scalar1=1e-5, scalar2=None, op0=ALU.add)
            E_("act", "activation", reads=(rV2,), writes=(rV2,),
               out=V2[:, :], in_=V2[:, :], func=AF.Ln)
            E_("act", "activation", reads=(rV2,), writes=(rV2,),
               out=V2[:, :], in_=V2[:, :], func=AF.Exp, scale=-0.5)
            E_("dve", "scalar_tensor_tensor", reads=(rV1, rV2), writes=(rS2,),
               out=V3, in0=V1[:, :], scalar=-1.0, in1=V2[:, :], op0=ALU.mult, op1=ALU.mult)
            for i in range(NT):
                E_("pe", "transpose", reads=(rV2, rIDF), writes=(rP[0],),
                   out=P[0][:, 0:1], in_=V2[:, i * 128:(i + 1) * 128], identity=IDF[0:1, 0:1])
                E_("act", "activation", reads=(rP[0],), writes=(rSM,),
                   out=SM[:, 16 + i:17 + i], in_=P[0][:, 0:1], func=AF.Identity)
                E_("pe", "transpose", reads=(rS2, rIDF), writes=(rP[1],),
                   out=P[1][:, 0:1], in_=S2[0:1, i * 128:(i + 1) * 128], identity=IDF[0:1, 0:1])
                E_("act", "activation", reads=(rP[1],), writes=(rSM,),
                   out=SM[:, 32 + i:33 + i], in_=P[1][:, 0:1], func=AF.Identity)
            for i in range(NT):
                hb = i % 2
                xbuf = S3[:, hb * 1024:hb * 1024 + 1024]
                E_("sy", "dma_start", reads=(), writes=(S3_HALF[hb],),
                   out=xbuf, in_=x_in[i * 128:(i + 1) * 128, :])
                for m in range(8):
                    src_t, sres = (AQ, rAQ) if m < 4 else (AK, rAK)
                    E_("pe", "transpose", reads=(sres, rIDB), writes=(rPBp,),
                       out=PB[:, (m % 2) * 128:(m % 2) * 128 + 128],
                       in_=src_t[:, (m % 4) * T + i * 128:(m % 4) * T + (i + 1) * 128],
                       identity=IDB[:, :])
                    E_("act", "activation", reads=(rPBp, rSM), writes=(rS1,),
                       out=S1[:, m * 128:(m + 1) * 128],
                       in_=PB[:, (m % 2) * 128:(m % 2) * 128 + 128],
                       func=AF.Identity, scale=SM[:, 16 + i:17 + i], bias=SM[:, 32 + i:33 + i])
                x1buf = S2[:, 0:1024]
                E_("dve", "tensor_tensor", reads=(S3_HALF[hb], rS1), writes=(rS2,),
                   out=x1buf, in0=xbuf, in1=S1[:, 0:1024], op=ALU.add)
                ln_stats(x1buf, rS2)
                E_("dve", "reciprocal", reads=(rSM,), writes=(rSM,),
                   out=SM[:, 98:99], in_=SM[:, 94:95])
                E_("dve", "tensor_tensor", reads=(rSM,), writes=(rSM,),
                   out=SM[:, 99:100], in0=SM[:, 93:94], in1=SM[:, 98:99], op=ALU.mult)
                E_("act", "activation", reads=(rSM,), writes=(rSM,),
                   out=SM[:, 64 + i:65 + i], in_=SM[:, 99:100], func=AF.Sqrt, scale=float(HID))
                E_("act", "activation", reads=(rS2, rSM), writes=(rS1, rSM),
                   out=S1[:, 0:1024], in_=x1buf, func=AF.Abs, scale=SM[:, 96:97],
                   bias=SM[:, 97:98], accum_out=SM[:, i:i + 1])
                E_("dve", "tensor_copy", reads=(rSM,), writes=(rSM,),
                   out=SM[:, 32 + i:33 + i], in_=SM[:, 96:97])
                E_("dve", "tensor_copy", reads=(rSM,), writes=(rSM,),
                   out=SM[:, 48 + i:49 + i], in_=SM[:, 97:98])
                E_("sy", "dma_start", reads=(rS2,), writes=(rX1SP,),
                   out=(x1out if mode == "attn" else x1sp)[i * 128:(i + 1) * 128, :],
                   in_=x1buf)

            if phases < 5:
                raise _PhaseDone()
        if mode == "ffn":
            for i in range(NT):
                hb = i % 2
                xbuf = S3[:, hb * 1024:hb * 1024 + 1024]
                E_("sy", "dma_start", reads=(), writes=(S3_HALF[hb],),
                   out=xbuf, in_=x1in[i * 128:(i + 1) * 128, :])
                ln_stats(xbuf, S3_HALF[hb])
                E_("dve", "reciprocal", reads=(rSM,), writes=(rSM,),
                   out=SM[:, 98:99], in_=SM[:, 94:95])
                E_("dve", "tensor_tensor", reads=(rSM,), writes=(rSM,),
                   out=SM[:, 99:100], in0=SM[:, 93:94], in1=SM[:, 98:99], op=ALU.mult)
                E_("act", "activation", reads=(rSM,), writes=(rSM,),
                   out=SM[:, 64 + i:65 + i], in_=SM[:, 99:100], func=AF.Sqrt,
                   scale=float(HID))
                E_("act", "activation", reads=(S3_HALF[hb], rSM), writes=(rS1, rSM),
                   out=S1[:, 0:1024], in_=xbuf, func=AF.Abs, scale=SM[:, 96:97],
                   bias=SM[:, 97:98], accum_out=SM[:, i:i + 1])
                E_("dve", "tensor_copy", reads=(rSM,), writes=(rSM,),
                   out=SM[:, 32 + i:33 + i], in_=SM[:, 96:97])
                E_("dve", "tensor_copy", reads=(rSM,), writes=(rSM,),
                   out=SM[:, 48 + i:49 + i], in_=SM[:, 97:98])
        if mode != "ffn":
            E_("pe", "transpose", reads=(rSM, rIDF), writes=(rP[0],),
               out=P[0][0:NT, 0:128], in_=SM[:, 64:64 + NT], identity=IDF[:, :])
            E_("act", "activation", reads=(rP[0],), writes=(rS2,),
               out=S2[0:NT, 0:128], in_=P[0][0:NT, 0:128], func=AF.Identity)
        if mode == "attn":
            E_("sy", "dma_start", reads=(rS2,), writes=(res_impout,),
               out=impout[:, :], in_=S2[0:NT, 0:128])
            raise _PhaseDone()
        if mode == "ffn":
            E_("gd", "dma_start", reads=(), writes=(rGI,),
               out=GI[0:NROWS, 0:128], in_=gimp[:, :])
        else:
            E_("gd", "dma_start", reads=(rS2,), writes=(rCCI,),
               out=cc_in.ap()[:, :], in_=S2[0:NT, 0:128])
            E_("gd", "collective_compute", reads=(rCCI,), writes=(rCCO,),
               kind="AllGather", op=ALU.bypass, replica_groups=[list(range(NCORES))],
               ins=[cc_in.ap().opt()], outs=[cc_out.ap().opt()], alt_q="cc")
            E_("gd", "dma_start", reads=(rCCO,), writes=(rGI,),
               out=GI[0:NROWS, 0:128], in_=cc_out.ap()[:, :])
        E_("dve", "tensor_tensor", reads=(rGI,), writes=(rS2,),
           out=S2[0:NROWS, 0:128], in0=GI[0:NROWS, 0:128], in1=GI[0:NROWS, 0:128],
           op=ALU.mult)
        for srcap, sres, col in ((GI[0:NROWS, 0:128], rGI, 0), (S2[0:NROWS, 0:128], rS2, 1)):
            mm(PS[0:1, 0:128], ONEF[0:NROWS, :], srcap, rd=[rONEF, sres], wr=[rPSp],
               start=True, stop=True)
            E_("act", "activation", reads=(rPSp,), writes=(rSC1,),
               out=SC1[:, 32:32 + 128], in_=PS[0:1, 0:128], func=AF.Identity)
            E_("dve", "tensor_reduce", reads=(rSC1,), writes=(rEUS,),
               out=EUS[:, 10 + col:11 + col], in_=SC1[:, 32:32 + 128], axis=AX.X, op=ALU.add)
        ntotf = float(NTOT)
        E_("dve", "tensor_scalar", reads=(rEUS,), writes=(rEUS,),
           out=EUS[:, 10:12], in0=EUS[:, 10:12], scalar1=1.0 / ntotf, scalar2=None,
           op0=ALU.mult)
        mm(P[2][0:NROWS, 0:1], ONRF[0:1, 0:NROWS], EUS[0:1, 10:11],
           rd=[rONRF, rEUS], wr=[rP[2]], start=True, stop=True)
        E_("act", "activation", reads=(rP[2],), writes=(rSCB,),
           out=SCB[0:NROWS, 0:1], in_=P[2][0:NROWS, 0:1], func=AF.Identity)
        E_("dve", "tensor_scalar", reads=(rGI, rSCB), writes=(rS2,),
           out=S2[0:NROWS, 0:128], in0=GI[0:NROWS, 0:128],
           scalar1=SCB[0:NROWS, 0:1], scalar2=None, op0=ALU.subtract)
        E_("dve", "tensor_tensor", reads=(rS2,), writes=(rS2,),
           out=S2[0:NROWS, 0:128], in0=S2[0:NROWS, 0:128], in1=S2[0:NROWS, 0:128],
           op=ALU.mult)
        mm(PS[0:1, 0:128], ONEF[0:NROWS, :], S2[0:NROWS, 0:128],
           rd=[rONEF, rS2], wr=[rPSp], start=True, stop=True)
        E_("act", "activation", reads=(rPSp,), writes=(rSC1,),
           out=SC1[:, 32:32 + 128], in_=PS[0:1, 0:128], func=AF.Identity)
        E_("dve", "tensor_reduce", reads=(rSC1,), writes=(rEUS,),
           out=EUS[:, 13:14], in_=SC1[:, 32:32 + 128], axis=AX.X, op=ALU.add)
        E_("dve", "tensor_scalar", reads=(rEUS,), writes=(rEUS,),
           out=EUS[:, 13:14], in0=EUS[:, 13:14], scalar1=1.0 / ntotf, scalar2=None,
           op0=ALU.mult)
        E_("act", "activation", reads=(rEUS,), writes=(rEUS,),
           out=EUS[:, 14:15], in_=EUS[:, 13:14], func=AF.Sqrt)
        E_("dve", "tensor_scalar", reads=(rEUS,), writes=(rEUS,),
           out=EUS[:, 15:16], in0=EUS[:, 14:15], scalar1=0.1, scalar2=1.0,
           op0=ALU.mult, op1=ALU.add)
        E_("dve", "reciprocal", reads=(rEUS,), writes=(rEUS,),
           out=EUS[:, 9:10], in_=EUS[:, 15:16])
        E_("dve", "memset", reads=(), writes=(rSC1,), ap=SC1[:, 0:4], constant=-1.0)
        E_("dve", "tensor_reduce", reads=(rGI,), writes=(rSM,),
           out=SM[0:NROWS, 88:89], in_=GI[0:NROWS, 0:128], axis=AX.X, op=ALU.max)
        E_("pe", "transpose", reads=(rSM, rIDF), writes=(rP[0],),
           out=P[0][0:1, 0:NROWS], in_=SM[0:NROWS, 88:89],
           identity=IDF[0:NROWS, 0:NROWS])
        E_("act", "activation", reads=(rP[0],), writes=(rSC1,),
           out=SC1[:, 32:32 + NROWS], in_=P[0][0:1, 0:NROWS], func=AF.Identity)
        E_("dve", "tensor_reduce", reads=(rSC1,), writes=(rSC1,),
           out=SC1[:, 4:5], in_=SC1[:, 32:32 + NROWS], axis=AX.X, op=ALU.max)
        E_("dve", "tensor_scalar", reads=(rSC1,), writes=(rSC1,),
           out=SC1[:, 1:2], in0=SC1[:, 4:5], scalar1=1.0, scalar2=None, op0=ALU.add)
        E_("dve", "tensor_copy", reads=(rSC1,), writes=(rSC1,),
           out=SC1[:, 3:4], in_=SC1[:, 1:2])
        for it in range(64):
            for j in range(2):
                E_("dve", "tensor_tensor", reads=(rSC1,), writes=(rSC1,),
                   out=SC1[:, 5:6], in0=SC1[:, 2 * j:2 * j + 1],
                   in1=SC1[:, 2 * j + 1:2 * j + 2], op=ALU.add)
                E_("dve", "tensor_scalar", reads=(rSC1,), writes=(rSC1,),
                   out=SC1[:, 5:6], in0=SC1[:, 5:6], scalar1=0.5, scalar2=None,
                   op0=ALU.mult)
                mm(P[2][0:NROWS, 0:1], ONRF[0:1, 0:NROWS], SC1[0:1, 5:6],
                   rd=[rONRF, rSC1], wr=[rP[2]], start=True, stop=True)
                E_("act", "activation", reads=(rP[2],), writes=(rSCB,),
                   out=SCB[0:NROWS, 0:1], in_=P[2][0:NROWS, 0:1], func=AF.Identity)
                E_("dve", "tensor_scalar", reads=(rGI, rSCB), writes=(rS2,),
                   out=S2[0:NROWS, 0:128], in0=GI[0:NROWS, 0:128],
                   scalar1=SCB[0:NROWS, 0:1], scalar2=None, op0=ALU.is_le)
                mm(PS[0:1, 0:128], ONEF[0:NROWS, :], S2[0:NROWS, 0:128],
                   rd=[rONEF, rS2], wr=[rPSp], start=True, stop=True)
                E_("act", "activation", reads=(rPSp,), writes=(rSC1,),
                   out=SC1[:, 32:32 + 128], in_=PS[0:1, 0:128], func=AF.Identity)
                E_("dve", "tensor_reduce", reads=(rSC1,), writes=(rSC1,),
                   out=SC1[:, 6:7], in_=SC1[:, 32:32 + 128], axis=AX.X, op=ALU.add)
                E_("dve", "tensor_scalar", reads=(rSC1,), writes=(rSC1,),
                   out=SC1[:, 7:8], in0=SC1[:, 6:7], scalar1=float(KPOS + 1 + j),
                   scalar2=None, op0=ALU.is_ge)
                E_("dve", "tensor_tensor", reads=(rSC1,), writes=(rSC1,),
                   out=SC1[:, 8:9], in0=SC1[:, 5:6],
                   in1=SC1[:, 2 * j + 1:2 * j + 2], op=ALU.subtract)
                E_("dve", "tensor_tensor", reads=(rSC1,), writes=(rSC1,),
                   out=SC1[:, 8:9], in0=SC1[:, 8:9], in1=SC1[:, 7:8], op=ALU.mult)
                E_("dve", "tensor_tensor", reads=(rSC1,), writes=(rSC1,),
                   out=SC1[:, 2 * j + 1:2 * j + 2],
                   in0=SC1[:, 2 * j + 1:2 * j + 2], in1=SC1[:, 8:9], op=ALU.add)
                E_("dve", "tensor_tensor", reads=(rSC1,), writes=(rSC1,),
                   out=SC1[:, 8:9], in0=SC1[:, 2 * j:2 * j + 1],
                   in1=SC1[:, 5:6], op=ALU.subtract)
                E_("dve", "tensor_tensor", reads=(rSC1,), writes=(rSC1,),
                   out=SC1[:, 8:9], in0=SC1[:, 8:9], in1=SC1[:, 7:8], op=ALU.mult)
                E_("dve", "tensor_tensor", reads=(rSC1,), writes=(rSC1,),
                   out=SC1[:, 2 * j:2 * j + 1], in0=SC1[:, 5:6],
                   in1=SC1[:, 8:9], op=ALU.add)
        E_("dve", "tensor_tensor", reads=(rSC1,), writes=(rSC1,),
           out=SC1[:, 5:6], in0=SC1[:, 3:4], in1=SC1[:, 1:2], op=ALU.subtract)
        E_("dve", "tensor_scalar", reads=(rSC1,), writes=(rSC1,),
           out=SC1[:, 5:6], in0=SC1[:, 5:6], scalar1=float(FRAC), scalar2=None,
           op0=ALU.mult)
        E_("dve", "tensor_tensor", reads=(rSC1,), writes=(rSC1,),
           out=SC1[:, 6:7], in0=SC1[:, 1:2], in1=SC1[:, 5:6], op=ALU.add)
        mm(P[2][:, 0:1], ONRF[:, :], SC1[0:1, 6:7], rd=[rONRF, rSC1], wr=[rP[2]],
           start=True, stop=True)
        E_("act", "activation", reads=(rP[2],), writes=(rSCB,),
           out=SCB[:, 1:2], in_=P[2][:, 0:1], func=AF.Identity)
        mm(P[2][:, 0:1], ONRF[:, :], EUS[0:1, 9:10], rd=[rONRF, rEUS], wr=[rP[2]],
           start=True, stop=True)
        E_("act", "activation", reads=(rP[2],), writes=(rSCB,),
           out=SCB[:, 2:3], in_=P[2][:, 0:1], func=AF.Identity)
        E_("dve", "tensor_scalar", reads=(rSM, rSCB), writes=(rSM,),
           out=SM[:, 80:80 + NT], in0=SM[:, 64:64 + NT], scalar1=SCB[:, 1:2],
           scalar2=None, op0=ALU.is_gt)
        E_("dve", "tensor_tensor", reads=(rSM,), writes=(rSM,),
           out=SM[:, 64:64 + NT], in0=SM[:, 80:80 + NT], in1=SM[:, 0:NT], op=ALU.mult)
        E_("dve", "tensor_scalar", reads=(rSM,), writes=(rSM,),
           out=SM[:, 64:64 + NT], in0=SM[:, 64:64 + NT], scalar1=1e-8, scalar2=None,
           op0=ALU.add)
        E_("dve", "reciprocal", reads=(rSM,), writes=(rSM,),
           out=SM[:, 0:NT], in_=SM[:, 64:64 + NT])
        E_("dve", "tensor_tensor", reads=(rSM,), writes=(rSM,),
           out=SM[:, 0:NT], in0=SM[:, 0:NT], in1=SM[:, 80:80 + NT], op=ALU.mult)
        E_("dve", "tensor_scalar", reads=(rSM, rSCB), writes=(rSM,),
           out=SM[:, 0:NT], in0=SM[:, 0:NT], scalar1=SCB[:, 2:3], scalar2=None,
           op0=ALU.mult)
        E_("dve", "tensor_tensor", reads=(rSM,), writes=(rSM,),
           out=SM[:, 16:16 + NT], in0=SM[:, 0:NT], in1=SM[:, 48:48 + NT], op=ALU.mult)
        E_("dve", "tensor_tensor", reads=(rSM,), writes=(rSM,),
           out=SM[:, 0:NT], in0=SM[:, 0:NT], in1=SM[:, 32:32 + NT], op=ALU.mult)

        def accsl(i):
            return AW[:, i * 1024:(i + 1) * 1024]

        def xnTsl(db, t0, t1):
            t_, r_ = (AQ, rAQ) if db < 4 else (AK, rAK)
            return t_[:, (db % 4) * T + t0:(db % 4) * T + t1], r_

        for i in range(NT):
            hb = i % 2
            x1buf = S3[:, hb * 1024:hb * 1024 + 1024]
            E_("sy", "dma_start", reads=(), writes=(S3_HALF[hb],),
               out=x1buf,
               in_=(x1in if mode == "ffn" else x1sp)[i * 128:(i + 1) * 128, :])
            E_("act", "activation", reads=(S3_HALF[hb], rSM), writes=(rS1,),
               out=S1[:, 0:1024], in_=x1buf, func=AF.Identity,
               scale=SM[:, i:i + 1], bias=SM[:, 16 + i:17 + i])
            E_("act", "activation", reads=(rS1,), writes=(rAW,),
               out=accsl(i), in_=S1[:, 0:1024], func=AF.Identity, scale=0.1)
            for db in range(8):
                E_("pe", "transpose", reads=(rS1, rIDB), writes=(rPBp,),
                   out=PB[:, 0:128], in_=S1[:, db * 128:(db + 1) * 128],
                   identity=IDB[:, :])
                dsl, dres = xnTsl(db, i * 128, (i + 1) * 128)
                E_("dve", "tensor_copy", reads=(rPBp,), writes=(dres,),
                   out=dsl, in_=PB[:, 0:128])

        if phases < 6:
            raise _PhaseDone()
        for sl in range(4):
            bi = sl % 2
            E_("gd", "dma_start", reads=(), writes=(rWB[bi],),
               out=WB[:, bi * 2048: bi * 2048 + 2048],
               in_=gw1[:, sl * 256:(sl + 1) * 256].rearrange("(kt p) m -> p kt m", p=128))
            for mt2 in range(2):
                mt = sl * 2 + mt2
                for c in range(NCH):
                    pi = c % 2
                    for kt in range(8):
                        rhs, rr = xnTsl(kt, c * 512, (c + 1) * 512)
                        mm(P[pi][:, :],
                           WB[:, bi * 2048 + kt * 256 + mt2 * 128: bi * 2048 + kt * 256 + (mt2 + 1) * 128],
                           rhs, rd=[rWB[bi], rr], wr=[rP[pi]], start=(kt == 0), stop=(kt == 7))
                    E_("act", "activation", reads=(rP[pi], rCT), writes=(rAV,),
                       out=AV[:, mt * T + c * 512: mt * T + (c + 1) * 512], in_=P[pi][:, :],
                       func=AF.Silu, bias=CT[:, CO["GB1"] + mt:CO["GB1"] + mt + 1])
        for c in range(NCH):
            for mt in range(8):
                mm(PS[:, :], ONEB[:, :], AV[:, mt * T + c * 512: mt * T + (c + 1) * 512],
                   rd=[rONEB, rAV], wr=[rPSp], start=(mt == 0), stop=(mt == 7))
            E_("act", "activation", reads=(rPSp,), writes=(rV1,),
               out=V1[:, c * 512:(c + 1) * 512], in_=PS[:, :], func=AF.Identity)
            for mt in range(8):
                E_("act", "activation", reads=(rAV,), writes=(rS1,),
                   out=S1[:, 0:512], in_=AV[:, mt * T + c * 512: mt * T + (c + 1) * 512],
                   func=AF.Square)
                mm(PS[:, :], ONEB[:, :], S1[:, 0:512], rd=[rONEB, rS1], wr=[rPSp],
                   start=(mt == 0), stop=(mt == 7))
            E_("act", "activation", reads=(rPSp,), writes=(rV2,),
               out=V2[:, c * 512:(c + 1) * 512], in_=PS[:, :], func=AF.Identity)
        V3 = S2[0:1, 0:T]
        E_("dve", "tensor_scalar", reads=(rV1,), writes=(rV1,),
           out=V1[:, :], in0=V1[:, :], scalar1=1.0 / HID, scalar2=None, op0=ALU.mult)
        E_("dve", "tensor_scalar", reads=(rV2,), writes=(rV2,),
           out=V2[:, :], in0=V2[:, :], scalar1=1.0 / HID, scalar2=None, op0=ALU.mult)
        E_("dve", "tensor_tensor", reads=(rV1,), writes=(rS2,),
           out=V3, in0=V1[:, :], in1=V1[:, :], op=ALU.mult)
        E_("dve", "tensor_tensor", reads=(rV2, rS2), writes=(rV2,),
           out=V2[:, :], in0=V2[:, :], in1=V3, op=ALU.subtract)
        E_("dve", "tensor_scalar", reads=(rV2,), writes=(rV2,),
           out=V2[:, :], in0=V2[:, :], scalar1=1e-5, scalar2=None, op0=ALU.add)
        E_("act", "activation", reads=(rV2,), writes=(rV2,),
           out=V2[:, :], in_=V2[:, :], func=AF.Ln)
        E_("act", "activation", reads=(rV2,), writes=(rV2,),
           out=V2[:, :], in_=V2[:, :], func=AF.Exp, scale=-0.5)
        E_("dve", "scalar_tensor_tensor", reads=(rV1, rV2), writes=(rS2,),
           out=V3, in0=V1[:, :], scalar=-1.0, in1=V2[:, :], op0=ALU.mult, op1=ALU.mult)
        for c in range(NCH):
            mm(P[0][:, :], ONRF[:, :], V2[:, c * 512:(c + 1) * 512],
               rd=[rONRF, rV2], wr=[rP[0]], start=True, stop=True)
            mm(P[1][:, :], ONRF[:, :], S2[0:1, c * 512:(c + 1) * 512],
               rd=[rONRF, rS2], wr=[rP[1]], start=True, stop=True)
            for mt in range(8):
                sl_ = AV[:, mt * T + c * 512: mt * T + (c + 1) * 512]
                E_("dve", "tensor_tensor", reads=(rAV, rP[0]), writes=(rS1,),
                   out=S1[:, 0:512], in0=sl_, in1=P[0][:, :], op=ALU.mult)
                E_("dve", "tensor_tensor", reads=(rS1, rP[1]), writes=(rS1,),
                   out=S1[:, 0:512], in0=S1[:, 0:512], in1=P[1][:, :], op=ALU.add)
                E_("act", "activation", reads=(rS1, rCT), writes=(rAV,),
                   out=sl_, in_=S1[:, 0:512], func=AF.Identity,
                   scale=CT[:, CO["LNS"] + mt:CO["LNS"] + mt + 1],
                   bias=CT[:, CO["LNO"] + mt:CO["LNO"] + mt + 1])
        for sl in range(2):
            bi = sl % 2
            E_("gd", "dma_start", reads=(), writes=(rWB[bi],),
               out=WB[:, bi * 2048: bi * 2048 + 2048],
               in_=gw2[:, sl * 256:(sl + 1) * 256].rearrange("(kt p) m -> p kt m", p=128))
            for mt2 in range(2):
                mt = sl * 2 + mt2
                for c in range(NCH):
                    pi = 2 + c % 2
                    for kt in range(8):
                        mm(P[pi][:, :],
                           WB[:, bi * 2048 + kt * 256 + mt2 * 128: bi * 2048 + kt * 256 + (mt2 + 1) * 128],
                           AV[:, kt * T + c * 512: kt * T + (c + 1) * 512],
                           rd=[rWB[bi], rAV], wr=[rP[pi]], start=(kt == 0), stop=(kt == 7))
                    E_("act", "activation", reads=(rP[pi], rCT), writes=(rAG,),
                       out=AG[:, mt * T + c * 512: mt * T + (c + 1) * 512], in_=P[pi][:, :],
                       func=AF.Silu, bias=CT[:, CO["GB2"] + mt:CO["GB2"] + mt + 1])
        E_("sy", "dma_start", reads=(), writes=(rS2,),
           out=S2[0:E, 0:T], in_=noise_d[:, :])
        E_("dve", "tensor_scalar", reads=(rEUS,), writes=(rSC1,),
           out=SC1[:, 16:16 + E], in0=EUS[:, 0:E], scalar1=1e-8, scalar2=None, op0=ALU.add)
        E_("dve", "tensor_reduce", reads=(rSC1,), writes=(rSC1,),
           out=SC1[:, 24:25], in_=SC1[:, 16:16 + E], axis=AX.X, op=ALU.min)
        E_("dve", "tensor_scalar", reads=(rSC1,), writes=(rSC1,),
           out=SC1[:, 25:26], in0=SC1[:, 24:25], scalar1=0.0, scalar2=None, op0=ALU.is_lt)
        E_("dve", "tensor_scalar", reads=(rSC1,), writes=(rSC1,),
           out=SC1[:, 16:16 + E], in0=SC1[:, 16:16 + E], scalar1=1e-30, scalar2=None,
           op0=ALU.max)
        E_("act", "activation", reads=(rSC1,), writes=(rSC1,),
           out=SC1[:, 16:16 + E], in_=SC1[:, 16:16 + E], func=AF.Ln)
        E_("dve", "tensor_tensor", reads=(rSC1, rEUS), writes=(rSC1,),
           out=SC1[:, 16:16 + E], in0=SC1[:, 16:16 + E], in1=EUS[:, 0:E], op=ALU.mult)
        E_("dve", "tensor_reduce", reads=(rSC1,), writes=(rSC1,),
           out=SC1[:, 26:27], in_=SC1[:, 16:16 + E], axis=AX.X, op=ALU.add)
        E_("dve", "tensor_scalar", reads=(rSC1,), writes=(rSC1,),
           out=SC1[:, 26:27], in0=SC1[:, 26:27], scalar1=0.01, scalar2=0.02,
           op0=ALU.mult, op1=ALU.add)
        E_("dve", "tensor_scalar", reads=(rSC1,), writes=(rSC1,),
           out=SC1[:, 27:28], in0=SC1[:, 26:27], scalar1=1e-4, scalar2=None, op0=ALU.max)
        mm(P[2][0:E, 0:1], ONRF[0:1, 0:E], SC1[0:1, 27:28], rd=[rONRF, rSC1],
           wr=[rP[2]], start=True, stop=True)
        E_("act", "activation", reads=(rP[2],), writes=(rSCB,),
           out=SCB[0:E, 3:4], in_=P[2][0:E, 0:1], func=AF.Identity)
        mm(P[2][:, 0:1], ONRF[:, :], SC1[0:1, 25:26], rd=[rONRF, rSC1],
           wr=[rP[2]], start=True, stop=True)
        E_("act", "activation", reads=(rP[2],), writes=(rSCB,),
           out=SCB[:, 4:5], in_=P[2][:, 0:1], func=AF.Identity)
        E_("dve", "tensor_scalar", reads=(rSCB,), writes=(rSCB,),
           out=SCB[:, 5:6], in0=SCB[:, 4:5], scalar1=-1.0, scalar2=1.0,
           op0=ALU.mult, op1=ALU.add)
        for db in range(8):
            dsl, dres = xnTsl(db, 0, T)
            E_("dve", "tensor_reduce", reads=(dres,), writes=(rSCB,),
               out=SCB[:, 6:7], in_=dsl, axis=AX.X, op=ALU.add)
            E_("dve", "tensor_scalar", reads=(rSCB,), writes=(rSCB,),
               out=SCB[:, 7:8], in0=SCB[:, 6:7], scalar1=0.3 / float(T), scalar2=None,
               op0=ALU.mult)
            mm(P[3][0:1, 0:E], SCB[:, 7:8], AFW[:, db * 8:(db + 1) * 8],
               rd=[rSCB, rAFW], wr=[rP[3]], start=(db == 0), stop=(db == 7))
        E_("act", "activation", reads=(rP[3],), writes=(rSC1,),
           out=SC1[:, 40:40 + E], in_=P[3][0:1, 0:E], func=AF.Identity)
        E_("dve", "tensor_tensor", reads=(rSC1, rCT), writes=(rSC1,),
           out=SC1[:, 40:40 + E], in0=SC1[:, 40:40 + E],
           in1=CT[0:1, CO["AFFB"]:CO["AFFB"] + E], op=ALU.add)
        E_("pe", "transpose", reads=(rSC1, rIDF), writes=(rP[2],),
           out=P[2][0:E, 0:1], in_=SC1[0:1, 40:40 + E], identity=IDF[0:1, 0:1])
        E_("act", "activation", reads=(rP[2],), writes=(rSCB,),
           out=SCB[0:E, 0:1], in_=P[2][0:E, 0:1], func=AF.Identity)
        for c in range(NCH):
            for kt in range(4):
                mm(P[5][0:E, :], W3B[:, kt * 8:(kt + 1) * 8],
                   AG[:, kt * T + c * 512: kt * T + (c + 1) * 512],
                   rd=[rW3B, rAG], wr=[rP[5]], start=(kt == 0), stop=(kt == 3))
            E_("act", "activation", reads=(rP[5], rCT), writes=(rLGS,),
               out=LGS[:, :], in_=P[5][0:E, :], func=AF.Identity,
               bias=CT[0:E, CO["GB3"]:CO["GB3"] + 1])
            E_("dve", "tensor_scalar", reads=(rLGS, rSCB), writes=(rLGS,),
               out=LGS[:, :], in0=LGS[:, :], scalar1=SCB[0:E, 0:1], scalar2=None,
               op0=ALU.add)
            E_("dve", "tensor_scalar", reads=(rS2, rSCB), writes=(rS2,),
               out=S2[0:E, c * 512:(c + 1) * 512], in0=S2[0:E, c * 512:(c + 1) * 512],
               scalar1=SCB[0:E, 3:4], scalar2=None, op0=ALU.mult)
            E_("dve", "tensor_tensor", reads=(rLGS, rS2), writes=(rLGS,),
               out=LGS[:, :], in0=LGS[:, :], in1=S2[0:E, c * 512:(c + 1) * 512],
               op=ALU.add)
            E_("dve", "tensor_scalar", reads=(rLGS,), writes=(rLGS,),
               out=LGS[:, :], in0=LGS[:, :], scalar1=1.0 / 1.5, scalar2=None,
               op0=ALU.mult)
            for s4 in range(4):
                i = c * 4 + s4
                E_("pe", "transpose", reads=(rLGS, rIDF), writes=(rP[0],),
                   out=P[0][:, 0:E], in_=LGS[:, s4 * 128:(s4 + 1) * 128],
                   identity=IDF[0:E, 0:E])
                E_("act", "activation", reads=(rP[0],), writes=(rSM,),
                   out=SM[:, 80:88], in_=P[0][:, 0:E], func=AF.Identity)
                E_("dve", "tensor_reduce", reads=(rSM,), writes=(rSM,),
                   out=SM[:, 88:89], in_=SM[:, 80:88], axis=AX.X, op=ALU.max)
                E_("dve", "tensor_scalar", reads=(rSM,), writes=(rS3a,),
                   out=S3[:, 0:8], in0=SM[:, 80:88], scalar1=SM[:, 88:89],
                   scalar2=None, op0=ALU.is_ge)
                E_("dve", "tensor_reduce", reads=(rS3a,), writes=(rSM,),
                   out=SM[:, 89:90], in_=S3[:, 0:8], axis=AX.X, op=ALU.add)
                E_("dve", "tensor_scalar", reads=(rS3a,), writes=(rS3a,),
                   out=S3[:, 8:16], in0=S3[:, 0:8], scalar1=-2e9, scalar2=None,
                   op0=ALU.mult)
                E_("dve", "tensor_tensor", reads=(rSM, rS3a), writes=(rS3a,),
                   out=S3[:, 8:16], in0=SM[:, 80:88], in1=S3[:, 8:16], op=ALU.add)
                E_("dve", "tensor_reduce", reads=(rS3a,), writes=(rSM,),
                   out=SM[:, 91:92], in_=S3[:, 8:16], axis=AX.X, op=ALU.max)
                E_("dve", "tensor_scalar", reads=(rSM,), writes=(rSM,),
                   out=SM[:, 92:93], in0=SM[:, 89:90], scalar1=2.0, scalar2=None,
                   op0=ALU.is_ge)
                E_("dve", "tensor_tensor", reads=(rSM,), writes=(rSM,),
                   out=SM[:, 93:94], in0=SM[:, 88:89], in1=SM[:, 91:92],
                   op=ALU.subtract)
                E_("dve", "tensor_tensor", reads=(rSM,), writes=(rSM,),
                   out=SM[:, 93:94], in0=SM[:, 93:94], in1=SM[:, 92:93], op=ALU.mult)
                E_("dve", "tensor_tensor", reads=(rSM,), writes=(rSM,),
                   out=SM[:, 93:94], in0=SM[:, 91:92], in1=SM[:, 93:94], op=ALU.add)
                E_("dve", "tensor_scalar", reads=(rSM,), writes=(rS3a,),
                   out=S3[:, 0:8], in0=SM[:, 80:88], scalar1=SM[:, 93:94],
                   scalar2=None, op0=ALU.is_ge)
                E_("dve", "tensor_scalar", reads=(rS3a, rSCB), writes=(rS3a,),
                   out=S3[:, 0:8], in0=S3[:, 0:8], scalar1=SCB[:, 5:6], scalar2=None,
                   op0=ALU.mult)
                E_("dve", "tensor_tensor", reads=(rS3a, rSM), writes=(rS3b,),
                   out=S3[:, 16:24], in0=SM[:, 80:88], in1=S3[:, 0:8], op=ALU.mult)
                E_("dve", "tensor_scalar", reads=(rS3a,), writes=(rS3a,),
                   out=S3[:, 24:32], in0=S3[:, 0:8], scalar1=1e9, scalar2=-1e9,
                   op0=ALU.mult, op1=ALU.add)
                E_("dve", "tensor_tensor", reads=(rS3a, rS3b), writes=(rS3b,),
                   out=S3[:, 16:24], in0=S3[:, 16:24], in1=S3[:, 24:32], op=ALU.add)
                E_("dve", "tensor_reduce", reads=(rS3b,), writes=(rSM,),
                   out=SM[:, 94:95], in_=S3[:, 16:24], axis=AX.X, op=ALU.max)
                E_("dve", "tensor_scalar", reads=(rSM,), writes=(rSM,),
                   out=SM[:, 95:96], in0=SM[:, 94:95], scalar1=-1.0, scalar2=None,
                   op0=ALU.mult)
                E_("act", "activation", reads=(rS3b, rSM), writes=(rS3b,),
                   out=S3[:, 16:24], in_=S3[:, 16:24], func=AF.Exp,
                   bias=SM[:, 95:96])
                E_("dve", "tensor_reduce", reads=(rS3b,), writes=(rSM,),
                   out=SM[:, 94:95], in_=S3[:, 16:24], axis=AX.X, op=ALU.add)
                E_("dve", "reciprocal", reads=(rSM,), writes=(rSM,),
                   out=SM[:, 95:96], in_=SM[:, 94:95])
                E_("dve", "tensor_scalar", reads=(rS3b, rSM), writes=(rS3b,),
                   out=S3[:, 16:24], in0=S3[:, 16:24], scalar1=SM[:, 95:96],
                   scalar2=None, op0=ALU.mult)
                E_("dve", "tensor_tensor", reads=(rS3b, rS3a), writes=(rS3a,),
                   out=S3[:, 0:8], in0=S3[:, 16:24], in1=S3[:, 0:8], op=ALU.mult)
                E_("dve", "tensor_scalar", reads=(rNZE, rSCB), writes=(rS3b,),
                   out=S3[:, 24:32], in0=NZE[:, 0:8], scalar1=SCB[:, 4:5],
                   scalar2=None, op0=ALU.mult)
                E_("dve", "tensor_tensor", reads=(rS3a, rS3b), writes=(rPRB,),
                   out=PRB[:, i * E:(i + 1) * E], in0=S3[:, 0:8], in1=S3[:, 24:32],
                   op=ALU.add)
                E_("pe", "transpose", reads=(rPRB, rIDF), writes=(rP[0],),
                   out=P[0][0:E, 0:128], in_=PRB[:, i * E:(i + 1) * E],
                   identity=IDF[:, :])
                E_("act", "activation", reads=(rP[0],), writes=(rPRT,),
                   out=PRT[:, i * 128:(i + 1) * 128], in_=P[0][0:E, 0:128],
                   func=AF.Identity)
        for i in range(NT):
            for dc in range(2):
                mm(P[0][:, :], PRT[:, i * 128:(i + 1) * 128], BM[:, dc * 512:(dc + 1) * 512],
                   rd=[rPRT, rBM], wr=[rP[0]], start=True, stop=True)
                E_("dve", "tensor_tensor", reads=(rP[0], rAW), writes=(rAW,),
                   out=accsl(i)[:, dc * 512:(dc + 1) * 512],
                   in0=accsl(i)[:, dc * 512:(dc + 1) * 512], in1=P[0][:, :], op=ALU.add)

        if phases < 7:
            raise _PhaseDone()
        PO = P

        def hblk(j):
            if j < 8 * T // ETOK:
                return AV[:, j * ETOK:(j + 1) * ETOK], rAV
            jj = j - 8 * T // ETOK
            return AG[:, jj * ETOK:(jj + 1) * ETOK], rAG

        NBH = 8 * T // ETOK

        def h_slab_dma(wdram, e_idx, sl, kh, KH, bi):
            E_("gd", "dma_start", reads=(), writes=(rWB[bi],),
               out=WB[:, bi * 2048: bi * 2048 + 2048],
               in_=wdram[e_idx, kh * 1024:(kh + 1) * 1024, sl * 256:(sl + 1) * 256]
               .rearrange("(kt p) m -> p kt m", p=128))

        def h_stage(wdram, e_idx, KIN, MOUT, actf, biascol, blk0, ec,
                    gelu=False, sigsq=False):
            KH = KIN // 1024
            slabs = MOUT // 256
            seq = [(sl, kh) for sl in range(slabs) for kh in range(KH)]
            h_slab_dma(wdram, e_idx, 0, 0, KH, 0)
            for si, (sl, kh) in enumerate(seq):
                bi = si % 2
                if si + 1 < len(seq):
                    h_slab_dma(wdram, e_idx, seq[si + 1][0], seq[si + 1][1], KH,
                               (si + 1) % 2)
                for mt2 in range(2):
                    mt = sl * 2 + mt2
                    for nc2 in range(ENC):
                        pi = mt2 * ENC + nc2
                        for kt in range(8):
                            gk = kh * 8 + kt
                            if KIN == HID:
                                rhs, rr = xnTsl(gk, ec * ETOK + nc2 * 512,
                                                ec * ETOK + (nc2 + 1) * 512)
                            else:
                                bsl, rr = hblk(gk)
                                rhs = bsl[:, nc2 * 512:(nc2 + 1) * 512]
                            mm(PO[pi][:, :],
                               WB[:, bi * 2048 + kt * 256 + mt2 * 128: bi * 2048 + kt * 256 + (mt2 + 1) * 128],
                               rhs, rd=[rWB[bi], rr], wr=[rP[pi]],
                               start=(kh == 0 and kt == 0), stop=(kh == KH - 1 and kt == 7))
                        if kh != KH - 1:
                            continue
                        bsl, br = hblk(blk0 + mt)
                        dst = bsl[:, nc2 * 512:(nc2 + 1) * 512]
                        if gelu:
                            E_("act", "activation", reads=(rP[pi], rCT), writes=(br,),
                               out=dst, in_=PO[pi][:, :], func=AF.Identity,
                               bias=CT[:, biascol + mt:biascol + mt + 1])
                            E_("act", "activation", reads=(br,), writes=(rS1,),
                               out=S1[:, 0:512], in_=dst, func=AF.Square)
                            E_("dve", "tensor_tensor", reads=(rS1, br), writes=(rS1,),
                               out=S1[:, 0:512], in0=S1[:, 0:512], in1=dst, op=ALU.mult)
                            E_("dve", "tensor_scalar", reads=(rS1,), writes=(rS1,),
                               out=S1[:, 0:512], in0=S1[:, 0:512], scalar1=C2G,
                               scalar2=None, op0=ALU.mult)
                            E_("dve", "tensor_tensor", reads=(rS1, br), writes=(rS1,),
                               out=S1[:, 0:512], in0=S1[:, 0:512], in1=dst, op=ALU.add)
                            E_("act", "activation", reads=(rS1,), writes=(rS1,),
                               out=S1[:, 512:1024], in_=S1[:, 0:512], func=AF.Sigmoid,
                               scale=C1G)
                            E_("dve", "tensor_tensor", reads=(rS1, br), writes=(br,),
                               out=dst, in0=dst, in1=S1[:, 512:1024], op=ALU.mult)
                        elif sigsq:
                            E_("act", "activation", reads=(rP[pi], rCT), writes=(br,),
                               out=dst, in_=PO[pi][:, :], func=AF.Identity,
                               bias=CT[:, biascol + mt:biascol + mt + 1])
                            E_("act", "activation", reads=(br,), writes=(rS1,),
                               out=S1[:, 0:512], in_=dst, func=AF.Sigmoid)
                            E_("dve", "tensor_tensor", reads=(rS1,), writes=(rS1,),
                               out=S1[:, 512:1024], in0=S1[:, 0:512], in1=S1[:, 0:512],
                               op=ALU.mult)
                            E_("dve", "tensor_tensor", reads=(rS1, br), writes=(br,),
                               out=dst, in0=dst, in1=S1[:, 512:1024], op=ALU.mult)
                        else:
                            E_("act", "activation", reads=(rP[pi], rCT), writes=(br,),
                               out=dst, in_=PO[pi][:, :], func=actf,
                               bias=CT[:, biascol + mt:biascol + mt + 1])

        def o_slab_dma(wdram, e_idx, kt, bi):
            E_("gd", "dma_start", reads=(), writes=(rWB[bi],),
               out=WB[:, bi * 2048: bi * 2048 + 1024],
               in_=wdram[e_idx, kt * 128:(kt + 1) * 128, :])

        def out_stage(wdram, e_idx, KIN, gcol, ec, blk0):
            ktn = KIN // 128
            groups = [(g, min(g + 3, NTT)) for g in range(0, NTT, 3)]
            for g0, g1 in groups:
                o_slab_dma(wdram, e_idx, 0, 0)
                for kt in range(ktn):
                    bi = kt % 2
                    if kt + 1 < ktn:
                        o_slab_dma(wdram, e_idx, kt + 1, (kt + 1) % 2)
                    bsl, br = hblk(blk0 + kt)
                    for tt in range(g0, g1):
                        for dc in range(2):
                            pidx = (tt - g0) * 2 + dc
                            mm(P[pidx][:, :], bsl[:, tt * 128:(tt + 1) * 128],
                               WB[:, bi * 2048 + dc * 512: bi * 2048 + (dc + 1) * 512],
                               rd=[rWB[bi], br], wr=[rP[pidx]],
                               start=(kt == 0), stop=(kt == ktn - 1))
                for tt in range(g0, g1):
                    i = ec * NTT + tt
                    for dc in range(2):
                        pidx = (tt - g0) * 2 + dc
                        E_("act", "activation", reads=(rP[pidx], rPRB), writes=(rS1,),
                           out=S1[:, pidx * 512:(pidx + 1) * 512], in_=P[pidx][:, :],
                           func=AF.Identity,
                           scale=PRB[:, i * E + gcol:i * E + gcol + 1])
                        E_("dve", "tensor_tensor", reads=(rS1, rAW), writes=(rAW,),
                           out=accsl(i)[:, dc * 512:(dc + 1) * 512],
                           in0=accsl(i)[:, dc * 512:(dc + 1) * 512],
                           in1=S1[:, pidx * 512:(pidx + 1) * 512], op=ALU.add)

        for ec in range(NG):
            for e_i in range(3):
                h_stage(e0w1, e_i, HID, 4096, None, CO["E0B1"] + e_i * 32, 0, ec, gelu=True)
                out_stage(e0w2, e_i, 4096, 3 * e_i, ec, 0)
            for e_i in range(2):
                h_stage(e2w1, e_i, HID, 3072, None, CO["E2B1"] + e_i * 24, 0, ec, sigsq=True)
                out_stage(e2w2, e_i, 3072, 3 * e_i + 2, ec, 0)
            for e_i in range(3):
                h_stage(e1w1, e_i, HID, 2048, AF.Silu, CO["E1B1"] + e_i * 16, 0, ec)
                h_stage(e1w2, e_i, 2048, 2048, AF.Silu, CO["E1B2"] + e_i * 16, 16, ec)
                out_stage(e1w3, e_i, 2048, 3 * e_i + 1, ec, 16)

        for i in range(NT):
            hb = i % 2
            x1buf = S3[:, hb * 1024:hb * 1024 + 1024]
            E_("sy", "dma_start", reads=(), writes=(S3_HALF[hb],),
               out=x1buf,
               in_=(x1in if mode == "ffn" else x1sp)[i * 128:(i + 1) * 128, :])
            ln_stats(accsl(i), rAW)
            E_("act", "activation", reads=(rAW, rSM), writes=(rS1,),
               out=S1[:, 0:1024], in_=accsl(i), func=AF.Identity, scale=SM[:, 96:97],
               bias=SM[:, 97:98])
            E_("dve", "tensor_tensor", reads=(rS1, S3_HALF[hb]), writes=(rS2,),
               out=S2[:, 0:1024], in0=x1buf, in1=S1[:, 0:1024], op=ALU.add)
            E_("sy", "dma_start", reads=(rS2,), writes=(rOUT,),
               out=out_d[i * 128:(i + 1) * 128, :], in_=S2[:, 0:1024])
    except _PhaseDone:
        pass
    for s, v in sd.qdone("sy") + sd.qdone("gd"):
        sd.rawwait("dve", s, v)

    with nc.Block() as block:
        @block.tensor
        def _(e):
            sd.replay("pe", e)

        @block.vector
        def _(e):
            sd.replay("dve", e)

        @block.scalar
        def _(e):
            sd.replay("act", e)

        @block.gpsimd
        def _(e):
            sd.replay("gd", e)

        @block.sync
        def _(e):
            sd.replay("sy", e)

    es.close()
    return nc


def _host_consts(T, NCORES, inputs):
    NT = T // 128
    NDIFF = NT + 3
    CO = ctab_layout(T)
    g = GAMMA
    ctab = np.zeros((128, CO["W"]), dtype=np.float64)
    sl = np.arange(128)
    for h in range(H):
        for di in range(NDIFF):
            d0 = 128 * (di - 3)
            ctab[:, CO["ESC"] + h * NDIFF + di] = \
                np.exp2((d0 - sl) * np.log2(g[h])) * DH ** -0.5
    ctab[:, CO["GNW"]:CO["GNW"] + 8] = inputs["L0_attn_gn_weight"].reshape(8, 128).T
    ctab[:, CO["GNB"]:CO["GNB"] + 8] = inputs["L0_attn_gn_bias"].reshape(8, 128).T
    ctab[:, CO["GB1"]:CO["GB1"] + 8] = inputs["L0_ffn_gate_b1"].reshape(8, 128).T
    ctab[:, CO["GB2"]:CO["GB2"] + 4] = inputs["L0_ffn_gate_b2"].reshape(4, 128).T
    ctab[0:8, CO["GB3"]] = inputs["L0_ffn_gate_b3"]
    ctab[:, CO["E0B1"]:CO["E0B1"] + 96] = inputs["L0_ffn_e0_b1"].reshape(96, 128).T
    ctab[:, CO["E1B1"]:CO["E1B1"] + 48] = inputs["L0_ffn_e1_b1"].reshape(48, 128).T
    ctab[:, CO["E1B2"]:CO["E1B2"] + 48] = inputs["L0_ffn_e1_b2"].reshape(48, 128).T
    ctab[:, CO["E2B1"]:CO["E2B1"] + 48] = inputs["L0_ffn_e2_b1"].reshape(48, 128).T
    ctab[:, CO["LNS"]:CO["LNS"] + 8] = inputs["L0_ffn_ln_scale"].reshape(8, 128).T
    ctab[:, CO["LNO"]:CO["LNO"] + 8] = inputs["L0_ffn_ln_offset"].reshape(8, 128).T
    ctab[0, CO["AFFB"]:CO["AFFB"] + 8] = 0.3 * inputs["L0_ffn_aff_b"]
    ctab = ctab.astype(np.float32)

    inv_freq = 1.0 / 10000.0 ** (np.arange(0, DH, 2, dtype=np.float32) / DH)
    t = np.arange(T, dtype=np.float32)
    emb = np.concatenate([t[:, None] * inv_freq[None, :]] * 2, axis=-1)
    cosv, sinv = np.cos(emb), np.sin(emb)
    trig = np.zeros((64, 2 * T), dtype=np.float32)
    trig[:, 0:T] = cosv.T
    trig[:, T:2 * T] = sinv.T

    masks = np.zeros((128, 4 * 512), dtype=np.float32)
    for k in range(4):
        masks[:, k * 512:(k + 1) * 512] = \
            (np.arange(512)[None, :] >= np.arange(128)[:, None] + 128 * k)

    idn = np.eye(128, dtype=np.float32)

    tmod = np.arange(T) % 512
    epsr = np.stack([1e-5 * np.exp2(-2.0 * tmod * np.log2(g[h])) for h in range(H)]
                    ).astype(np.float32)

    import jax
    import jax.numpy as jnp
    try:
        cpu = jax.devices("cpu")[0]
        with jax.default_device(cpu):
            noise = np.asarray(jax.random.normal(jax.random.key(1),
                                                 (NCORES, T, E), dtype=jnp.float32))
    except Exception:
        noise = np.asarray(jax.random.normal(jax.random.key(1),
                                             (NCORES, T, E), dtype=jnp.float32))

    bm = np.zeros((E, HID), dtype=np.float32)
    for e in range(E):
        fam, inst = e % 3, e // 3
        if fam == 0:
            bm[e] = inputs["L0_ffn_e0_b2"][inst]
        elif fam == 1:
            bm[e] = inputs["L0_ffn_e1_b3"][inst]
        else:
            bm[e] = inputs["L0_ffn_e2_b2"][inst]
    affwT = np.ascontiguousarray(
        inputs["L0_ffn_aff_w"].reshape(8, 128, 8).transpose(1, 0, 2).reshape(128, 64))
    return ctab, trig, masks, idn, epsr, noise, bm, affwT


def kernel(**inputs):
    T = inputs["L0_attn_x"].shape[1]
    NCORES = inputs["L0_attn_x"].shape[0]
    ctab, trig, masks, idn, epsr, noise, bm, affwT = _host_consts(T, NCORES, inputs)
    two = True
    asc = np.ascontiguousarray
    shared = {
        "wqT": asc(inputs["L0_attn_Wq"].T), "wkT": asc(inputs["L0_attn_Wk"].T),
        "wvT": asc(inputs["L0_attn_Wv"].T), "wgT": asc(inputs["L0_attn_Wg"].T),
        "woT": asc(inputs["L0_attn_Wo"].T),
        "gw1": asc(inputs["L0_ffn_gate_w1"]), "gw2": asc(inputs["L0_ffn_gate_w2"]),
        "gw3": asc(inputs["L0_ffn_gate_w3"]), "affwT": affwT,
        "e0w1": asc(inputs["L0_ffn_e0_w1"]), "e0w2": asc(inputs["L0_ffn_e0_w2"]),
        "e1w1": asc(inputs["L0_ffn_e1_w1"]), "e1w2": asc(inputs["L0_ffn_e1_w2"]),
        "e1w3": asc(inputs["L0_ffn_e1_w3"]), "e2w1": asc(inputs["L0_ffn_e2_w1"]),
        "e2w2": asc(inputs["L0_ffn_e2_w2"]), "biasmat": bm, "ctab": ctab,
        "trig": trig, "masks": masks, "idn": idn,
        "eu": asc(inputs["L0_ffn_expert_usage"].reshape(1, E)), "epsrow": epsr,
    }
    if not two:
        nc = build_nc(T, NCORES)
        in_maps = []
        for b in range(NCORES):
            m = dict(shared)
            m["x"] = asc(inputs["L0_attn_x"][b])
            m["noiseT"] = asc(noise[b].T)
            in_maps.append(m)
        res = run_bass_kernel_spmd(nc, in_maps, core_ids=list(range(NCORES)))
        return np.stack([np.asarray(r["out"]) for r in res.results], axis=0)
    nca = build_nc(T, NCORES, mode="attn")
    maps_a = []
    for b in range(NCORES):
        m = dict(shared)
        m["x"] = asc(inputs["L0_attn_x"][b])
        m["noiseT"] = asc(noise[b].T)
        maps_a.append(m)
    ra = run_bass_kernel_spmd(nca, maps_a, core_ids=list(range(NCORES)))
    x1s = [np.asarray(r["x1out"]) for r in ra.results]
    gimp = np.ascontiguousarray(
        np.concatenate([np.asarray(r["impout"]) for r in ra.results], axis=0))
    ncf = build_nc(T, NCORES, mode="ffn")
    maps_f = []
    for b in range(NCORES):
        m = dict(shared)
        m["x"] = asc(inputs["L0_attn_x"][b])
        m["noiseT"] = asc(noise[b].T)
        m["x1in"] = asc(x1s[b])
        m["gimp"] = gimp
        maps_f.append(m)
    rf = run_bass_kernel_spmd(ncf, maps_f, core_ids=list(range(NCORES)))
    return np.stack([np.asarray(r["out"]) for r in rf.results], axis=0)


# version 06
# speedup vs baseline: 1.4871x; 1.4871x over previous
import math
from contextlib import ExitStack

import numpy as np
import concourse.bass as bass
import concourse.mybir as mybir
from concourse.bass_utils import run_bass_kernel_spmd

F32 = mybir.dt.float32
BF16 = mybir.dt.bfloat16
AF = mybir.ActivationFunctionType
ALU = mybir.AluOpType
AX = mybir.AxisListType

HID = 1024
H = 8
DH = 64
VH = 128
E = 8
GAMMA = 1.0 - 2.0 ** (-5.0 - np.arange(H, dtype=np.float64))
C1G = 1.5957691216057308
C2G = 0.044715


def ctab_layout(T):
    NT = T // 128
    NDIFF = NT + 3
    o = {}
    c = 0
    o["ESC"] = c; c += H * NDIFF
    o["GNW"] = c; c += 8
    o["GNB"] = c; c += 8
    o["GB1"] = c; c += 8
    o["GB2"] = c; c += 4
    o["GB3"] = c; c += 1
    o["E0B1"] = c; c += 96
    o["E1B1"] = c; c += 48
    o["E1B2"] = c; c += 48
    o["E2B1"] = c; c += 48
    o["LNS"] = c; c += 8
    o["LNO"] = c; c += 8
    o["AFFB"] = c; c += 8
    o["W"] = c
    return o


class _PhaseDone(Exception):
    pass


class Res:
    __slots__ = ("w", "r")

    def __init__(self):
        self.w = None
        self.r = []


class Sched:
    def __init__(self):
        self.q = {}

    def add(self, name, sems, inc):
        self.q[name] = dict(sems=list(sems), inc=inc, idx=0,
                            uses=[0] * len(sems), hw={}, prog=[])

    def emit(self, qn, method, *args, reads=(), writes=(), **kw):
        q = self.q[qn]
        need = {}

        def _n(tag):
            if tag is None:
                return
            tq, slot, tc = tag
            if tq == qn == "pe":
                return
            k = (tq, slot)
            if need.get(k, 0) < tc:
                need[k] = tc

        for r in reads:
            _n(r.w)
        for w in writes:
            _n(w.w)
            for tag in w.r:
                _n(tag)
        alt = kw.pop("alt_q", None)
        qs = self.q[alt] if alt else q
        an = alt if alt else qn
        nsl = len(qs["sems"])
        slot = qs["idx"] % nsl
        if qs["inc"] != 1 and qs["uses"][slot] > 0:
            k = (an, slot)
            tc = qs["inc"] * qs["uses"][slot]
            if need.get(k, 0) < tc:
                need[k] = tc
        for k in sorted(need):
            tc = need[k]
            tq, tslot = k
            if q["hw"].get(k, 0) >= tc:
                continue
            q["prog"].append(("w", self.q[tq]["sems"][tslot], tc, None))
            q["hw"][k] = tc
        qs["idx"] += 1
        qs["uses"][slot] += 1
        q["prog"].append(("i", method, (args, kw), (an, slot, qs["inc"])))
        tag = (an, slot, qs["inc"] * qs["uses"][slot])
        for w in writes:
            w.w = tag
            w.r = []
        for r in reads:
            r.r.append(tag)

    def rawwait(self, qn, sem, val):
        self.q[qn]["prog"].append(("w", sem, val, None))

    def qdone(self, qn):
        q = self.q[qn]
        return [(q["sems"][s], q["inc"] * q["uses"][s])
                for s in range(len(q["sems"])) if q["uses"][s] > 0]

    def replay(self, qn, eng):
        q = self.q[qn]
        for kind, a, b, inc in q["prog"]:
            if kind == "w":
                eng.wait_ge(a, b)
            else:
                args, kw = b
                an, slot, v = inc
                getattr(eng, a)(*args, **kw).then_inc(self.q[an]["sems"][slot], v)


def build_nc(T, NCORES, dbg=False, phases=99, mode="full"):
    NT = T // 128
    NCH = T // 512
    ETOK = 1024 if T >= 1024 else T
    NG = T // ETOK
    ENC = ETOK // 512
    NTT = ETOK // 128
    NDIFF = NT + 3
    CO = ctab_layout(T)
    CTW = CO["W"]
    NROWS = NCORES * NT
    assert NROWS <= 128
    NTOT = NCORES * T
    KPOS = math.floor(0.9 * (NTOT - 1))
    FRAC = 0.9 * (NTOT - 1) - KPOS

    nc = bass.Bass("TRN2", target_bir_lowering=False)
    dt = nc.dram_tensor
    x_in = dt("x", [T, HID], F32, kind="ExternalInput").ap()
    wqT = dt("wqT", [HID, H * DH], F32, kind="ExternalInput").ap()
    wkT = dt("wkT", [HID, H * DH], F32, kind="ExternalInput").ap()
    wvT = dt("wvT", [HID, H * VH], F32, kind="ExternalInput").ap()
    wgT = dt("wgT", [HID, H * VH], F32, kind="ExternalInput").ap()
    woT = dt("woT", [H * VH, HID], F32, kind="ExternalInput").ap()
    gw1 = dt("gw1", [HID, HID], F32, kind="ExternalInput").ap()
    gw2 = dt("gw2", [HID, 512], F32, kind="ExternalInput").ap()
    gw3 = dt("gw3", [512, E], F32, kind="ExternalInput").ap()
    affwT = dt("affwT", [128, 64], F32, kind="ExternalInput").ap()
    e0w1 = dt("e0w1", [3, HID, 4096], F32, kind="ExternalInput").ap()
    e0w2 = dt("e0w2", [3, 4096, HID], F32, kind="ExternalInput").ap()
    e1w1 = dt("e1w1", [3, HID, 2048], F32, kind="ExternalInput").ap()
    e1w2 = dt("e1w2", [3, 2048, 2048], F32, kind="ExternalInput").ap()
    e1w3 = dt("e1w3", [3, 2048, HID], F32, kind="ExternalInput").ap()
    e2w1 = dt("e2w1", [2, HID, 3072], F32, kind="ExternalInput").ap()
    e2w2 = dt("e2w2", [2, 3072, HID], F32, kind="ExternalInput").ap()
    biasmat = dt("biasmat", [E, HID], F32, kind="ExternalInput").ap()
    ctab_d = dt("ctab", [128, CTW], F32, kind="ExternalInput").ap()
    trig_d = dt("trig", [64, 2 * T], F32, kind="ExternalInput").ap()
    masks_d = dt("masks", [128, 4 * 512], F32, kind="ExternalInput").ap()
    idn_d = dt("idn", [128, 128], F32, kind="ExternalInput").ap()
    noise_d = dt("noiseT", [E, T], F32, kind="ExternalInput").ap()
    eu_d = dt("eu", [1, E], F32, kind="ExternalInput").ap()
    epsr_d = dt("epsrow", [H, T], F32, kind="ExternalInput").ap()
    out_d = dt("out", [T, HID], F32, kind="ExternalOutput").ap()
    x1sp = dt("x1spill", [T, HID], F32).ap()
    if mode == "attn":
        x1out = dt("x1out", [T, HID], F32, kind="ExternalOutput").ap()
        impout = dt("impout", [NT, 128], F32, kind="ExternalOutput").ap()
    if mode == "ffn":
        x1in = dt("x1in", [T, HID], F32, kind="ExternalInput").ap()
        gimp = dt("gimp", [NROWS, 128], F32, kind="ExternalInput").ap()
    if dbg:
        d_ln0T = dt("d_ln0T", [128, 8 * T], F32).ap()
        d_q = dt("d_q", [128, 4 * T], F32).ap()
        d_k = dt("d_k", [128, 4 * T], F32).ap()
        d_v = dt("d_v", [128, 8 * T], F32).ap()
        d_g = dt("d_g", [128, 8 * T], F32).ap()
        d_of = dt("d_of", [128, 8 * T], F32).ap()
        d_ao = dt("d_ao", [128, 8 * T], F32).ap()
        d_sm = dt("d_sm", [128, 18], F32, kind="ExternalOutput").ap()
        d_xb = dt("d_xb", [128, 16], F32, kind="ExternalOutput").ap()
        d_s1 = dt("d_s1", [128, 1024], F32).ap()
    cc_in = dt("cc_in", [NT, 128], F32)
    cc_out = dt("cc_out", [NROWS, 128], F32)

    es = ExitStack()
    sb = lambda n, shp, d: es.enter_context(nc.sbuf_tensor(n, shp, d))
    mksem = lambda n: es.enter_context(nc.semaphore(n))

    AWW = max(8 * T, NT * 512 + 8192, NT * 1024)
    AW = sb("AW", [128, AWW], BF16)
    AQ = sb("AQ", [128, 4 * T], BF16)
    AK = sb("AK", [128, 4 * T], BF16)
    AVW = max(8 * T, 16 * ETOK)
    AV = sb("AV", [128, AVW], BF16)
    AG = sb("AG", [128, AVW], BF16)
    WB = sb("WB", [128, 4096], BF16)
    S1 = sb("S1", [128, 2 * T], BF16)
    S2 = sb("S2", [128, T], F32)
    S3 = sb("S3", [128, T], F32)
    V1 = sb("V1", [1, T], F32)
    V2 = sb("V2", [1, T], F32)
    SM = sb("SM", [128, 112], F32)
    PRB = sb("PRB", [128, NT * E], F32)
    GI = sb("GI", [128, 128], F32)
    LGS = sb("LGS", [E, 512], F32)
    CT = sb("CT", [128, CTW], F32)
    TRG = sb("TRG", [64, 2 * T], BF16)
    MSK = sb("MSK", [128, 4 * 512], BF16)
    IDB = sb("IDB", [128, 128], BF16)
    IDF = sb("IDF", [128, 128], F32)
    ONEB = sb("ONEB", [128, 1], BF16)
    ONEF = sb("ONEF", [128, 1], F32)
    ONRF = sb("ONRF", [1, 128], F32)
    BM = sb("BM", [E, HID], BF16)
    W3B = sb("W3B", [128, 4 * E], BF16)
    AFW = sb("AFW", [128, 64], F32)
    EUS = sb("EUS", [1, 16], F32)
    SC1 = sb("SC1", [1, 160], F32)
    SCB = sb("SCB", [128, 8], F32)
    NZE = sb("NZE", [128, 8], F32)
    NEGT = sb("NEGT", [128, 8], F32)
    PRT = sb("PRT", [E, T], BF16)

    ps_es = ExitStack()
    P = [ps_es.enter_context(nc.psum_tensor(f"p{i}", [128, 512], F32)) for i in range(6)]
    PB = ps_es.enter_context(nc.psum_tensor("pb", [128, 1024], BF16))
    PS = ps_es.enter_context(nc.psum_tensor("pst", [1, 512], F32))

    sd = Sched()
    R = {}

    def rs(*names):
        out = []
        for n in names:
            if n not in R:
                R[n] = Res()
            out.append(R[n])
        return out

    (rAW, rAQ, rAK, rAV, rAG, rWB0, rWB1, rS1, rS2, rS3a, rS3b, rV1, rV2, rSM,
     rPRB, rGI, rLGS, rCT, rTRG, rMSK, rIDB, rIDF, rONEB, rONEF, rONRF, rBM,
     rW3B, rAFW, rEUS, rSC1, rSCB, rNZE, rNEGT, rPRT, rX1SP, rCCI, rCCO, rOUT, rPBp,
     rPSp) = rs("AW", "AQ", "AK", "AV", "AG", "WB0", "WB1", "S1", "S2", "S3a",
                "S3b", "V1", "V2", "SM", "PRB", "GI", "LGS", "CT", "TRG", "MSK",
                "IDB", "IDF", "ONEB", "ONEF", "ONRF", "BM", "W3B", "AFW", "EUS",
                "SC1", "SCB", "NZE", "NEGT", "PRT", "X1SP", "CCI", "CCO", "OUT", "PBp",
                "PSp")
    rP = rs(*[f"P{i}" for i in range(6)])
    (res_impout,) = rs("IMPOUT")
    rPO = rs(*[f"PO{i}" for i in range(8)])
    rWB = [rWB0, rWB1]
    S3_HALF = [rS3a, rS3b]

    try:
     if True:
        for n in ("pe", "dve", "act"):
            sd.add(n, [mksem("s_" + n)], 1)
        sd.add("gd", [mksem(f"s_gd{i}") for i in range(8)], 16)
        sd.add("sy", [mksem(f"s_sy{i}") for i in range(8)], 16)
        sd.add("cc", [mksem("s_cc")], 16)
        E_ = sd.emit

        def mm(out_ap, lhsT_ap, rhs_ap, rd, wr, start, stop):
            E_("pe", "matmul", reads=tuple(rd), writes=tuple(wr),
               out=out_ap, lhsT=lhsT_ap, rhs=rhs_ap, start=start, stop=stop)

        E_("sy", "dma_start", reads=(), writes=(rCT,), out=CT[:, :], in_=ctab_d[:, :])
        E_("gd", "dma_start", reads=(), writes=(rTRG,), out=TRG[:, :], in_=trig_d[:, :])
        E_("gd", "dma_start", reads=(), writes=(rMSK,), out=MSK[:, :], in_=masks_d[:, :])
        E_("gd", "dma_start", reads=(), writes=(rIDB,), out=IDB[:, :], in_=idn_d[:, :])
        E_("sy", "dma_start", reads=(), writes=(rIDF,), out=IDF[:, :], in_=idn_d[:, :])
        E_("gd", "dma_start", reads=(), writes=(rBM,), out=BM[:, :], in_=biasmat[:, :])
        E_("gd", "dma_start", reads=(), writes=(rW3B,), out=W3B[:, :],
           in_=gw3.rearrange("(kt p) e -> p kt e", p=128))
        E_("sy", "dma_start", reads=(), writes=(rAFW,), out=AFW[:, :], in_=affwT[:, :])
        E_("sy", "dma_start", reads=(), writes=(rEUS,), out=EUS[:, 0:E], in_=eu_d[:, :])
        E_("dve", "memset", reads=(), writes=(rONEB,), ap=ONEB[:, :], constant=1.0)
        E_("dve", "memset", reads=(), writes=(rONEF,), ap=ONEF[:, :], constant=1.0)
        E_("dve", "memset", reads=(), writes=(rONRF,), ap=ONRF[:, :], constant=1.0)
        E_("dve", "memset", reads=(), writes=(rNZE,), ap=NZE[:, :], constant=0.0)
        E_("dve", "memset", reads=(rNZE,), writes=(rNZE,), ap=NZE[:, 0:2], constant=0.125)
        E_("dve", "memset", reads=(), writes=(rNEGT,), ap=NEGT[:, :], constant=-1e9)

        def ln_stats(src_ap, src_res):
            E_("dve", "bn_stats", reads=(src_res,), writes=(rSM,),
               out=SM[:, 80:86], in_=src_ap[:, 0:512])
            E_("dve", "bn_stats", reads=(src_res,), writes=(rSM,),
               out=SM[:, 86:92], in_=src_ap[:, 512:1024])
            E_("dve", "bn_aggr", reads=(rSM,), writes=(rSM,),
               out=SM[:, 92:94], in_=SM[:, 80:92])
            E_("dve", "tensor_scalar", reads=(rSM,), writes=(rSM,),
               out=SM[:, 94:95], in0=SM[:, 93:94], scalar1=1e-5, scalar2=None, op0=ALU.add)
            E_("act", "activation", reads=(rSM,), writes=(rSM,),
               out=SM[:, 95:96], in_=SM[:, 94:95], func=AF.Ln)
            E_("act", "activation", reads=(rSM,), writes=(rSM,),
               out=SM[:, 96:97], in_=SM[:, 95:96], func=AF.Exp, scale=-0.5)
            E_("dve", "scalar_tensor_tensor", reads=(rSM,), writes=(rSM,),
               out=SM[:, 97:98], in0=SM[:, 92:93], scalar=-1.0, in1=SM[:, 96:97],
               op0=ALU.mult, op1=ALU.mult)

        skip_attn = (mode == "ffn")
        if not skip_attn:
            for i in range(NT):
                hb = i % 2
                xbuf = S3[:, hb * 1024:hb * 1024 + 1024]
                E_("sy", "dma_start", reads=(), writes=(S3_HALF[hb],),
                   out=xbuf, in_=x_in[i * 128:(i + 1) * 128, :])
                ln_stats(xbuf, S3_HALF[hb])
                E_("act", "activation", reads=(S3_HALF[hb], rSM), writes=(rS1,),
                   out=S1[:, 0:1024], in_=xbuf, func=AF.Identity, scale=SM[:, 96:97],
                   bias=SM[:, 97:98])
                if dbg and i == 0:
                    E_("gd", "dma_start", reads=(rSM,), writes=(), out=d_sm[:, :],
                       in_=SM[:, 80:98])
                    E_("gd", "dma_start", reads=(S3_HALF[0],), writes=(), out=d_xb[:, :],
                       in_=S3[:, 0:16])
                    E_("gd", "dma_start", reads=(rS1,), writes=(), out=d_s1[:, :],
                       in_=S1[:, 0:1024])
                for db in range(8):
                    E_("pe", "transpose", reads=(rS1, rIDB), writes=(rPBp,),
                       out=PB[:, 0:128], in_=S1[:, db * 128:(db + 1) * 128], identity=IDB[:, :])
                    E_("dve", "tensor_copy", reads=(rPBp,), writes=(rAW,),
                       out=AW[:, db * T + i * 128: db * T + (i + 1) * 128], in_=PB[:, 0:128])

            if dbg:
                E_("gd", "dma_start", reads=(rAW,), writes=(), out=d_ln0T[:, :],
                   in_=AW[:, 0:8 * T])
            if phases < 2:
                raise _PhaseDone()
            def rotary(dest, rdest, mt, src32_res):
                cosF = TRG[0:64, 0:T]
                sinF = TRG[0:64, T:2 * T]
                for par in range(2):
                    if par == 1:
                        E_("dve", "tensor_copy", reads=(src32_res,), writes=(src32_res,),
                           out=S2[0:64, :], in_=S2[64:128, :])
                    E_("dve", "tensor_scalar", reads=(src32_res,), writes=(rS3a, rS3b),
                       out=S3[0:32, :], in0=S2[32:64, :], scalar1=-1.0, scalar2=None,
                       op0=ALU.mult)
                    E_("dve", "tensor_copy", reads=(src32_res,), writes=(rS3a, rS3b),
                       out=S3[32:64, :], in_=S2[0:32, :])
                    E_("dve", "tensor_tensor", reads=(rS3a, rS3b, rTRG), writes=(rS3a, rS3b),
                       out=S3[0:64, :], in0=S3[0:64, :], in1=sinF, op=ALU.mult)
                    E_("dve", "tensor_tensor", reads=(src32_res, rTRG), writes=(src32_res,),
                       out=S2[0:64, :], in0=S2[0:64, :], in1=cosF, op=ALU.mult)
                    E_("dve", "tensor_tensor", reads=(src32_res, rS3a, rS3b), writes=(rdest,),
                       out=dest[64 * par:64 * par + 64, mt * T:(mt + 1) * T],
                       in0=S2[0:64, :], in1=S3[0:64, :], op=ALU.add)

            for wap, dest, rdest in ((wqT, AQ, rAQ), (wkT, AK, rAK)):
                E_("gd", "dma_start", reads=(), writes=(rWB0, rWB1),
                   out=WB[:, 0:4096], in_=wap.rearrange("(kt p) m -> p kt m", p=128))
                for mt in range(4):
                    for c in range(NCH):
                        pi = c % 2
                        for kt in range(8):
                            mm(P[pi][:, :], WB[:, kt * 512 + mt * 128: kt * 512 + (mt + 1) * 128],
                               AW[:, kt * T + c * 512: kt * T + (c + 1) * 512],
                               rd=[rWB0, rWB1, rAW], wr=[rP[pi]], start=(kt == 0), stop=(kt == 7))
                        E_("act", "activation", reads=(rP[pi],), writes=(rS2,),
                           out=S2[:, c * 512:(c + 1) * 512], in_=P[pi][:, :], func=AF.Identity)
                    rotary(dest, rdest, mt, rS2)
            for half in range(2):
                E_("gd", "dma_start", reads=(), writes=(rWB0, rWB1),
                   out=WB[:, 0:4096],
                   in_=wvT[:, half * 512:(half + 1) * 512].rearrange("(kt p) m -> p kt m", p=128))
                for tt in range(NT):
                    pi = tt % 2
                    for kt in range(8):
                        mm(P[pi][:, :], AW[:, kt * T + tt * 128: kt * T + (tt + 1) * 128],
                           WB[:, kt * 512:(kt + 1) * 512],
                           rd=[rWB0, rWB1, rAW], wr=[rP[pi]], start=(kt == 0), stop=(kt == 7))
                    E_("act", "activation", reads=(rP[pi],), writes=(rAV,),
                       out=AV[:, tt * 1024 + half * 512: tt * 1024 + (half + 1) * 512],
                       in_=P[pi][:, :], func=AF.Identity)
            for half in range(2):
                E_("gd", "dma_start", reads=(), writes=(rWB0, rWB1),
                   out=WB[:, 0:4096],
                   in_=wgT[:, half * 512:(half + 1) * 512].rearrange("(kt p) m -> p kt m", p=128))
                for mt in range(4):
                    gmt = half * 4 + mt
                    for c in range(NCH):
                        pi = 2 + c % 2
                        for kt in range(8):
                            mm(P[pi][:, :],
                               WB[:, kt * 512 + mt * 128: kt * 512 + (mt + 1) * 128],
                               AW[:, kt * T + c * 512: kt * T + (c + 1) * 512],
                               rd=[rWB0, rWB1, rAW], wr=[rP[pi]], start=(kt == 0), stop=(kt == 7))
                        E_("act", "activation", reads=(rP[pi],), writes=(rAG,),
                           out=AG[:, gmt * T + c * 512: gmt * T + (c + 1) * 512],
                           in_=P[pi][:, :], func=AF.Identity)
            for gmt in range(8):
                E_("act", "activation", reads=(rAG,), writes=(rAG,),
                   out=AG[:, gmt * T:(gmt + 1) * T], in_=AG[:, gmt * T:(gmt + 1) * T],
                   func=AF.Silu)

            if dbg:
                E_("gd", "dma_start", reads=(rAQ,), writes=(), out=d_q[:, :], in_=AQ[:, :])
                E_("gd", "dma_start", reads=(rAK,), writes=(), out=d_k[:, :], in_=AK[:, :])
                E_("gd", "dma_start", reads=(rAV,), writes=(), out=d_v[:, :], in_=AV[:, 0:8 * T])
                E_("gd", "dma_start", reads=(rAG,), writes=(), out=d_g[:, :], in_=AG[:, 0:8 * T])
            if phases < 3:
                raise _PhaseDone()
            WOC = AWW - 8192
            E_("gd", "dma_start", reads=(rAW,), writes=(rAW,),
               out=AW[:, WOC:WOC + 8192], in_=woT.rearrange("(ct p) o -> p ct o", p=128))
            for h in range(H):
                mt, par = h // 2, h % 2
                base = 64 * par
                E_("sy", "dma_start", reads=(), writes=(rS3a, rS3b),
                   out=S3[0:1, 0:T], in_=epsr_d[h:h + 1, :])
                for c in range(NCH):
                    nst = 4 * c + 4
                    for st in range(nst):
                        pi = st % 2
                        di = 4 * c - st + 3
                        mm(P[pi][:, :], AK[base:base + 64, mt * T + st * 128: mt * T + (st + 1) * 128],
                           AQ[base:base + 64, mt * T + c * 512: mt * T + (c + 1) * 512],
                           rd=[rAK, rAQ], wr=[rP[pi]], start=True, stop=True)
                        E_("act", "activation", reads=(rP[pi], rCT), writes=(rAW,),
                           out=AW[:, st * 512:(st + 1) * 512], in_=P[pi][:, :], func=AF.Identity,
                           scale=CT[:, CO["ESC"] + h * NDIFF + di: CO["ESC"] + h * NDIFF + di + 1])
                        if st >= 4 * c:
                            k = st - 4 * c
                            E_("dve", "tensor_tensor", reads=(rAW, rMSK), writes=(rAW,),
                               out=AW[:, st * 512:(st + 1) * 512],
                               in0=AW[:, st * 512:(st + 1) * 512],
                               in1=MSK[:, k * 512:(k + 1) * 512], op=ALU.mult)
                    po = 4 + (c % 2)
                    for st in range(nst):
                        mm(P[po][:, :], AV[:, st * 1024 + h * 128: st * 1024 + h * 128 + 128],
                           AW[:, st * 512:(st + 1) * 512],
                           rd=[rAV, rAW], wr=[rP[po]], start=(st == 0), stop=(st == nst - 1))
                    E_("act", "activation", reads=(rP[po],), writes=(rS1,),
                       out=S1[:, c * 512:(c + 1) * 512], in_=P[po][:, :], func=AF.Identity)
                E_("act", "activation", reads=(rS1,), writes=(rS1,),
                   out=S1[:, T:2 * T], in_=S1[:, 0:T], func=AF.Square)
                for c in range(NCH):
                    mm(PS[:, :], ONEB[:, :], S1[:, c * 512:(c + 1) * 512],
                       rd=[rONEB, rS1], wr=[rPSp], start=True, stop=True)
                    E_("act", "activation", reads=(rPSp,), writes=(rV1,),
                       out=V1[:, c * 512:(c + 1) * 512], in_=PS[:, :], func=AF.Identity)
                    mm(PS[:, :], ONEB[:, :], S1[:, T + c * 512: T + (c + 1) * 512],
                       rd=[rONEB, rS1], wr=[rPSp], start=True, stop=True)
                    E_("act", "activation", reads=(rPSp,), writes=(rV2,),
                       out=V2[:, c * 512:(c + 1) * 512], in_=PS[:, :], func=AF.Identity)
                V3 = S2[0:1, 0:T]
                E_("dve", "tensor_scalar", reads=(rV1,), writes=(rV1,),
                   out=V1[:, :], in0=V1[:, :], scalar1=1.0 / VH, scalar2=None, op0=ALU.mult)
                E_("dve", "tensor_scalar", reads=(rV2,), writes=(rV2,),
                   out=V2[:, :], in0=V2[:, :], scalar1=1.0 / VH, scalar2=None, op0=ALU.mult)
                E_("dve", "tensor_tensor", reads=(rV1,), writes=(rS2,),
                   out=V3, in0=V1[:, :], in1=V1[:, :], op=ALU.mult)
                E_("dve", "tensor_tensor", reads=(rV2, rS2), writes=(rV2,),
                   out=V2[:, :], in0=V2[:, :], in1=V3, op=ALU.subtract)
                E_("dve", "tensor_tensor", reads=(rV2, rS3a, rS3b), writes=(rV2,),
                   out=V2[:, :], in0=V2[:, :], in1=S3[0:1, 0:T], op=ALU.add)
                E_("act", "activation", reads=(rV2,), writes=(rV2,),
                   out=V2[:, :], in_=V2[:, :], func=AF.Ln)
                E_("act", "activation", reads=(rV2,), writes=(rV2,),
                   out=V2[:, :], in_=V2[:, :], func=AF.Exp, scale=-0.5)
                E_("dve", "scalar_tensor_tensor", reads=(rV1, rV2), writes=(rS2,),
                   out=V3, in0=V1[:, :], scalar=-1.0, in1=V2[:, :],
                   op0=ALU.mult, op1=ALU.mult)
                for c in range(NCH):
                    mm(P[0][:, :], ONRF[:, :], V2[:, c * 512:(c + 1) * 512],
                       rd=[rONRF, rV2], wr=[rP[0]], start=True, stop=True)
                    mm(P[1][:, :], ONRF[:, :], S2[0:1, c * 512:(c + 1) * 512],
                       rd=[rONRF, rS2], wr=[rP[1]], start=True, stop=True)
                    tmp = S1[:, T + c * 512: T + (c + 1) * 512]
                    E_("dve", "tensor_tensor", reads=(rS1, rP[0]), writes=(rS1,),
                       out=tmp, in0=S1[:, c * 512:(c + 1) * 512], in1=P[0][:, :], op=ALU.mult)
                    E_("dve", "tensor_tensor", reads=(rS1, rP[1]), writes=(rS1,),
                       out=tmp, in0=tmp, in1=P[1][:, :], op=ALU.add)
                    E_("act", "activation", reads=(rS1, rCT), writes=(rS1,),
                       out=tmp, in_=tmp, func=AF.Identity,
                       scale=CT[:, CO["GNW"] + h:CO["GNW"] + h + 1],
                       bias=CT[:, CO["GNB"] + h:CO["GNB"] + h + 1])
                    E_("dve", "tensor_tensor", reads=(rS1, rAG), writes=(rAG,),
                       out=AG[:, h * T + c * 512: h * T + (c + 1) * 512],
                       in0=tmp, in1=AG[:, h * T + c * 512: h * T + (c + 1) * 512],
                       op=ALU.mult)

            if dbg:
                E_("gd", "dma_start", reads=(rAG,), writes=(), out=d_of[:, :], in_=AG[:, 0:8 * T])
            if phases < 4:
                raise _PhaseDone()
            for m in range(8):
                dest, dres = (AQ, rAQ) if m < 4 else (AK, rAK)
                for c in range(NCH):
                    pi = c % 2
                    for h in range(H):
                        mm(P[pi][:, :],
                           AW[:, WOC + h * 1024 + m * 128: WOC + h * 1024 + (m + 1) * 128],
                           AG[:, h * T + c * 512: h * T + (c + 1) * 512],
                           rd=[rAW, rAG], wr=[rP[pi]], start=(h == 0), stop=(h == 7))
                    E_("act", "activation", reads=(rP[pi],), writes=(dres,),
                       out=dest[:, (m % 4) * T + c * 512:(m % 4) * T + (c + 1) * 512],
                       in_=P[pi][:, :], func=AF.Identity)
            if dbg:
                E_("gd", "dma_start", reads=(rAQ,), writes=(), out=d_ao[:, 0:4 * T], in_=AQ[:, :])
                E_("gd", "dma_start", reads=(rAK,), writes=(), out=d_ao[:, 4 * T:8 * T], in_=AK[:, :])
            for c in range(NCH):
                for m in range(8):
                    src_t, sres = (AQ, rAQ) if m < 4 else (AK, rAK)
                    sl = src_t[:, (m % 4) * T + c * 512:(m % 4) * T + (c + 1) * 512]
                    mm(PS[:, :], ONEB[:, :], sl, rd=[rONEB, sres], wr=[rPSp],
                       start=(m == 0), stop=(m == 7))
                E_("act", "activation", reads=(rPSp,), writes=(rV1,),
                   out=V1[:, c * 512:(c + 1) * 512], in_=PS[:, :], func=AF.Identity)
                for m in range(8):
                    src_t, sres = (AQ, rAQ) if m < 4 else (AK, rAK)
                    sl = src_t[:, (m % 4) * T + c * 512:(m % 4) * T + (c + 1) * 512]
                    E_("act", "activation", reads=(sres,), writes=(rS1,),
                       out=S1[:, 0:512], in_=sl, func=AF.Square)
                    mm(PS[:, :], ONEB[:, :], S1[:, 0:512], rd=[rONEB, rS1], wr=[rPSp],
                       start=(m == 0), stop=(m == 7))
                E_("act", "activation", reads=(rPSp,), writes=(rV2,),
                   out=V2[:, c * 512:(c + 1) * 512], in_=PS[:, :], func=AF.Identity)
            V3 = S2[0:1, 0:T]
            E_("dve", "tensor_scalar", reads=(rV1,), writes=(rV1,),
               out=V1[:, :], in0=V1[:, :], scalar1=1.0 / HID, scalar2=None, op0=ALU.mult)
            E_("dve", "tensor_scalar", reads=(rV2,), writes=(rV2,),
               out=V2[:, :], in0=V2[:, :], scalar1=1.0 / HID, scalar2=None, op0=ALU.mult)
            E_("dve", "tensor_tensor", reads=(rV1,), writes=(rS2,),
               out=V3, in0=V1[:, :], in1=V1[:, :], op=ALU.mult)
            E_("dve", "tensor_tensor", reads=(rV2, rS2), writes=(rV2,),
               out=V2[:, :], in0=V2[:, :], in1=V3, op=ALU.subtract)
            E_("dve", "tensor_scalar", reads=(rV2,), writes=(rV2,),
               out=V2[:, :], in0=V2[:, :], scalar1=1e-5, scalar2=None, op0=ALU.add)
            E_("act", "activation", reads=(rV2,), writes=(rV2,),
               out=V2[:, :], in_=V2[:, :], func=AF.Ln)
            E_("act", "activation", reads=(rV2,), writes=(rV2,),
               out=V2[:, :], in_=V2[:, :], func=AF.Exp, scale=-0.5)
            E_("dve", "scalar_tensor_tensor", reads=(rV1, rV2), writes=(rS2,),
               out=V3, in0=V1[:, :], scalar=-1.0, in1=V2[:, :], op0=ALU.mult, op1=ALU.mult)
            for i in range(NT):
                E_("pe", "transpose", reads=(rV2, rIDF), writes=(rP[0],),
                   out=P[0][:, 0:1], in_=V2[:, i * 128:(i + 1) * 128], identity=IDF[0:1, 0:1])
                E_("act", "activation", reads=(rP[0],), writes=(rSM,),
                   out=SM[:, 16 + i:17 + i], in_=P[0][:, 0:1], func=AF.Identity)
                E_("pe", "transpose", reads=(rS2, rIDF), writes=(rP[1],),
                   out=P[1][:, 0:1], in_=S2[0:1, i * 128:(i + 1) * 128], identity=IDF[0:1, 0:1])
                E_("act", "activation", reads=(rP[1],), writes=(rSM,),
                   out=SM[:, 32 + i:33 + i], in_=P[1][:, 0:1], func=AF.Identity)
            for i in range(NT):
                hb = i % 2
                xbuf = S3[:, hb * 1024:hb * 1024 + 1024]
                E_("sy", "dma_start", reads=(), writes=(S3_HALF[hb],),
                   out=xbuf, in_=x_in[i * 128:(i + 1) * 128, :])
                for m in range(8):
                    src_t, sres = (AQ, rAQ) if m < 4 else (AK, rAK)
                    E_("pe", "transpose", reads=(sres, rIDB), writes=(rPBp,),
                       out=PB[:, (m % 2) * 128:(m % 2) * 128 + 128],
                       in_=src_t[:, (m % 4) * T + i * 128:(m % 4) * T + (i + 1) * 128],
                       identity=IDB[:, :])
                    E_("act", "activation", reads=(rPBp, rSM), writes=(rS1,),
                       out=S1[:, m * 128:(m + 1) * 128],
                       in_=PB[:, (m % 2) * 128:(m % 2) * 128 + 128],
                       func=AF.Identity, scale=SM[:, 16 + i:17 + i], bias=SM[:, 32 + i:33 + i])
                x1buf = S2[:, 0:1024]
                E_("dve", "tensor_tensor", reads=(S3_HALF[hb], rS1), writes=(rS2,),
                   out=x1buf, in0=xbuf, in1=S1[:, 0:1024], op=ALU.add)
                ln_stats(x1buf, rS2)
                E_("dve", "reciprocal", reads=(rSM,), writes=(rSM,),
                   out=SM[:, 98:99], in_=SM[:, 94:95])
                E_("dve", "tensor_tensor", reads=(rSM,), writes=(rSM,),
                   out=SM[:, 99:100], in0=SM[:, 93:94], in1=SM[:, 98:99], op=ALU.mult)
                E_("act", "activation", reads=(rSM,), writes=(rSM,),
                   out=SM[:, 64 + i:65 + i], in_=SM[:, 99:100], func=AF.Sqrt, scale=float(HID))
                E_("act", "activation", reads=(rS2, rSM), writes=(rS1, rSM),
                   out=S1[:, 0:1024], in_=x1buf, func=AF.Abs, scale=SM[:, 96:97],
                   bias=SM[:, 97:98], accum_out=SM[:, i:i + 1])
                E_("dve", "tensor_copy", reads=(rSM,), writes=(rSM,),
                   out=SM[:, 32 + i:33 + i], in_=SM[:, 96:97])
                E_("dve", "tensor_copy", reads=(rSM,), writes=(rSM,),
                   out=SM[:, 48 + i:49 + i], in_=SM[:, 97:98])
                E_("sy", "dma_start", reads=(rS2,), writes=(rX1SP,),
                   out=(x1out if mode == "attn" else x1sp)[i * 128:(i + 1) * 128, :],
                   in_=x1buf)

            if phases < 5:
                raise _PhaseDone()
        if mode == "ffn":
            for i in range(NT):
                hb = i % 2
                xbuf = S3[:, hb * 1024:hb * 1024 + 1024]
                E_("sy", "dma_start", reads=(), writes=(S3_HALF[hb],),
                   out=xbuf, in_=x1in[i * 128:(i + 1) * 128, :])
                ln_stats(xbuf, S3_HALF[hb])
                E_("dve", "reciprocal", reads=(rSM,), writes=(rSM,),
                   out=SM[:, 98:99], in_=SM[:, 94:95])
                E_("dve", "tensor_tensor", reads=(rSM,), writes=(rSM,),
                   out=SM[:, 99:100], in0=SM[:, 93:94], in1=SM[:, 98:99], op=ALU.mult)
                E_("act", "activation", reads=(rSM,), writes=(rSM,),
                   out=SM[:, 64 + i:65 + i], in_=SM[:, 99:100], func=AF.Sqrt,
                   scale=float(HID))
                E_("act", "activation", reads=(S3_HALF[hb], rSM), writes=(rS1, rSM),
                   out=S1[:, 0:1024], in_=xbuf, func=AF.Abs, scale=SM[:, 96:97],
                   bias=SM[:, 97:98], accum_out=SM[:, i:i + 1])
                E_("dve", "tensor_copy", reads=(rSM,), writes=(rSM,),
                   out=SM[:, 32 + i:33 + i], in_=SM[:, 96:97])
                E_("dve", "tensor_copy", reads=(rSM,), writes=(rSM,),
                   out=SM[:, 48 + i:49 + i], in_=SM[:, 97:98])
        if mode != "ffn":
            E_("pe", "transpose", reads=(rSM, rIDF), writes=(rP[0],),
               out=P[0][0:NT, 0:128], in_=SM[:, 64:64 + NT], identity=IDF[:, :])
            E_("act", "activation", reads=(rP[0],), writes=(rS2,),
               out=S2[0:NT, 0:128], in_=P[0][0:NT, 0:128], func=AF.Identity)
        if mode == "attn":
            E_("sy", "dma_start", reads=(rS2,), writes=(res_impout,),
               out=impout[:, :], in_=S2[0:NT, 0:128])
            raise _PhaseDone()
        if mode == "ffn":
            E_("gd", "dma_start", reads=(), writes=(rGI,),
               out=GI[0:NROWS, 0:128], in_=gimp[:, :])
        else:
            E_("gd", "dma_start", reads=(rS2,), writes=(rCCI,),
               out=cc_in.ap()[:, :], in_=S2[0:NT, 0:128])
            E_("gd", "collective_compute", reads=(rCCI,), writes=(rCCO,),
               kind="AllGather", op=ALU.bypass, replica_groups=[list(range(NCORES))],
               ins=[cc_in.ap().opt()], outs=[cc_out.ap().opt()], alt_q="cc")
            E_("gd", "dma_start", reads=(rCCO,), writes=(rGI,),
               out=GI[0:NROWS, 0:128], in_=cc_out.ap()[:, :])
        E_("dve", "tensor_tensor", reads=(rGI,), writes=(rS2,),
           out=S2[0:NROWS, 0:128], in0=GI[0:NROWS, 0:128], in1=GI[0:NROWS, 0:128],
           op=ALU.mult)
        for srcap, sres, col in ((GI[0:NROWS, 0:128], rGI, 0), (S2[0:NROWS, 0:128], rS2, 1)):
            mm(PS[0:1, 0:128], ONEF[0:NROWS, :], srcap, rd=[rONEF, sres], wr=[rPSp],
               start=True, stop=True)
            E_("act", "activation", reads=(rPSp,), writes=(rSC1,),
               out=SC1[:, 32:32 + 128], in_=PS[0:1, 0:128], func=AF.Identity)
            E_("dve", "tensor_reduce", reads=(rSC1,), writes=(rEUS,),
               out=EUS[:, 10 + col:11 + col], in_=SC1[:, 32:32 + 128], axis=AX.X, op=ALU.add)
        ntotf = float(NTOT)
        E_("dve", "tensor_scalar", reads=(rEUS,), writes=(rEUS,),
           out=EUS[:, 10:12], in0=EUS[:, 10:12], scalar1=1.0 / ntotf, scalar2=None,
           op0=ALU.mult)
        mm(P[2][0:NROWS, 0:1], ONRF[0:1, 0:NROWS], EUS[0:1, 10:11],
           rd=[rONRF, rEUS], wr=[rP[2]], start=True, stop=True)
        E_("act", "activation", reads=(rP[2],), writes=(rSCB,),
           out=SCB[0:NROWS, 0:1], in_=P[2][0:NROWS, 0:1], func=AF.Identity)
        E_("dve", "tensor_scalar", reads=(rGI, rSCB), writes=(rS2,),
           out=S2[0:NROWS, 0:128], in0=GI[0:NROWS, 0:128],
           scalar1=SCB[0:NROWS, 0:1], scalar2=None, op0=ALU.subtract)
        E_("dve", "tensor_tensor", reads=(rS2,), writes=(rS2,),
           out=S2[0:NROWS, 0:128], in0=S2[0:NROWS, 0:128], in1=S2[0:NROWS, 0:128],
           op=ALU.mult)
        mm(PS[0:1, 0:128], ONEF[0:NROWS, :], S2[0:NROWS, 0:128],
           rd=[rONEF, rS2], wr=[rPSp], start=True, stop=True)
        E_("act", "activation", reads=(rPSp,), writes=(rSC1,),
           out=SC1[:, 32:32 + 128], in_=PS[0:1, 0:128], func=AF.Identity)
        E_("dve", "tensor_reduce", reads=(rSC1,), writes=(rEUS,),
           out=EUS[:, 13:14], in_=SC1[:, 32:32 + 128], axis=AX.X, op=ALU.add)
        E_("dve", "tensor_scalar", reads=(rEUS,), writes=(rEUS,),
           out=EUS[:, 13:14], in0=EUS[:, 13:14], scalar1=1.0 / ntotf, scalar2=None,
           op0=ALU.mult)
        E_("act", "activation", reads=(rEUS,), writes=(rEUS,),
           out=EUS[:, 14:15], in_=EUS[:, 13:14], func=AF.Sqrt)
        E_("dve", "tensor_scalar", reads=(rEUS,), writes=(rEUS,),
           out=EUS[:, 15:16], in0=EUS[:, 14:15], scalar1=0.1, scalar2=1.0,
           op0=ALU.mult, op1=ALU.add)
        E_("dve", "reciprocal", reads=(rEUS,), writes=(rEUS,),
           out=EUS[:, 9:10], in_=EUS[:, 15:16])
        E_("dve", "memset", reads=(), writes=(rSC1,), ap=SC1[:, 0:4], constant=-1.0)
        E_("dve", "tensor_reduce", reads=(rGI,), writes=(rSM,),
           out=SM[0:NROWS, 88:89], in_=GI[0:NROWS, 0:128], axis=AX.X, op=ALU.max)
        E_("pe", "transpose", reads=(rSM, rIDF), writes=(rP[0],),
           out=P[0][0:1, 0:NROWS], in_=SM[0:NROWS, 88:89],
           identity=IDF[0:NROWS, 0:NROWS])
        E_("act", "activation", reads=(rP[0],), writes=(rSC1,),
           out=SC1[:, 32:32 + NROWS], in_=P[0][0:1, 0:NROWS], func=AF.Identity)
        E_("dve", "tensor_reduce", reads=(rSC1,), writes=(rSC1,),
           out=SC1[:, 4:5], in_=SC1[:, 32:32 + NROWS], axis=AX.X, op=ALU.max)
        E_("dve", "tensor_scalar", reads=(rSC1,), writes=(rSC1,),
           out=SC1[:, 1:2], in0=SC1[:, 4:5], scalar1=1.0, scalar2=None, op0=ALU.add)
        E_("dve", "tensor_copy", reads=(rSC1,), writes=(rSC1,),
           out=SC1[:, 3:4], in_=SC1[:, 1:2])
        for it in range(64):
            for j in range(2):
                E_("dve", "tensor_tensor", reads=(rSC1,), writes=(rSC1,),
                   out=SC1[:, 5:6], in0=SC1[:, 2 * j:2 * j + 1],
                   in1=SC1[:, 2 * j + 1:2 * j + 2], op=ALU.add)
                E_("dve", "tensor_scalar", reads=(rSC1,), writes=(rSC1,),
                   out=SC1[:, 5:6], in0=SC1[:, 5:6], scalar1=0.5, scalar2=None,
                   op0=ALU.mult)
                mm(P[2][0:NROWS, 0:1], ONRF[0:1, 0:NROWS], SC1[0:1, 5:6],
                   rd=[rONRF, rSC1], wr=[rP[2]], start=True, stop=True)
                E_("act", "activation", reads=(rP[2],), writes=(rSCB,),
                   out=SCB[0:NROWS, 0:1], in_=P[2][0:NROWS, 0:1], func=AF.Identity)
                E_("dve", "tensor_scalar", reads=(rGI, rSCB), writes=(rS2,),
                   out=S2[0:NROWS, 0:128], in0=GI[0:NROWS, 0:128],
                   scalar1=SCB[0:NROWS, 0:1], scalar2=None, op0=ALU.is_le)
                mm(PS[0:1, 0:128], ONEF[0:NROWS, :], S2[0:NROWS, 0:128],
                   rd=[rONEF, rS2], wr=[rPSp], start=True, stop=True)
                E_("act", "activation", reads=(rPSp,), writes=(rSC1,),
                   out=SC1[:, 32:32 + 128], in_=PS[0:1, 0:128], func=AF.Identity)
                E_("dve", "tensor_reduce", reads=(rSC1,), writes=(rSC1,),
                   out=SC1[:, 6:7], in_=SC1[:, 32:32 + 128], axis=AX.X, op=ALU.add)
                E_("dve", "tensor_scalar", reads=(rSC1,), writes=(rSC1,),
                   out=SC1[:, 7:8], in0=SC1[:, 6:7], scalar1=float(KPOS + 1 + j),
                   scalar2=None, op0=ALU.is_ge)
                E_("dve", "tensor_tensor", reads=(rSC1,), writes=(rSC1,),
                   out=SC1[:, 8:9], in0=SC1[:, 5:6],
                   in1=SC1[:, 2 * j + 1:2 * j + 2], op=ALU.subtract)
                E_("dve", "tensor_tensor", reads=(rSC1,), writes=(rSC1,),
                   out=SC1[:, 8:9], in0=SC1[:, 8:9], in1=SC1[:, 7:8], op=ALU.mult)
                E_("dve", "tensor_tensor", reads=(rSC1,), writes=(rSC1,),
                   out=SC1[:, 2 * j + 1:2 * j + 2],
                   in0=SC1[:, 2 * j + 1:2 * j + 2], in1=SC1[:, 8:9], op=ALU.add)
                E_("dve", "tensor_tensor", reads=(rSC1,), writes=(rSC1,),
                   out=SC1[:, 8:9], in0=SC1[:, 2 * j:2 * j + 1],
                   in1=SC1[:, 5:6], op=ALU.subtract)
                E_("dve", "tensor_tensor", reads=(rSC1,), writes=(rSC1,),
                   out=SC1[:, 8:9], in0=SC1[:, 8:9], in1=SC1[:, 7:8], op=ALU.mult)
                E_("dve", "tensor_tensor", reads=(rSC1,), writes=(rSC1,),
                   out=SC1[:, 2 * j:2 * j + 1], in0=SC1[:, 5:6],
                   in1=SC1[:, 8:9], op=ALU.add)
        E_("dve", "tensor_tensor", reads=(rSC1,), writes=(rSC1,),
           out=SC1[:, 5:6], in0=SC1[:, 3:4], in1=SC1[:, 1:2], op=ALU.subtract)
        E_("dve", "tensor_scalar", reads=(rSC1,), writes=(rSC1,),
           out=SC1[:, 5:6], in0=SC1[:, 5:6], scalar1=float(FRAC), scalar2=None,
           op0=ALU.mult)
        E_("dve", "tensor_tensor", reads=(rSC1,), writes=(rSC1,),
           out=SC1[:, 6:7], in0=SC1[:, 1:2], in1=SC1[:, 5:6], op=ALU.add)
        mm(P[2][:, 0:1], ONRF[:, :], SC1[0:1, 6:7], rd=[rONRF, rSC1], wr=[rP[2]],
           start=True, stop=True)
        E_("act", "activation", reads=(rP[2],), writes=(rSCB,),
           out=SCB[:, 1:2], in_=P[2][:, 0:1], func=AF.Identity)
        mm(P[2][:, 0:1], ONRF[:, :], EUS[0:1, 9:10], rd=[rONRF, rEUS], wr=[rP[2]],
           start=True, stop=True)
        E_("act", "activation", reads=(rP[2],), writes=(rSCB,),
           out=SCB[:, 2:3], in_=P[2][:, 0:1], func=AF.Identity)
        E_("dve", "tensor_scalar", reads=(rSM, rSCB), writes=(rSM,),
           out=SM[:, 80:80 + NT], in0=SM[:, 64:64 + NT], scalar1=SCB[:, 1:2],
           scalar2=None, op0=ALU.is_gt)
        E_("dve", "tensor_tensor", reads=(rSM,), writes=(rSM,),
           out=SM[:, 64:64 + NT], in0=SM[:, 80:80 + NT], in1=SM[:, 0:NT], op=ALU.mult)
        E_("dve", "tensor_scalar", reads=(rSM,), writes=(rSM,),
           out=SM[:, 64:64 + NT], in0=SM[:, 64:64 + NT], scalar1=1e-8, scalar2=None,
           op0=ALU.add)
        E_("dve", "reciprocal", reads=(rSM,), writes=(rSM,),
           out=SM[:, 0:NT], in_=SM[:, 64:64 + NT])
        E_("dve", "tensor_tensor", reads=(rSM,), writes=(rSM,),
           out=SM[:, 0:NT], in0=SM[:, 0:NT], in1=SM[:, 80:80 + NT], op=ALU.mult)
        E_("dve", "tensor_scalar", reads=(rSM, rSCB), writes=(rSM,),
           out=SM[:, 0:NT], in0=SM[:, 0:NT], scalar1=SCB[:, 2:3], scalar2=None,
           op0=ALU.mult)
        E_("dve", "tensor_tensor", reads=(rSM,), writes=(rSM,),
           out=SM[:, 16:16 + NT], in0=SM[:, 0:NT], in1=SM[:, 48:48 + NT], op=ALU.mult)
        E_("dve", "tensor_tensor", reads=(rSM,), writes=(rSM,),
           out=SM[:, 0:NT], in0=SM[:, 0:NT], in1=SM[:, 32:32 + NT], op=ALU.mult)

        def accsl(i):
            return AW[:, i * 1024:(i + 1) * 1024]

        def xnTsl(db, t0, t1):
            t_, r_ = (AQ, rAQ) if db < 4 else (AK, rAK)
            return t_[:, (db % 4) * T + t0:(db % 4) * T + t1], r_

        for i in range(NT):
            hb = i % 2
            x1buf = S3[:, hb * 1024:hb * 1024 + 1024]
            E_("sy", "dma_start", reads=(), writes=(S3_HALF[hb],),
               out=x1buf,
               in_=(x1in if mode == "ffn" else x1sp)[i * 128:(i + 1) * 128, :])
            E_("act", "activation", reads=(S3_HALF[hb], rSM), writes=(rS1,),
               out=S1[:, 0:1024], in_=x1buf, func=AF.Identity,
               scale=SM[:, i:i + 1], bias=SM[:, 16 + i:17 + i])
            E_("act", "activation", reads=(rS1,), writes=(rAW,),
               out=accsl(i), in_=S1[:, 0:1024], func=AF.Identity, scale=0.1)
            for db in range(8):
                E_("pe", "transpose", reads=(rS1, rIDB), writes=(rPBp,),
                   out=PB[:, 0:128], in_=S1[:, db * 128:(db + 1) * 128],
                   identity=IDB[:, :])
                dsl, dres = xnTsl(db, i * 128, (i + 1) * 128)
                E_("dve", "tensor_copy", reads=(rPBp,), writes=(dres,),
                   out=dsl, in_=PB[:, 0:128])

        if phases < 6:
            raise _PhaseDone()
        for sl in range(4):
            bi = sl % 2
            E_("gd", "dma_start", reads=(), writes=(rWB[bi],),
               out=WB[:, bi * 2048: bi * 2048 + 2048],
               in_=gw1[:, sl * 256:(sl + 1) * 256].rearrange("(kt p) m -> p kt m", p=128))
            for mt2 in range(2):
                mt = sl * 2 + mt2
                for c in range(NCH):
                    pi = c % 2
                    for kt in range(8):
                        rhs, rr = xnTsl(kt, c * 512, (c + 1) * 512)
                        mm(P[pi][:, :],
                           WB[:, bi * 2048 + kt * 256 + mt2 * 128: bi * 2048 + kt * 256 + (mt2 + 1) * 128],
                           rhs, rd=[rWB[bi], rr], wr=[rP[pi]], start=(kt == 0), stop=(kt == 7))
                    E_("act", "activation", reads=(rP[pi], rCT), writes=(rAV,),
                       out=AV[:, mt * T + c * 512: mt * T + (c + 1) * 512], in_=P[pi][:, :],
                       func=AF.Silu, bias=CT[:, CO["GB1"] + mt:CO["GB1"] + mt + 1])
        for c in range(NCH):
            for mt in range(8):
                mm(PS[:, :], ONEB[:, :], AV[:, mt * T + c * 512: mt * T + (c + 1) * 512],
                   rd=[rONEB, rAV], wr=[rPSp], start=(mt == 0), stop=(mt == 7))
            E_("act", "activation", reads=(rPSp,), writes=(rV1,),
               out=V1[:, c * 512:(c + 1) * 512], in_=PS[:, :], func=AF.Identity)
            for mt in range(8):
                E_("act", "activation", reads=(rAV,), writes=(rS1,),
                   out=S1[:, 0:512], in_=AV[:, mt * T + c * 512: mt * T + (c + 1) * 512],
                   func=AF.Square)
                mm(PS[:, :], ONEB[:, :], S1[:, 0:512], rd=[rONEB, rS1], wr=[rPSp],
                   start=(mt == 0), stop=(mt == 7))
            E_("act", "activation", reads=(rPSp,), writes=(rV2,),
               out=V2[:, c * 512:(c + 1) * 512], in_=PS[:, :], func=AF.Identity)
        V3 = S2[0:1, 0:T]
        E_("dve", "tensor_scalar", reads=(rV1,), writes=(rV1,),
           out=V1[:, :], in0=V1[:, :], scalar1=1.0 / HID, scalar2=None, op0=ALU.mult)
        E_("dve", "tensor_scalar", reads=(rV2,), writes=(rV2,),
           out=V2[:, :], in0=V2[:, :], scalar1=1.0 / HID, scalar2=None, op0=ALU.mult)
        E_("dve", "tensor_tensor", reads=(rV1,), writes=(rS2,),
           out=V3, in0=V1[:, :], in1=V1[:, :], op=ALU.mult)
        E_("dve", "tensor_tensor", reads=(rV2, rS2), writes=(rV2,),
           out=V2[:, :], in0=V2[:, :], in1=V3, op=ALU.subtract)
        E_("dve", "tensor_scalar", reads=(rV2,), writes=(rV2,),
           out=V2[:, :], in0=V2[:, :], scalar1=1e-5, scalar2=None, op0=ALU.add)
        E_("act", "activation", reads=(rV2,), writes=(rV2,),
           out=V2[:, :], in_=V2[:, :], func=AF.Ln)
        E_("act", "activation", reads=(rV2,), writes=(rV2,),
           out=V2[:, :], in_=V2[:, :], func=AF.Exp, scale=-0.5)
        E_("dve", "scalar_tensor_tensor", reads=(rV1, rV2), writes=(rS2,),
           out=V3, in0=V1[:, :], scalar=-1.0, in1=V2[:, :], op0=ALU.mult, op1=ALU.mult)
        for c in range(NCH):
            mm(P[0][:, :], ONRF[:, :], V2[:, c * 512:(c + 1) * 512],
               rd=[rONRF, rV2], wr=[rP[0]], start=True, stop=True)
            mm(P[1][:, :], ONRF[:, :], S2[0:1, c * 512:(c + 1) * 512],
               rd=[rONRF, rS2], wr=[rP[1]], start=True, stop=True)
            for mt in range(8):
                sl_ = AV[:, mt * T + c * 512: mt * T + (c + 1) * 512]
                E_("dve", "tensor_tensor", reads=(rAV, rP[0]), writes=(rS1,),
                   out=S1[:, 0:512], in0=sl_, in1=P[0][:, :], op=ALU.mult)
                E_("dve", "tensor_tensor", reads=(rS1, rP[1]), writes=(rS1,),
                   out=S1[:, 0:512], in0=S1[:, 0:512], in1=P[1][:, :], op=ALU.add)
                E_("act", "activation", reads=(rS1, rCT), writes=(rAV,),
                   out=sl_, in_=S1[:, 0:512], func=AF.Identity,
                   scale=CT[:, CO["LNS"] + mt:CO["LNS"] + mt + 1],
                   bias=CT[:, CO["LNO"] + mt:CO["LNO"] + mt + 1])
        for sl in range(2):
            bi = sl % 2
            E_("gd", "dma_start", reads=(), writes=(rWB[bi],),
               out=WB[:, bi * 2048: bi * 2048 + 2048],
               in_=gw2[:, sl * 256:(sl + 1) * 256].rearrange("(kt p) m -> p kt m", p=128))
            for mt2 in range(2):
                mt = sl * 2 + mt2
                for c in range(NCH):
                    pi = 2 + c % 2
                    for kt in range(8):
                        mm(P[pi][:, :],
                           WB[:, bi * 2048 + kt * 256 + mt2 * 128: bi * 2048 + kt * 256 + (mt2 + 1) * 128],
                           AV[:, kt * T + c * 512: kt * T + (c + 1) * 512],
                           rd=[rWB[bi], rAV], wr=[rP[pi]], start=(kt == 0), stop=(kt == 7))
                    E_("act", "activation", reads=(rP[pi], rCT), writes=(rAG,),
                       out=AG[:, mt * T + c * 512: mt * T + (c + 1) * 512], in_=P[pi][:, :],
                       func=AF.Silu, bias=CT[:, CO["GB2"] + mt:CO["GB2"] + mt + 1])
        E_("sy", "dma_start", reads=(), writes=(rS2,),
           out=S2[0:E, 0:T], in_=noise_d[:, :])
        E_("dve", "tensor_scalar", reads=(rEUS,), writes=(rSC1,),
           out=SC1[:, 16:16 + E], in0=EUS[:, 0:E], scalar1=1e-8, scalar2=None, op0=ALU.add)
        E_("dve", "tensor_reduce", reads=(rSC1,), writes=(rSC1,),
           out=SC1[:, 24:25], in_=SC1[:, 16:16 + E], axis=AX.X, op=ALU.min)
        E_("dve", "tensor_scalar", reads=(rSC1,), writes=(rSC1,),
           out=SC1[:, 25:26], in0=SC1[:, 24:25], scalar1=0.0, scalar2=None, op0=ALU.is_lt)
        E_("dve", "tensor_scalar", reads=(rSC1,), writes=(rSC1,),
           out=SC1[:, 16:16 + E], in0=SC1[:, 16:16 + E], scalar1=1e-30, scalar2=None,
           op0=ALU.max)
        E_("act", "activation", reads=(rSC1,), writes=(rSC1,),
           out=SC1[:, 16:16 + E], in_=SC1[:, 16:16 + E], func=AF.Ln)
        E_("dve", "tensor_tensor", reads=(rSC1, rEUS), writes=(rSC1,),
           out=SC1[:, 16:16 + E], in0=SC1[:, 16:16 + E], in1=EUS[:, 0:E], op=ALU.mult)
        E_("dve", "tensor_reduce", reads=(rSC1,), writes=(rSC1,),
           out=SC1[:, 26:27], in_=SC1[:, 16:16 + E], axis=AX.X, op=ALU.add)
        E_("dve", "tensor_scalar", reads=(rSC1,), writes=(rSC1,),
           out=SC1[:, 26:27], in0=SC1[:, 26:27], scalar1=0.01, scalar2=0.02,
           op0=ALU.mult, op1=ALU.add)
        E_("dve", "tensor_scalar", reads=(rSC1,), writes=(rSC1,),
           out=SC1[:, 27:28], in0=SC1[:, 26:27], scalar1=1e-4, scalar2=None, op0=ALU.max)
        mm(P[2][0:E, 0:1], ONRF[0:1, 0:E], SC1[0:1, 27:28], rd=[rONRF, rSC1],
           wr=[rP[2]], start=True, stop=True)
        E_("act", "activation", reads=(rP[2],), writes=(rSCB,),
           out=SCB[0:E, 3:4], in_=P[2][0:E, 0:1], func=AF.Identity)
        mm(P[2][:, 0:1], ONRF[:, :], SC1[0:1, 25:26], rd=[rONRF, rSC1],
           wr=[rP[2]], start=True, stop=True)
        E_("act", "activation", reads=(rP[2],), writes=(rSCB,),
           out=SCB[:, 4:5], in_=P[2][:, 0:1], func=AF.Identity)
        E_("dve", "tensor_scalar", reads=(rSCB,), writes=(rSCB,),
           out=SCB[:, 5:6], in0=SCB[:, 4:5], scalar1=-1.0, scalar2=1.0,
           op0=ALU.mult, op1=ALU.add)
        for db in range(8):
            dsl, dres = xnTsl(db, 0, T)
            E_("dve", "tensor_reduce", reads=(dres,), writes=(rSCB,),
               out=SCB[:, 6:7], in_=dsl, axis=AX.X, op=ALU.add)
            E_("dve", "tensor_scalar", reads=(rSCB,), writes=(rSCB,),
               out=SCB[:, 7:8], in0=SCB[:, 6:7], scalar1=0.3 / float(T), scalar2=None,
               op0=ALU.mult)
            mm(P[3][0:1, 0:E], SCB[:, 7:8], AFW[:, db * 8:(db + 1) * 8],
               rd=[rSCB, rAFW], wr=[rP[3]], start=(db == 0), stop=(db == 7))
        E_("act", "activation", reads=(rP[3],), writes=(rSC1,),
           out=SC1[:, 40:40 + E], in_=P[3][0:1, 0:E], func=AF.Identity)
        E_("dve", "tensor_tensor", reads=(rSC1, rCT), writes=(rSC1,),
           out=SC1[:, 40:40 + E], in0=SC1[:, 40:40 + E],
           in1=CT[0:1, CO["AFFB"]:CO["AFFB"] + E], op=ALU.add)
        E_("pe", "transpose", reads=(rSC1, rIDF), writes=(rP[2],),
           out=P[2][0:E, 0:1], in_=SC1[0:1, 40:40 + E], identity=IDF[0:1, 0:1])
        E_("act", "activation", reads=(rP[2],), writes=(rSCB,),
           out=SCB[0:E, 0:1], in_=P[2][0:E, 0:1], func=AF.Identity)
        for c in range(NCH):
            for kt in range(4):
                mm(P[5][0:E, :], W3B[:, kt * 8:(kt + 1) * 8],
                   AG[:, kt * T + c * 512: kt * T + (c + 1) * 512],
                   rd=[rW3B, rAG], wr=[rP[5]], start=(kt == 0), stop=(kt == 3))
            E_("act", "activation", reads=(rP[5], rCT), writes=(rLGS,),
               out=LGS[:, :], in_=P[5][0:E, :], func=AF.Identity,
               bias=CT[0:E, CO["GB3"]:CO["GB3"] + 1])
            E_("dve", "tensor_scalar", reads=(rLGS, rSCB), writes=(rLGS,),
               out=LGS[:, :], in0=LGS[:, :], scalar1=SCB[0:E, 0:1], scalar2=None,
               op0=ALU.add)
            E_("dve", "tensor_scalar", reads=(rS2, rSCB), writes=(rS2,),
               out=S2[0:E, c * 512:(c + 1) * 512], in0=S2[0:E, c * 512:(c + 1) * 512],
               scalar1=SCB[0:E, 3:4], scalar2=None, op0=ALU.mult)
            E_("dve", "tensor_tensor", reads=(rLGS, rS2), writes=(rLGS,),
               out=LGS[:, :], in0=LGS[:, :], in1=S2[0:E, c * 512:(c + 1) * 512],
               op=ALU.add)
            E_("dve", "tensor_scalar", reads=(rLGS,), writes=(rLGS,),
               out=LGS[:, :], in0=LGS[:, :], scalar1=1.0 / 1.5, scalar2=None,
               op0=ALU.mult)
            for s4 in range(4):
                i = c * 4 + s4
                E_("pe", "transpose", reads=(rLGS, rIDF), writes=(rP[0],),
                   out=P[0][:, 0:E], in_=LGS[:, s4 * 128:(s4 + 1) * 128],
                   identity=IDF[0:E, 0:E])
                E_("act", "activation", reads=(rP[0],), writes=(rSM,),
                   out=SM[:, 80:88], in_=P[0][:, 0:E], func=AF.Identity)
                E_("dve", "tensor_reduce", reads=(rSM,), writes=(rSM,),
                   out=SM[:, 88:89], in_=SM[:, 80:88], axis=AX.X, op=ALU.max)
                E_("dve", "tensor_scalar", reads=(rSM,), writes=(rS3a,),
                   out=S3[:, 0:8], in0=SM[:, 80:88], scalar1=SM[:, 88:89],
                   scalar2=None, op0=ALU.is_ge)
                E_("dve", "tensor_reduce", reads=(rS3a,), writes=(rSM,),
                   out=SM[:, 89:90], in_=S3[:, 0:8], axis=AX.X, op=ALU.add)
                E_("dve", "tensor_scalar", reads=(rS3a,), writes=(rS3a,),
                   out=S3[:, 8:16], in0=S3[:, 0:8], scalar1=-2e9, scalar2=None,
                   op0=ALU.mult)
                E_("dve", "tensor_tensor", reads=(rSM, rS3a), writes=(rS3a,),
                   out=S3[:, 8:16], in0=SM[:, 80:88], in1=S3[:, 8:16], op=ALU.add)
                E_("dve", "tensor_reduce", reads=(rS3a,), writes=(rSM,),
                   out=SM[:, 91:92], in_=S3[:, 8:16], axis=AX.X, op=ALU.max)
                E_("dve", "tensor_scalar", reads=(rSM,), writes=(rSM,),
                   out=SM[:, 92:93], in0=SM[:, 89:90], scalar1=2.0, scalar2=None,
                   op0=ALU.is_ge)
                E_("dve", "tensor_tensor", reads=(rSM,), writes=(rSM,),
                   out=SM[:, 93:94], in0=SM[:, 88:89], in1=SM[:, 91:92],
                   op=ALU.subtract)
                E_("dve", "tensor_tensor", reads=(rSM,), writes=(rSM,),
                   out=SM[:, 93:94], in0=SM[:, 93:94], in1=SM[:, 92:93], op=ALU.mult)
                E_("dve", "tensor_tensor", reads=(rSM,), writes=(rSM,),
                   out=SM[:, 93:94], in0=SM[:, 91:92], in1=SM[:, 93:94], op=ALU.add)
                E_("dve", "tensor_scalar", reads=(rSM,), writes=(rS3a,),
                   out=S3[:, 0:8], in0=SM[:, 80:88], scalar1=SM[:, 93:94],
                   scalar2=None, op0=ALU.is_ge)
                E_("dve", "tensor_scalar", reads=(rS3a, rSCB), writes=(rS3a,),
                   out=S3[:, 0:8], in0=S3[:, 0:8], scalar1=SCB[:, 5:6], scalar2=None,
                   op0=ALU.mult)
                E_("dve", "tensor_tensor", reads=(rS3a, rSM), writes=(rS3b,),
                   out=S3[:, 16:24], in0=SM[:, 80:88], in1=S3[:, 0:8], op=ALU.mult)
                E_("dve", "tensor_scalar", reads=(rS3a,), writes=(rS3a,),
                   out=S3[:, 24:32], in0=S3[:, 0:8], scalar1=1e9, scalar2=-1e9,
                   op0=ALU.mult, op1=ALU.add)
                E_("dve", "tensor_tensor", reads=(rS3a, rS3b), writes=(rS3b,),
                   out=S3[:, 16:24], in0=S3[:, 16:24], in1=S3[:, 24:32], op=ALU.add)
                E_("dve", "tensor_reduce", reads=(rS3b,), writes=(rSM,),
                   out=SM[:, 94:95], in_=S3[:, 16:24], axis=AX.X, op=ALU.max)
                E_("dve", "tensor_scalar", reads=(rSM,), writes=(rSM,),
                   out=SM[:, 95:96], in0=SM[:, 94:95], scalar1=-1.0, scalar2=None,
                   op0=ALU.mult)
                E_("act", "activation", reads=(rS3b, rSM), writes=(rS3b,),
                   out=S3[:, 16:24], in_=S3[:, 16:24], func=AF.Exp,
                   bias=SM[:, 95:96])
                E_("dve", "tensor_reduce", reads=(rS3b,), writes=(rSM,),
                   out=SM[:, 94:95], in_=S3[:, 16:24], axis=AX.X, op=ALU.add)
                E_("dve", "reciprocal", reads=(rSM,), writes=(rSM,),
                   out=SM[:, 95:96], in_=SM[:, 94:95])
                E_("dve", "tensor_scalar", reads=(rS3b, rSM), writes=(rS3b,),
                   out=S3[:, 16:24], in0=S3[:, 16:24], scalar1=SM[:, 95:96],
                   scalar2=None, op0=ALU.mult)
                E_("dve", "tensor_tensor", reads=(rS3b, rS3a), writes=(rS3a,),
                   out=S3[:, 0:8], in0=S3[:, 16:24], in1=S3[:, 0:8], op=ALU.mult)
                E_("dve", "tensor_scalar", reads=(rNZE, rSCB), writes=(rS3b,),
                   out=S3[:, 24:32], in0=NZE[:, 0:8], scalar1=SCB[:, 4:5],
                   scalar2=None, op0=ALU.mult)
                E_("dve", "tensor_tensor", reads=(rS3a, rS3b), writes=(rPRB,),
                   out=PRB[:, i * E:(i + 1) * E], in0=S3[:, 0:8], in1=S3[:, 24:32],
                   op=ALU.add)
                E_("pe", "transpose", reads=(rPRB, rIDF), writes=(rP[0],),
                   out=P[0][0:E, 0:128], in_=PRB[:, i * E:(i + 1) * E],
                   identity=IDF[:, :])
                E_("act", "activation", reads=(rP[0],), writes=(rPRT,),
                   out=PRT[:, i * 128:(i + 1) * 128], in_=P[0][0:E, 0:128],
                   func=AF.Identity)
        for i in range(NT):
            for dc in range(2):
                mm(P[0][:, :], PRT[:, i * 128:(i + 1) * 128], BM[:, dc * 512:(dc + 1) * 512],
                   rd=[rPRT, rBM], wr=[rP[0]], start=True, stop=True)
                E_("dve", "tensor_tensor", reads=(rP[0], rAW), writes=(rAW,),
                   out=accsl(i)[:, dc * 512:(dc + 1) * 512],
                   in0=accsl(i)[:, dc * 512:(dc + 1) * 512], in1=P[0][:, :], op=ALU.add)

        if phases < 7:
            raise _PhaseDone()
        PO = P

        def hblk(j):
            if j < 8 * T // ETOK:
                return AV[:, j * ETOK:(j + 1) * ETOK], rAV
            jj = j - 8 * T // ETOK
            return AG[:, jj * ETOK:(jj + 1) * ETOK], rAG

        NBH = 8 * T // ETOK

        def h_slab_dma(wdram, e_idx, sl, kh, KH, bi):
            E_("gd", "dma_start", reads=(), writes=(rWB[bi],),
               out=WB[:, bi * 2048: bi * 2048 + 2048],
               in_=wdram[e_idx, kh * 1024:(kh + 1) * 1024, sl * 256:(sl + 1) * 256]
               .rearrange("(kt p) m -> p kt m", p=128))

        def h_stage(wdram, e_idx, KIN, MOUT, actf, biascol, blk0, ec,
                    gelu=False, sigsq=False):
            KH = KIN // 1024
            slabs = MOUT // 256
            seq = [(sl, kh) for sl in range(slabs) for kh in range(KH)]
            h_slab_dma(wdram, e_idx, 0, 0, KH, 0)
            for si, (sl, kh) in enumerate(seq):
                bi = si % 2
                if si + 1 < len(seq):
                    h_slab_dma(wdram, e_idx, seq[si + 1][0], seq[si + 1][1], KH,
                               (si + 1) % 2)
                for mt2 in range(2):
                    mt = sl * 2 + mt2
                    for nc2 in range(ENC):
                        pi = mt2 * ENC + nc2
                        for kt in range(8):
                            gk = kh * 8 + kt
                            if KIN == HID:
                                rhs, rr = xnTsl(gk, ec * ETOK + nc2 * 512,
                                                ec * ETOK + (nc2 + 1) * 512)
                            else:
                                bsl, rr = hblk(gk)
                                rhs = bsl[:, nc2 * 512:(nc2 + 1) * 512]
                            mm(PO[pi][:, :],
                               WB[:, bi * 2048 + kt * 256 + mt2 * 128: bi * 2048 + kt * 256 + (mt2 + 1) * 128],
                               rhs, rd=[rWB[bi], rr], wr=[rP[pi]],
                               start=(kh == 0 and kt == 0), stop=(kh == KH - 1 and kt == 7))
                        if kh != KH - 1:
                            continue
                        bsl, br = hblk(blk0 + mt)
                        dst = bsl[:, nc2 * 512:(nc2 + 1) * 512]
                        if gelu:
                            E_("act", "activation", reads=(rP[pi], rCT), writes=(br,),
                               out=dst, in_=PO[pi][:, :], func=AF.Identity,
                               bias=CT[:, biascol + mt:biascol + mt + 1])
                            E_("act", "activation", reads=(br,), writes=(rS1,),
                               out=S1[:, 0:512], in_=dst, func=AF.Square)
                            E_("dve", "tensor_tensor", reads=(rS1, br), writes=(rS1,),
                               out=S1[:, 0:512], in0=S1[:, 0:512], in1=dst, op=ALU.mult)
                            E_("dve", "tensor_scalar", reads=(rS1,), writes=(rS1,),
                               out=S1[:, 0:512], in0=S1[:, 0:512], scalar1=C2G,
                               scalar2=None, op0=ALU.mult)
                            E_("dve", "tensor_tensor", reads=(rS1, br), writes=(rS1,),
                               out=S1[:, 0:512], in0=S1[:, 0:512], in1=dst, op=ALU.add)
                            E_("act", "activation", reads=(rS1,), writes=(rS1,),
                               out=S1[:, 512:1024], in_=S1[:, 0:512], func=AF.Sigmoid,
                               scale=C1G)
                            E_("dve", "tensor_tensor", reads=(rS1, br), writes=(br,),
                               out=dst, in0=dst, in1=S1[:, 512:1024], op=ALU.mult)
                        elif sigsq:
                            E_("act", "activation", reads=(rP[pi], rCT), writes=(br,),
                               out=dst, in_=PO[pi][:, :], func=AF.Identity,
                               bias=CT[:, biascol + mt:biascol + mt + 1])
                            E_("act", "activation", reads=(br,), writes=(rS1,),
                               out=S1[:, 0:512], in_=dst, func=AF.Sigmoid)
                            E_("dve", "tensor_tensor", reads=(rS1,), writes=(rS1,),
                               out=S1[:, 512:1024], in0=S1[:, 0:512], in1=S1[:, 0:512],
                               op=ALU.mult)
                            E_("dve", "tensor_tensor", reads=(rS1, br), writes=(br,),
                               out=dst, in0=dst, in1=S1[:, 512:1024], op=ALU.mult)
                        else:
                            E_("act", "activation", reads=(rP[pi], rCT), writes=(br,),
                               out=dst, in_=PO[pi][:, :], func=actf,
                               bias=CT[:, biascol + mt:biascol + mt + 1])

        def o_slab_dma(wdram, e_idx, kt, bi):
            E_("gd", "dma_start", reads=(), writes=(rWB[bi],),
               out=WB[:, bi * 2048: bi * 2048 + 1024],
               in_=wdram[e_idx, kt * 128:(kt + 1) * 128, :])

        def out_stage(wdram, e_idx, KIN, gcol, ec, blk0):
            ktn = KIN // 128
            groups = [(g, min(g + 3, NTT)) for g in range(0, NTT, 3)]
            for g0, g1 in groups:
                o_slab_dma(wdram, e_idx, 0, 0)
                for kt in range(ktn):
                    bi = kt % 2
                    if kt + 1 < ktn:
                        o_slab_dma(wdram, e_idx, kt + 1, (kt + 1) % 2)
                    bsl, br = hblk(blk0 + kt)
                    for tt in range(g0, g1):
                        for dc in range(2):
                            pidx = (tt - g0) * 2 + dc
                            mm(P[pidx][:, :], bsl[:, tt * 128:(tt + 1) * 128],
                               WB[:, bi * 2048 + dc * 512: bi * 2048 + (dc + 1) * 512],
                               rd=[rWB[bi], br], wr=[rP[pidx]],
                               start=(kt == 0), stop=(kt == ktn - 1))
                for tt in range(g0, g1):
                    i = ec * NTT + tt
                    for dc in range(2):
                        pidx = (tt - g0) * 2 + dc
                        E_("act", "activation", reads=(rP[pidx], rPRB), writes=(rS1,),
                           out=S1[:, pidx * 512:(pidx + 1) * 512], in_=P[pidx][:, :],
                           func=AF.Identity,
                           scale=PRB[:, i * E + gcol:i * E + gcol + 1])
                        E_("dve", "tensor_tensor", reads=(rS1, rAW), writes=(rAW,),
                           out=accsl(i)[:, dc * 512:(dc + 1) * 512],
                           in0=accsl(i)[:, dc * 512:(dc + 1) * 512],
                           in1=S1[:, pidx * 512:(pidx + 1) * 512], op=ALU.add)

        for ec in range(NG):
            for e_i in range(3):
                h_stage(e0w1, e_i, HID, 4096, None, CO["E0B1"] + e_i * 32, 0, ec, gelu=True)
                out_stage(e0w2, e_i, 4096, 3 * e_i, ec, 0)
            for e_i in range(2):
                h_stage(e2w1, e_i, HID, 3072, None, CO["E2B1"] + e_i * 24, 0, ec, sigsq=True)
                out_stage(e2w2, e_i, 3072, 3 * e_i + 2, ec, 0)
            for e_i in range(3):
                h_stage(e1w1, e_i, HID, 2048, AF.Silu, CO["E1B1"] + e_i * 16, 0, ec)
                h_stage(e1w2, e_i, 2048, 2048, AF.Silu, CO["E1B2"] + e_i * 16, 16, ec)
                out_stage(e1w3, e_i, 2048, 3 * e_i + 1, ec, 16)

        for i in range(NT):
            hb = i % 2
            x1buf = S3[:, hb * 1024:hb * 1024 + 1024]
            E_("sy", "dma_start", reads=(), writes=(S3_HALF[hb],),
               out=x1buf,
               in_=(x1in if mode == "ffn" else x1sp)[i * 128:(i + 1) * 128, :])
            ln_stats(accsl(i), rAW)
            E_("act", "activation", reads=(rAW, rSM), writes=(rS1,),
               out=S1[:, 0:1024], in_=accsl(i), func=AF.Identity, scale=SM[:, 96:97],
               bias=SM[:, 97:98])
            E_("dve", "tensor_tensor", reads=(rS1, S3_HALF[hb]), writes=(rS2,),
               out=S2[:, 0:1024], in0=x1buf, in1=S1[:, 0:1024], op=ALU.add)
            E_("sy", "dma_start", reads=(rS2,), writes=(rOUT,),
               out=out_d[i * 128:(i + 1) * 128, :], in_=S2[:, 0:1024])
    except _PhaseDone:
        pass
    for s, v in sd.qdone("sy") + sd.qdone("gd"):
        sd.rawwait("dve", s, v)

    with nc.Block() as block:
        @block.tensor
        def _(e):
            sd.replay("pe", e)

        @block.vector
        def _(e):
            sd.replay("dve", e)

        @block.scalar
        def _(e):
            sd.replay("act", e)

        @block.gpsimd
        def _(e):
            sd.replay("gd", e)

        @block.sync
        def _(e):
            sd.replay("sy", e)

    es.close()
    return nc


def _host_consts(T, NCORES, inputs):
    NT = T // 128
    NDIFF = NT + 3
    CO = ctab_layout(T)
    g = GAMMA
    ctab = np.zeros((128, CO["W"]), dtype=np.float64)
    sl = np.arange(128)
    for h in range(H):
        for di in range(NDIFF):
            d0 = 128 * (di - 3)
            ctab[:, CO["ESC"] + h * NDIFF + di] = \
                np.exp2((d0 - sl) * np.log2(g[h])) * DH ** -0.5
    ctab[:, CO["GNW"]:CO["GNW"] + 8] = inputs["L0_attn_gn_weight"].reshape(8, 128).T
    ctab[:, CO["GNB"]:CO["GNB"] + 8] = inputs["L0_attn_gn_bias"].reshape(8, 128).T
    ctab[:, CO["GB1"]:CO["GB1"] + 8] = inputs["L0_ffn_gate_b1"].reshape(8, 128).T
    ctab[:, CO["GB2"]:CO["GB2"] + 4] = inputs["L0_ffn_gate_b2"].reshape(4, 128).T
    ctab[0:8, CO["GB3"]] = inputs["L0_ffn_gate_b3"]
    ctab[:, CO["E0B1"]:CO["E0B1"] + 96] = inputs["L0_ffn_e0_b1"].reshape(96, 128).T
    ctab[:, CO["E1B1"]:CO["E1B1"] + 48] = inputs["L0_ffn_e1_b1"].reshape(48, 128).T
    ctab[:, CO["E1B2"]:CO["E1B2"] + 48] = inputs["L0_ffn_e1_b2"].reshape(48, 128).T
    ctab[:, CO["E2B1"]:CO["E2B1"] + 48] = inputs["L0_ffn_e2_b1"].reshape(48, 128).T
    ctab[:, CO["LNS"]:CO["LNS"] + 8] = inputs["L0_ffn_ln_scale"].reshape(8, 128).T
    ctab[:, CO["LNO"]:CO["LNO"] + 8] = inputs["L0_ffn_ln_offset"].reshape(8, 128).T
    ctab[0, CO["AFFB"]:CO["AFFB"] + 8] = 0.3 * inputs["L0_ffn_aff_b"]
    ctab = ctab.astype(np.float32)

    inv_freq = 1.0 / 10000.0 ** (np.arange(0, DH, 2, dtype=np.float32) / DH)
    t = np.arange(T, dtype=np.float32)
    emb = np.concatenate([t[:, None] * inv_freq[None, :]] * 2, axis=-1)
    cosv, sinv = np.cos(emb), np.sin(emb)
    trig = np.zeros((64, 2 * T), dtype=np.float32)
    trig[:, 0:T] = cosv.T
    trig[:, T:2 * T] = sinv.T

    masks = np.zeros((128, 4 * 512), dtype=np.float32)
    for k in range(4):
        masks[:, k * 512:(k + 1) * 512] = \
            (np.arange(512)[None, :] >= np.arange(128)[:, None] + 128 * k)

    idn = np.eye(128, dtype=np.float32)

    tmod = np.arange(T) % 512
    epsr = np.stack([1e-5 * np.exp2(-2.0 * tmod * np.log2(g[h])) for h in range(H)]
                    ).astype(np.float32)

    import jax
    import jax.numpy as jnp
    try:
        cpu = jax.devices("cpu")[0]
        with jax.default_device(cpu):
            noise = np.asarray(jax.random.normal(jax.random.key(1),
                                                 (NCORES, T, E), dtype=jnp.float32))
    except Exception:
        noise = np.asarray(jax.random.normal(jax.random.key(1),
                                             (NCORES, T, E), dtype=jnp.float32))

    bm = np.zeros((E, HID), dtype=np.float32)
    for e in range(E):
        fam, inst = e % 3, e // 3
        if fam == 0:
            bm[e] = inputs["L0_ffn_e0_b2"][inst]
        elif fam == 1:
            bm[e] = inputs["L0_ffn_e1_b3"][inst]
        else:
            bm[e] = inputs["L0_ffn_e2_b2"][inst]
    affwT = np.ascontiguousarray(
        inputs["L0_ffn_aff_w"].reshape(8, 128, 8).transpose(1, 0, 2).reshape(128, 64))
    return ctab, trig, masks, idn, epsr, noise, bm, affwT


def kernel(**inputs):
    T = inputs["L0_attn_x"].shape[1]
    NCORES = inputs["L0_attn_x"].shape[0]
    ctab, trig, masks, idn, epsr, noise, bm, affwT = _host_consts(T, NCORES, inputs)
    two = True
    asc = np.ascontiguousarray
    shared = {
        "wqT": asc(inputs["L0_attn_Wq"].T), "wkT": asc(inputs["L0_attn_Wk"].T),
        "wvT": asc(inputs["L0_attn_Wv"].T), "wgT": asc(inputs["L0_attn_Wg"].T),
        "woT": asc(inputs["L0_attn_Wo"].T),
        "gw1": asc(inputs["L0_ffn_gate_w1"]), "gw2": asc(inputs["L0_ffn_gate_w2"]),
        "gw3": asc(inputs["L0_ffn_gate_w3"]), "affwT": affwT,
        "e0w1": asc(inputs["L0_ffn_e0_w1"]), "e0w2": asc(inputs["L0_ffn_e0_w2"]),
        "e1w1": asc(inputs["L0_ffn_e1_w1"]), "e1w2": asc(inputs["L0_ffn_e1_w2"]),
        "e1w3": asc(inputs["L0_ffn_e1_w3"]), "e2w1": asc(inputs["L0_ffn_e2_w1"]),
        "e2w2": asc(inputs["L0_ffn_e2_w2"]), "biasmat": bm, "ctab": ctab,
        "trig": trig, "masks": masks, "idn": idn,
        "eu": asc(inputs["L0_ffn_expert_usage"].reshape(1, E)), "epsrow": epsr,
    }
    if not two:
        nc = build_nc(T, NCORES)
        in_maps = []
        for b in range(NCORES):
            m = dict(shared)
            m["x"] = asc(inputs["L0_attn_x"][b])
            m["noiseT"] = asc(noise[b].T)
            in_maps.append(m)
        res = run_bass_kernel_spmd(nc, in_maps, core_ids=list(range(NCORES)))
        return np.stack([np.asarray(r["out"]) for r in res.results], axis=0)
    nca = build_nc(T, NCORES, mode="attn")
    maps_a = []
    for b in range(NCORES):
        m = dict(shared)
        m["x"] = asc(inputs["L0_attn_x"][b])
        m["noiseT"] = asc(noise[b].T)
        maps_a.append(m)
    ra = run_bass_kernel_spmd(nca, maps_a, core_ids=list(range(NCORES)))
    x1s = [np.asarray(r["x1out"]) for r in ra.results]
    gimp = np.ascontiguousarray(
        np.concatenate([np.asarray(r["impout"]) for r in ra.results], axis=0))
    ncf = build_nc(T, NCORES, mode="ffn")
    maps_f = []
    for b in range(NCORES):
        m = dict(shared)
        m["x"] = asc(inputs["L0_attn_x"][b])
        m["noiseT"] = asc(noise[b].T)
        m["x1in"] = asc(x1s[b])
        m["gimp"] = gimp
        maps_f.append(m)
    rf = run_bass_kernel_spmd(ncf, maps_f, core_ids=list(range(NCORES)))
    return np.stack([np.asarray(r["out"]) for r in rf.results], axis=0)


# version 07 (final)
# speedup vs baseline: 1.8395x; 1.2370x over previous
import math
from contextlib import ExitStack

import numpy as np
import concourse.bass as bass
import concourse.mybir as mybir
from concourse.bass_utils import run_bass_kernel_spmd

F32 = mybir.dt.float32
BF16 = mybir.dt.bfloat16
AF = mybir.ActivationFunctionType
ALU = mybir.AluOpType
AX = mybir.AxisListType

HID = 1024
H = 8
DH = 64
VH = 128
E = 8
GAMMA = 1.0 - 2.0 ** (-5.0 - np.arange(H, dtype=np.float64))
C1G = 1.5957691216057308
C2G = 0.044715


def ctab_layout(T):
    NT = T // 128
    NDIFF = NT + 3
    o = {}
    c = 0
    o["ESC"] = c; c += H * NDIFF
    o["GNW"] = c; c += 8
    o["GNB"] = c; c += 8
    o["GB1"] = c; c += 8
    o["GB2"] = c; c += 4
    o["GB3"] = c; c += 1
    o["E0B1"] = c; c += 96
    o["E1B1"] = c; c += 48
    o["E1B2"] = c; c += 48
    o["E2B1"] = c; c += 48
    o["LNS"] = c; c += 8
    o["LNO"] = c; c += 8
    o["AFFB"] = c; c += 8
    o["W"] = c
    return o


class _PhaseDone(Exception):
    pass


class Res:
    __slots__ = ("w", "r")

    def __init__(self):
        self.w = None
        self.r = []


class Sched:
    def __init__(self):
        self.q = {}

    def add(self, name, sems, inc):
        self.q[name] = dict(sems=list(sems), inc=inc, idx=0,
                            uses=[0] * len(sems), hw={}, prog=[])

    def emit(self, qn, method, *args, reads=(), writes=(), **kw):
        q = self.q[qn]
        need = {}

        def _n(tag):
            if tag is None:
                return
            tq, slot, tc = tag
            if tq == qn == "pe":
                return
            k = (tq, slot)
            if need.get(k, 0) < tc:
                need[k] = tc

        for r in reads:
            _n(r.w)
        for w in writes:
            _n(w.w)
            for tag in w.r:
                _n(tag)
        alt = kw.pop("alt_q", None)
        qs = self.q[alt] if alt else q
        an = alt if alt else qn
        nsl = len(qs["sems"])
        slot = qs["idx"] % nsl
        if qs["inc"] != 1 and qs["uses"][slot] > 0:
            k = (an, slot)
            tc = qs["inc"] * qs["uses"][slot]
            if need.get(k, 0) < tc:
                need[k] = tc
        for k in sorted(need):
            tc = need[k]
            tq, tslot = k
            if q["hw"].get(k, 0) >= tc:
                continue
            q["prog"].append(("w", self.q[tq]["sems"][tslot], tc, None))
            q["hw"][k] = tc
        qs["idx"] += 1
        qs["uses"][slot] += 1
        q["prog"].append(("i", method, (args, kw), (an, slot, qs["inc"])))
        tag = (an, slot, qs["inc"] * qs["uses"][slot])
        for w in writes:
            w.w = tag
            w.r = []
        for r in reads:
            r.r.append(tag)

    def rawwait(self, qn, sem, val):
        self.q[qn]["prog"].append(("w", sem, val, None))

    def qdone(self, qn):
        q = self.q[qn]
        return [(q["sems"][s], q["inc"] * q["uses"][s])
                for s in range(len(q["sems"])) if q["uses"][s] > 0]

    def replay(self, qn, eng):
        q = self.q[qn]
        for kind, a, b, inc in q["prog"]:
            if kind == "w":
                eng.wait_ge(a, b)
            else:
                args, kw = b
                an, slot, v = inc
                getattr(eng, a)(*args, **kw).then_inc(self.q[an]["sems"][slot], v)


def build_nc(T, NCORES, dbg=False, phases=99, mode="full"):
    NT = T // 128
    NCH = T // 512
    ETOK = 1024 if T >= 1024 else T
    NG = T // ETOK
    ENC = ETOK // 512
    NTT = ETOK // 128
    NDIFF = NT + 3
    CO = ctab_layout(T)
    CTW = CO["W"]
    NROWS = NCORES * NT
    assert NROWS <= 128
    NTOT = NCORES * T
    KPOS = math.floor(0.9 * (NTOT - 1))
    FRAC = 0.9 * (NTOT - 1) - KPOS

    nc = bass.Bass("TRN2", target_bir_lowering=False)
    dt = nc.dram_tensor
    x_in = dt("x", [T, HID], F32, kind="ExternalInput").ap()
    wqT = dt("wqT", [HID, H * DH], F32, kind="ExternalInput").ap()
    wkT = dt("wkT", [HID, H * DH], F32, kind="ExternalInput").ap()
    wvT = dt("wvT", [HID, H * VH], F32, kind="ExternalInput").ap()
    wgT = dt("wgT", [HID, H * VH], F32, kind="ExternalInput").ap()
    woT = dt("woT", [H * VH, HID], F32, kind="ExternalInput").ap()
    gw1 = dt("gw1", [HID, HID], F32, kind="ExternalInput").ap()
    gw2 = dt("gw2", [HID, 512], F32, kind="ExternalInput").ap()
    gw3 = dt("gw3", [512, E], F32, kind="ExternalInput").ap()
    affwT = dt("affwT", [128, 64], F32, kind="ExternalInput").ap()
    e0w1 = dt("e0w1", [3, HID, 4096], F32, kind="ExternalInput").ap()
    e0w2 = dt("e0w2", [3, 4096, HID], F32, kind="ExternalInput").ap()
    e1w1 = dt("e1w1", [3, HID, 2048], F32, kind="ExternalInput").ap()
    e1w2 = dt("e1w2", [3, 2048, 2048], F32, kind="ExternalInput").ap()
    e1w3 = dt("e1w3", [3, 2048, HID], F32, kind="ExternalInput").ap()
    e2w1 = dt("e2w1", [2, HID, 3072], F32, kind="ExternalInput").ap()
    e2w2 = dt("e2w2", [2, 3072, HID], F32, kind="ExternalInput").ap()
    biasmat = dt("biasmat", [E, HID], F32, kind="ExternalInput").ap()
    ctab_d = dt("ctab", [128, CTW], F32, kind="ExternalInput").ap()
    trig_d = dt("trig", [64, 2 * T], F32, kind="ExternalInput").ap()
    masks_d = dt("masks", [128, 4 * 512], F32, kind="ExternalInput").ap()
    idn_d = dt("idn", [128, 128], F32, kind="ExternalInput").ap()
    noise_d = dt("noiseT", [E, T], F32, kind="ExternalInput").ap()
    eu_d = dt("eu", [1, E], F32, kind="ExternalInput").ap()
    epsr_d = dt("epsrow", [H, T], F32, kind="ExternalInput").ap()
    out_d = dt("out", [T, HID], F32, kind="ExternalOutput").ap()
    x1sp = dt("x1spill", [T, HID], F32).ap()
    if mode == "attn":
        x1out = dt("x1out", [T, HID], F32, kind="ExternalOutput").ap()
        impout = dt("impout", [NT, 128], F32, kind="ExternalOutput").ap()
    if mode == "ffn":
        x1in = dt("x1in", [T, HID], F32, kind="ExternalInput").ap()
        gimp = dt("gimp", [NROWS, 128], F32, kind="ExternalInput").ap()
    if dbg:
        d_ln0T = dt("d_ln0T", [128, 8 * T], F32).ap()
        d_q = dt("d_q", [128, 4 * T], F32).ap()
        d_k = dt("d_k", [128, 4 * T], F32).ap()
        d_v = dt("d_v", [128, 8 * T], F32).ap()
        d_g = dt("d_g", [128, 8 * T], F32).ap()
        d_of = dt("d_of", [128, 8 * T], F32).ap()
        d_ao = dt("d_ao", [128, 8 * T], F32).ap()
        d_sm = dt("d_sm", [128, 18], F32, kind="ExternalOutput").ap()
        d_xb = dt("d_xb", [128, 16], F32, kind="ExternalOutput").ap()
        d_s1 = dt("d_s1", [128, 1024], F32).ap()
    cc_in = dt("cc_in", [NT, 128], F32)
    cc_out = dt("cc_out", [NROWS, 128], F32)

    es = ExitStack()
    sb = lambda n, shp, d: es.enter_context(nc.sbuf_tensor(n, shp, d))
    mksem = lambda n: es.enter_context(nc.semaphore(n))

    AWW = max(8 * T, NT * 512 + 8192, NT * 1024)
    AW = sb("AW", [128, AWW], BF16)
    AQ = sb("AQ", [128, 4 * T], BF16)
    AK = sb("AK", [128, 4 * T], BF16)
    AVW = max(8 * T, 16 * ETOK)
    AV = sb("AV", [128, AVW], BF16)
    AG = sb("AG", [128, AVW], BF16)
    WB = sb("WB", [128, 4096], BF16)
    S1 = sb("S1", [128, 2 * T], BF16)
    S2 = sb("S2", [128, T], F32)
    S3 = sb("S3", [128, T], F32)
    V1 = sb("V1", [1, T], F32)
    V2 = sb("V2", [1, T], F32)
    SM = sb("SM", [128, 112], F32)
    PRB = sb("PRB", [128, NT * E], F32)
    GI = sb("GI", [128, 128], F32)
    LGS = sb("LGS", [E, 512], F32)
    CT = sb("CT", [128, CTW], F32)
    TRG = sb("TRG", [64, 2 * T], BF16)
    MSK = sb("MSK", [128, 4 * 512], BF16)
    IDB = sb("IDB", [128, 128], BF16)
    IDF = sb("IDF", [128, 128], F32)
    ONEB = sb("ONEB", [128, 1], BF16)
    ONEF = sb("ONEF", [128, 1], F32)
    ONRF = sb("ONRF", [1, 128], F32)
    BM = sb("BM", [E, HID], BF16)
    W3B = sb("W3B", [128, 4 * E], BF16)
    AFW = sb("AFW", [128, 64], F32)
    EUS = sb("EUS", [1, 16], F32)
    SC1 = sb("SC1", [1, 160], F32)
    SCB = sb("SCB", [128, 8], F32)
    NZE = sb("NZE", [128, 8], F32)
    NEGT = sb("NEGT", [128, 8], F32)
    PRT = sb("PRT", [E, T], BF16)

    ps_es = ExitStack()
    P = [ps_es.enter_context(nc.psum_tensor(f"p{i}", [128, 512], F32)) for i in range(6)]
    PB = ps_es.enter_context(nc.psum_tensor("pb", [128, 1024], BF16))
    PS = ps_es.enter_context(nc.psum_tensor("pst", [1, 512], F32))

    sd = Sched()
    R = {}

    def rs(*names):
        out = []
        for n in names:
            if n not in R:
                R[n] = Res()
            out.append(R[n])
        return out

    (rAW, rAQ, rAK, rAV, rAG, rWB0, rWB1, rS1, rS2, rS3a, rS3b, rV1, rV2, rSM,
     rPRB, rGI, rLGS, rCT, rTRG, rMSK, rIDB, rIDF, rONEB, rONEF, rONRF, rBM,
     rW3B, rAFW, rEUS, rSC1, rSCB, rNZE, rNEGT, rPRT, rX1SP, rCCI, rCCO, rOUT, rPBp,
     rPSp) = rs("AW", "AQ", "AK", "AV", "AG", "WB0", "WB1", "S1", "S2", "S3a",
                "S3b", "V1", "V2", "SM", "PRB", "GI", "LGS", "CT", "TRG", "MSK",
                "IDB", "IDF", "ONEB", "ONEF", "ONRF", "BM", "W3B", "AFW", "EUS",
                "SC1", "SCB", "NZE", "NEGT", "PRT", "X1SP", "CCI", "CCO", "OUT", "PBp",
                "PSp")
    rP = rs(*[f"P{i}" for i in range(6)])
    (res_impout,) = rs("IMPOUT")
    rPO = rs(*[f"PO{i}" for i in range(8)])
    rWB = [rWB0, rWB1]
    S3_HALF = [rS3a, rS3b]

    try:
     if True:
        for n in ("pe", "dve", "act"):
            sd.add(n, [mksem("s_" + n)], 1)
        sd.add("gd", [mksem(f"s_gd{i}") for i in range(8)], 16)
        sd.add("sy", [mksem(f"s_sy{i}") for i in range(8)], 16)
        sd.add("cc", [mksem("s_cc")], 16)
        E_ = sd.emit

        def mm(out_ap, lhsT_ap, rhs_ap, rd, wr, start, stop):
            E_("pe", "matmul", reads=tuple(rd), writes=tuple(wr),
               out=out_ap, lhsT=lhsT_ap, rhs=rhs_ap, start=start, stop=stop)

        E_("sy", "dma_start", reads=(), writes=(rCT,), out=CT[:, :], in_=ctab_d[:, :])
        E_("gd", "dma_start", reads=(), writes=(rTRG,), out=TRG[:, :], in_=trig_d[:, :])
        E_("gd", "dma_start", reads=(), writes=(rMSK,), out=MSK[:, :], in_=masks_d[:, :])
        E_("gd", "dma_start", reads=(), writes=(rIDB,), out=IDB[:, :], in_=idn_d[:, :])
        E_("sy", "dma_start", reads=(), writes=(rIDF,), out=IDF[:, :], in_=idn_d[:, :])
        E_("gd", "dma_start", reads=(), writes=(rBM,), out=BM[:, :], in_=biasmat[:, :])
        E_("gd", "dma_start", reads=(), writes=(rW3B,), out=W3B[:, :],
           in_=gw3.rearrange("(kt p) e -> p kt e", p=128))
        E_("sy", "dma_start", reads=(), writes=(rAFW,), out=AFW[:, :], in_=affwT[:, :])
        E_("sy", "dma_start", reads=(), writes=(rEUS,), out=EUS[:, 0:E], in_=eu_d[:, :])
        E_("dve", "memset", reads=(), writes=(rONEB,), ap=ONEB[:, :], constant=1.0)
        E_("dve", "memset", reads=(), writes=(rONEF,), ap=ONEF[:, :], constant=1.0)
        E_("dve", "memset", reads=(), writes=(rONRF,), ap=ONRF[:, :], constant=1.0)
        E_("dve", "memset", reads=(), writes=(rNZE,), ap=NZE[:, :], constant=0.0)
        E_("dve", "memset", reads=(rNZE,), writes=(rNZE,), ap=NZE[:, 0:2], constant=0.125)
        E_("dve", "memset", reads=(), writes=(rNEGT,), ap=NEGT[:, :], constant=-1e9)

        def ln_stats(src_ap, src_res):
            E_("dve", "bn_stats", reads=(src_res,), writes=(rSM,),
               out=SM[:, 80:86], in_=src_ap[:, 0:512])
            E_("dve", "bn_stats", reads=(src_res,), writes=(rSM,),
               out=SM[:, 86:92], in_=src_ap[:, 512:1024])
            E_("dve", "bn_aggr", reads=(rSM,), writes=(rSM,),
               out=SM[:, 92:94], in_=SM[:, 80:92])
            E_("dve", "tensor_scalar", reads=(rSM,), writes=(rSM,),
               out=SM[:, 94:95], in0=SM[:, 93:94], scalar1=1e-5, scalar2=None, op0=ALU.add)
            E_("act", "activation", reads=(rSM,), writes=(rSM,),
               out=SM[:, 95:96], in_=SM[:, 94:95], func=AF.Ln)
            E_("act", "activation", reads=(rSM,), writes=(rSM,),
               out=SM[:, 96:97], in_=SM[:, 95:96], func=AF.Exp, scale=-0.5)
            E_("dve", "scalar_tensor_tensor", reads=(rSM,), writes=(rSM,),
               out=SM[:, 97:98], in0=SM[:, 92:93], scalar=-1.0, in1=SM[:, 96:97],
               op0=ALU.mult, op1=ALU.mult)

        skip_attn = (mode == "ffn")
        if not skip_attn:
            for i in range(NT):
                hb = i % 2
                xbuf = S3[:, hb * 1024:hb * 1024 + 1024]
                E_("sy", "dma_start", reads=(), writes=(S3_HALF[hb],),
                   out=xbuf, in_=x_in[i * 128:(i + 1) * 128, :])
                ln_stats(xbuf, S3_HALF[hb])
                E_("act", "activation", reads=(S3_HALF[hb], rSM), writes=(rS1,),
                   out=S1[:, 0:1024], in_=xbuf, func=AF.Identity, scale=SM[:, 96:97],
                   bias=SM[:, 97:98])
                if dbg and i == 0:
                    E_("gd", "dma_start", reads=(rSM,), writes=(), out=d_sm[:, :],
                       in_=SM[:, 80:98])
                    E_("gd", "dma_start", reads=(S3_HALF[0],), writes=(), out=d_xb[:, :],
                       in_=S3[:, 0:16])
                    E_("gd", "dma_start", reads=(rS1,), writes=(), out=d_s1[:, :],
                       in_=S1[:, 0:1024])
                for db in range(8):
                    E_("pe", "transpose", reads=(rS1, rIDB), writes=(rPBp,),
                       out=PB[:, 0:128], in_=S1[:, db * 128:(db + 1) * 128], identity=IDB[:, :])
                    E_("dve", "tensor_copy", reads=(rPBp,), writes=(rAW,),
                       out=AW[:, db * T + i * 128: db * T + (i + 1) * 128], in_=PB[:, 0:128])

            if dbg:
                E_("gd", "dma_start", reads=(rAW,), writes=(), out=d_ln0T[:, :],
                   in_=AW[:, 0:8 * T])
            if phases < 2:
                raise _PhaseDone()
            def rotary(dest, rdest, mt, src32_res):
                cosF = TRG[0:64, 0:T]
                sinF = TRG[0:64, T:2 * T]
                for par in range(2):
                    if par == 1:
                        E_("dve", "tensor_copy", reads=(src32_res,), writes=(src32_res,),
                           out=S2[0:64, :], in_=S2[64:128, :])
                    E_("dve", "tensor_scalar", reads=(src32_res,), writes=(rS3a, rS3b),
                       out=S3[0:32, :], in0=S2[32:64, :], scalar1=-1.0, scalar2=None,
                       op0=ALU.mult)
                    E_("dve", "tensor_copy", reads=(src32_res,), writes=(rS3a, rS3b),
                       out=S3[32:64, :], in_=S2[0:32, :])
                    E_("dve", "tensor_tensor", reads=(rS3a, rS3b, rTRG), writes=(rS3a, rS3b),
                       out=S3[0:64, :], in0=S3[0:64, :], in1=sinF, op=ALU.mult)
                    E_("dve", "tensor_tensor", reads=(src32_res, rTRG), writes=(src32_res,),
                       out=S2[0:64, :], in0=S2[0:64, :], in1=cosF, op=ALU.mult)
                    E_("dve", "tensor_tensor", reads=(src32_res, rS3a, rS3b), writes=(rdest,),
                       out=dest[64 * par:64 * par + 64, mt * T:(mt + 1) * T],
                       in0=S2[0:64, :], in1=S3[0:64, :], op=ALU.add)

            for wap, dest, rdest in ((wqT, AQ, rAQ), (wkT, AK, rAK)):
                E_("gd", "dma_start", reads=(), writes=(rWB0, rWB1),
                   out=WB[:, 0:4096], in_=wap.rearrange("(kt p) m -> p kt m", p=128))
                for mt in range(4):
                    for c in range(NCH):
                        pi = c % 2
                        for kt in range(8):
                            mm(P[pi][:, :], WB[:, kt * 512 + mt * 128: kt * 512 + (mt + 1) * 128],
                               AW[:, kt * T + c * 512: kt * T + (c + 1) * 512],
                               rd=[rWB0, rWB1, rAW], wr=[rP[pi]], start=(kt == 0), stop=(kt == 7))
                        E_("act", "activation", reads=(rP[pi],), writes=(rS2,),
                           out=S2[:, c * 512:(c + 1) * 512], in_=P[pi][:, :], func=AF.Identity)
                    rotary(dest, rdest, mt, rS2)
            for half in range(2):
                E_("gd", "dma_start", reads=(), writes=(rWB0, rWB1),
                   out=WB[:, 0:4096],
                   in_=wvT[:, half * 512:(half + 1) * 512].rearrange("(kt p) m -> p kt m", p=128))
                for tt in range(NT):
                    pi = tt % 2
                    for kt in range(8):
                        mm(P[pi][:, :], AW[:, kt * T + tt * 128: kt * T + (tt + 1) * 128],
                           WB[:, kt * 512:(kt + 1) * 512],
                           rd=[rWB0, rWB1, rAW], wr=[rP[pi]], start=(kt == 0), stop=(kt == 7))
                    E_("act", "activation", reads=(rP[pi],), writes=(rAV,),
                       out=AV[:, tt * 1024 + half * 512: tt * 1024 + (half + 1) * 512],
                       in_=P[pi][:, :], func=AF.Identity)
            for half in range(2):
                E_("gd", "dma_start", reads=(), writes=(rWB0, rWB1),
                   out=WB[:, 0:4096],
                   in_=wgT[:, half * 512:(half + 1) * 512].rearrange("(kt p) m -> p kt m", p=128))
                for mt in range(4):
                    gmt = half * 4 + mt
                    for c in range(NCH):
                        pi = 2 + c % 2
                        for kt in range(8):
                            mm(P[pi][:, :],
                               WB[:, kt * 512 + mt * 128: kt * 512 + (mt + 1) * 128],
                               AW[:, kt * T + c * 512: kt * T + (c + 1) * 512],
                               rd=[rWB0, rWB1, rAW], wr=[rP[pi]], start=(kt == 0), stop=(kt == 7))
                        E_("act", "activation", reads=(rP[pi],), writes=(rAG,),
                           out=AG[:, gmt * T + c * 512: gmt * T + (c + 1) * 512],
                           in_=P[pi][:, :], func=AF.Identity)
            for gmt in range(8):
                E_("act", "activation", reads=(rAG,), writes=(rAG,),
                   out=AG[:, gmt * T:(gmt + 1) * T], in_=AG[:, gmt * T:(gmt + 1) * T],
                   func=AF.Silu)

            if dbg:
                E_("gd", "dma_start", reads=(rAQ,), writes=(), out=d_q[:, :], in_=AQ[:, :])
                E_("gd", "dma_start", reads=(rAK,), writes=(), out=d_k[:, :], in_=AK[:, :])
                E_("gd", "dma_start", reads=(rAV,), writes=(), out=d_v[:, :], in_=AV[:, 0:8 * T])
                E_("gd", "dma_start", reads=(rAG,), writes=(), out=d_g[:, :], in_=AG[:, 0:8 * T])
            if phases < 3:
                raise _PhaseDone()
            WOC = AWW - 8192
            E_("gd", "dma_start", reads=(rAW,), writes=(rAW,),
               out=AW[:, WOC:WOC + 8192], in_=woT.rearrange("(ct p) o -> p ct o", p=128))
            for h in range(H):
                mt, par = h // 2, h % 2
                base = 64 * par
                E_("sy", "dma_start", reads=(), writes=(rS3a, rS3b),
                   out=S3[0:1, 0:T], in_=epsr_d[h:h + 1, :])
                for c in range(NCH):
                    nst = 4 * c + 4
                    for st in range(nst):
                        pi = st % 2
                        di = 4 * c - st + 3
                        mm(P[pi][:, :], AK[base:base + 64, mt * T + st * 128: mt * T + (st + 1) * 128],
                           AQ[base:base + 64, mt * T + c * 512: mt * T + (c + 1) * 512],
                           rd=[rAK, rAQ], wr=[rP[pi]], start=True, stop=True)
                        E_("act", "activation", reads=(rP[pi], rCT), writes=(rAW,),
                           out=AW[:, st * 512:(st + 1) * 512], in_=P[pi][:, :], func=AF.Identity,
                           scale=CT[:, CO["ESC"] + h * NDIFF + di: CO["ESC"] + h * NDIFF + di + 1])
                        if st >= 4 * c:
                            k = st - 4 * c
                            E_("dve", "tensor_tensor", reads=(rAW, rMSK), writes=(rAW,),
                               out=AW[:, st * 512:(st + 1) * 512],
                               in0=AW[:, st * 512:(st + 1) * 512],
                               in1=MSK[:, k * 512:(k + 1) * 512], op=ALU.mult)
                    po = 4 + (c % 2)
                    for st in range(nst):
                        mm(P[po][:, :], AV[:, st * 1024 + h * 128: st * 1024 + h * 128 + 128],
                           AW[:, st * 512:(st + 1) * 512],
                           rd=[rAV, rAW], wr=[rP[po]], start=(st == 0), stop=(st == nst - 1))
                    E_("act", "activation", reads=(rP[po],), writes=(rS1,),
                       out=S1[:, c * 512:(c + 1) * 512], in_=P[po][:, :], func=AF.Identity)
                E_("act", "activation", reads=(rS1,), writes=(rS1,),
                   out=S1[:, T:2 * T], in_=S1[:, 0:T], func=AF.Square)
                for c in range(NCH):
                    mm(PS[:, :], ONEB[:, :], S1[:, c * 512:(c + 1) * 512],
                       rd=[rONEB, rS1], wr=[rPSp], start=True, stop=True)
                    E_("act", "activation", reads=(rPSp,), writes=(rV1,),
                       out=V1[:, c * 512:(c + 1) * 512], in_=PS[:, :], func=AF.Identity)
                    mm(PS[:, :], ONEB[:, :], S1[:, T + c * 512: T + (c + 1) * 512],
                       rd=[rONEB, rS1], wr=[rPSp], start=True, stop=True)
                    E_("act", "activation", reads=(rPSp,), writes=(rV2,),
                       out=V2[:, c * 512:(c + 1) * 512], in_=PS[:, :], func=AF.Identity)
                V3 = S2[0:1, 0:T]
                E_("dve", "tensor_scalar", reads=(rV1,), writes=(rV1,),
                   out=V1[:, :], in0=V1[:, :], scalar1=1.0 / VH, scalar2=None, op0=ALU.mult)
                E_("dve", "tensor_scalar", reads=(rV2,), writes=(rV2,),
                   out=V2[:, :], in0=V2[:, :], scalar1=1.0 / VH, scalar2=None, op0=ALU.mult)
                E_("dve", "tensor_tensor", reads=(rV1,), writes=(rS2,),
                   out=V3, in0=V1[:, :], in1=V1[:, :], op=ALU.mult)
                E_("dve", "tensor_tensor", reads=(rV2, rS2), writes=(rV2,),
                   out=V2[:, :], in0=V2[:, :], in1=V3, op=ALU.subtract)
                E_("dve", "tensor_tensor", reads=(rV2, rS3a, rS3b), writes=(rV2,),
                   out=V2[:, :], in0=V2[:, :], in1=S3[0:1, 0:T], op=ALU.add)
                E_("act", "activation", reads=(rV2,), writes=(rV2,),
                   out=V2[:, :], in_=V2[:, :], func=AF.Ln)
                E_("act", "activation", reads=(rV2,), writes=(rV2,),
                   out=V2[:, :], in_=V2[:, :], func=AF.Exp, scale=-0.5)
                E_("dve", "scalar_tensor_tensor", reads=(rV1, rV2), writes=(rS2,),
                   out=V3, in0=V1[:, :], scalar=-1.0, in1=V2[:, :],
                   op0=ALU.mult, op1=ALU.mult)
                for c in range(NCH):
                    mm(P[0][:, :], ONRF[:, :], V2[:, c * 512:(c + 1) * 512],
                       rd=[rONRF, rV2], wr=[rP[0]], start=True, stop=True)
                    mm(P[1][:, :], ONRF[:, :], S2[0:1, c * 512:(c + 1) * 512],
                       rd=[rONRF, rS2], wr=[rP[1]], start=True, stop=True)
                    tmp = S1[:, T + c * 512: T + (c + 1) * 512]
                    E_("dve", "tensor_tensor", reads=(rS1, rP[0]), writes=(rS1,),
                       out=tmp, in0=S1[:, c * 512:(c + 1) * 512], in1=P[0][:, :], op=ALU.mult)
                    E_("dve", "tensor_tensor", reads=(rS1, rP[1]), writes=(rS1,),
                       out=tmp, in0=tmp, in1=P[1][:, :], op=ALU.add)
                    E_("act", "activation", reads=(rS1, rCT), writes=(rS1,),
                       out=tmp, in_=tmp, func=AF.Identity,
                       scale=CT[:, CO["GNW"] + h:CO["GNW"] + h + 1],
                       bias=CT[:, CO["GNB"] + h:CO["GNB"] + h + 1])
                    E_("dve", "tensor_tensor", reads=(rS1, rAG), writes=(rAG,),
                       out=AG[:, h * T + c * 512: h * T + (c + 1) * 512],
                       in0=tmp, in1=AG[:, h * T + c * 512: h * T + (c + 1) * 512],
                       op=ALU.mult)

            if dbg:
                E_("gd", "dma_start", reads=(rAG,), writes=(), out=d_of[:, :], in_=AG[:, 0:8 * T])
            if phases < 4:
                raise _PhaseDone()
            for m in range(8):
                dest, dres = (AQ, rAQ) if m < 4 else (AK, rAK)
                for c in range(NCH):
                    pi = c % 2
                    for h in range(H):
                        mm(P[pi][:, :],
                           AW[:, WOC + h * 1024 + m * 128: WOC + h * 1024 + (m + 1) * 128],
                           AG[:, h * T + c * 512: h * T + (c + 1) * 512],
                           rd=[rAW, rAG], wr=[rP[pi]], start=(h == 0), stop=(h == 7))
                    E_("act", "activation", reads=(rP[pi],), writes=(dres,),
                       out=dest[:, (m % 4) * T + c * 512:(m % 4) * T + (c + 1) * 512],
                       in_=P[pi][:, :], func=AF.Identity)
            if dbg:
                E_("gd", "dma_start", reads=(rAQ,), writes=(), out=d_ao[:, 0:4 * T], in_=AQ[:, :])
                E_("gd", "dma_start", reads=(rAK,), writes=(), out=d_ao[:, 4 * T:8 * T], in_=AK[:, :])
            for c in range(NCH):
                for m in range(8):
                    src_t, sres = (AQ, rAQ) if m < 4 else (AK, rAK)
                    sl = src_t[:, (m % 4) * T + c * 512:(m % 4) * T + (c + 1) * 512]
                    mm(PS[:, :], ONEB[:, :], sl, rd=[rONEB, sres], wr=[rPSp],
                       start=(m == 0), stop=(m == 7))
                E_("act", "activation", reads=(rPSp,), writes=(rV1,),
                   out=V1[:, c * 512:(c + 1) * 512], in_=PS[:, :], func=AF.Identity)
                for m in range(8):
                    src_t, sres = (AQ, rAQ) if m < 4 else (AK, rAK)
                    sl = src_t[:, (m % 4) * T + c * 512:(m % 4) * T + (c + 1) * 512]
                    E_("act", "activation", reads=(sres,), writes=(rS1,),
                       out=S1[:, 0:512], in_=sl, func=AF.Square)
                    mm(PS[:, :], ONEB[:, :], S1[:, 0:512], rd=[rONEB, rS1], wr=[rPSp],
                       start=(m == 0), stop=(m == 7))
                E_("act", "activation", reads=(rPSp,), writes=(rV2,),
                   out=V2[:, c * 512:(c + 1) * 512], in_=PS[:, :], func=AF.Identity)
            V3 = S2[0:1, 0:T]
            E_("dve", "tensor_scalar", reads=(rV1,), writes=(rV1,),
               out=V1[:, :], in0=V1[:, :], scalar1=1.0 / HID, scalar2=None, op0=ALU.mult)
            E_("dve", "tensor_scalar", reads=(rV2,), writes=(rV2,),
               out=V2[:, :], in0=V2[:, :], scalar1=1.0 / HID, scalar2=None, op0=ALU.mult)
            E_("dve", "tensor_tensor", reads=(rV1,), writes=(rS2,),
               out=V3, in0=V1[:, :], in1=V1[:, :], op=ALU.mult)
            E_("dve", "tensor_tensor", reads=(rV2, rS2), writes=(rV2,),
               out=V2[:, :], in0=V2[:, :], in1=V3, op=ALU.subtract)
            E_("dve", "tensor_scalar", reads=(rV2,), writes=(rV2,),
               out=V2[:, :], in0=V2[:, :], scalar1=1e-5, scalar2=None, op0=ALU.add)
            E_("act", "activation", reads=(rV2,), writes=(rV2,),
               out=V2[:, :], in_=V2[:, :], func=AF.Ln)
            E_("act", "activation", reads=(rV2,), writes=(rV2,),
               out=V2[:, :], in_=V2[:, :], func=AF.Exp, scale=-0.5)
            E_("dve", "scalar_tensor_tensor", reads=(rV1, rV2), writes=(rS2,),
               out=V3, in0=V1[:, :], scalar=-1.0, in1=V2[:, :], op0=ALU.mult, op1=ALU.mult)
            for i in range(NT):
                E_("pe", "transpose", reads=(rV2, rIDF), writes=(rP[0],),
                   out=P[0][:, 0:1], in_=V2[:, i * 128:(i + 1) * 128], identity=IDF[0:1, 0:1])
                E_("act", "activation", reads=(rP[0],), writes=(rSM,),
                   out=SM[:, 16 + i:17 + i], in_=P[0][:, 0:1], func=AF.Identity)
                E_("pe", "transpose", reads=(rS2, rIDF), writes=(rP[1],),
                   out=P[1][:, 0:1], in_=S2[0:1, i * 128:(i + 1) * 128], identity=IDF[0:1, 0:1])
                E_("act", "activation", reads=(rP[1],), writes=(rSM,),
                   out=SM[:, 32 + i:33 + i], in_=P[1][:, 0:1], func=AF.Identity)
            for i in range(NT):
                hb = i % 2
                xbuf = S3[:, hb * 1024:hb * 1024 + 1024]
                E_("sy", "dma_start", reads=(), writes=(S3_HALF[hb],),
                   out=xbuf, in_=x_in[i * 128:(i + 1) * 128, :])
                for m in range(8):
                    src_t, sres = (AQ, rAQ) if m < 4 else (AK, rAK)
                    E_("pe", "transpose", reads=(sres, rIDB), writes=(rPBp,),
                       out=PB[:, (m % 2) * 128:(m % 2) * 128 + 128],
                       in_=src_t[:, (m % 4) * T + i * 128:(m % 4) * T + (i + 1) * 128],
                       identity=IDB[:, :])
                    E_("act", "activation", reads=(rPBp, rSM), writes=(rS1,),
                       out=S1[:, m * 128:(m + 1) * 128],
                       in_=PB[:, (m % 2) * 128:(m % 2) * 128 + 128],
                       func=AF.Identity, scale=SM[:, 16 + i:17 + i], bias=SM[:, 32 + i:33 + i])
                x1buf = S2[:, 0:1024]
                E_("dve", "tensor_tensor", reads=(S3_HALF[hb], rS1), writes=(rS2,),
                   out=x1buf, in0=xbuf, in1=S1[:, 0:1024], op=ALU.add)
                ln_stats(x1buf, rS2)
                E_("dve", "reciprocal", reads=(rSM,), writes=(rSM,),
                   out=SM[:, 98:99], in_=SM[:, 94:95])
                E_("dve", "tensor_tensor", reads=(rSM,), writes=(rSM,),
                   out=SM[:, 99:100], in0=SM[:, 93:94], in1=SM[:, 98:99], op=ALU.mult)
                E_("act", "activation", reads=(rSM,), writes=(rSM,),
                   out=SM[:, 64 + i:65 + i], in_=SM[:, 99:100], func=AF.Sqrt, scale=float(HID))
                E_("act", "activation", reads=(rS2, rSM), writes=(rS1, rSM),
                   out=S1[:, 0:1024], in_=x1buf, func=AF.Abs, scale=SM[:, 96:97],
                   bias=SM[:, 97:98], accum_out=SM[:, i:i + 1])
                E_("dve", "tensor_copy", reads=(rSM,), writes=(rSM,),
                   out=SM[:, 32 + i:33 + i], in_=SM[:, 96:97])
                E_("dve", "tensor_copy", reads=(rSM,), writes=(rSM,),
                   out=SM[:, 48 + i:49 + i], in_=SM[:, 97:98])
                E_("sy", "dma_start", reads=(rS2,), writes=(rX1SP,),
                   out=(x1out if mode == "attn" else x1sp)[i * 128:(i + 1) * 128, :],
                   in_=x1buf)

            if phases < 5:
                raise _PhaseDone()
        if mode == "ffn":
            for i in range(NT):
                hb = i % 2
                xbuf = S3[:, hb * 1024:hb * 1024 + 1024]
                E_("sy", "dma_start", reads=(), writes=(S3_HALF[hb],),
                   out=xbuf, in_=x1in[i * 128:(i + 1) * 128, :])
                ln_stats(xbuf, S3_HALF[hb])
                E_("dve", "reciprocal", reads=(rSM,), writes=(rSM,),
                   out=SM[:, 98:99], in_=SM[:, 94:95])
                E_("dve", "tensor_tensor", reads=(rSM,), writes=(rSM,),
                   out=SM[:, 99:100], in0=SM[:, 93:94], in1=SM[:, 98:99], op=ALU.mult)
                E_("act", "activation", reads=(rSM,), writes=(rSM,),
                   out=SM[:, 64 + i:65 + i], in_=SM[:, 99:100], func=AF.Sqrt,
                   scale=float(HID))
                E_("act", "activation", reads=(S3_HALF[hb], rSM), writes=(rS1, rSM),
                   out=S1[:, 0:1024], in_=xbuf, func=AF.Abs, scale=SM[:, 96:97],
                   bias=SM[:, 97:98], accum_out=SM[:, i:i + 1])
                E_("dve", "tensor_copy", reads=(rSM,), writes=(rSM,),
                   out=SM[:, 32 + i:33 + i], in_=SM[:, 96:97])
                E_("dve", "tensor_copy", reads=(rSM,), writes=(rSM,),
                   out=SM[:, 48 + i:49 + i], in_=SM[:, 97:98])
        if mode != "ffn":
            E_("pe", "transpose", reads=(rSM, rIDF), writes=(rP[0],),
               out=P[0][0:NT, 0:128], in_=SM[:, 64:64 + NT], identity=IDF[:, :])
            E_("act", "activation", reads=(rP[0],), writes=(rS2,),
               out=S2[0:NT, 0:128], in_=P[0][0:NT, 0:128], func=AF.Identity)
        if mode == "attn":
            E_("sy", "dma_start", reads=(rS2,), writes=(res_impout,),
               out=impout[:, :], in_=S2[0:NT, 0:128])
            raise _PhaseDone()
        if mode == "ffn":
            E_("gd", "dma_start", reads=(), writes=(rGI,),
               out=GI[0:NROWS, 0:128], in_=gimp[:, :])
        else:
            E_("gd", "dma_start", reads=(rS2,), writes=(rCCI,),
               out=cc_in.ap()[:, :], in_=S2[0:NT, 0:128])
            E_("gd", "collective_compute", reads=(rCCI,), writes=(rCCO,),
               kind="AllGather", op=ALU.bypass, replica_groups=[list(range(NCORES))],
               ins=[cc_in.ap().opt()], outs=[cc_out.ap().opt()], alt_q="cc")
            E_("gd", "dma_start", reads=(rCCO,), writes=(rGI,),
               out=GI[0:NROWS, 0:128], in_=cc_out.ap()[:, :])
        E_("dve", "tensor_tensor", reads=(rGI,), writes=(rS2,),
           out=S2[0:NROWS, 0:128], in0=GI[0:NROWS, 0:128], in1=GI[0:NROWS, 0:128],
           op=ALU.mult)
        for srcap, sres, col in ((GI[0:NROWS, 0:128], rGI, 0), (S2[0:NROWS, 0:128], rS2, 1)):
            mm(PS[0:1, 0:128], ONEF[0:NROWS, :], srcap, rd=[rONEF, sres], wr=[rPSp],
               start=True, stop=True)
            E_("act", "activation", reads=(rPSp,), writes=(rSC1,),
               out=SC1[:, 32:32 + 128], in_=PS[0:1, 0:128], func=AF.Identity)
            E_("dve", "tensor_reduce", reads=(rSC1,), writes=(rEUS,),
               out=EUS[:, 10 + col:11 + col], in_=SC1[:, 32:32 + 128], axis=AX.X, op=ALU.add)
        ntotf = float(NTOT)
        E_("dve", "tensor_scalar", reads=(rEUS,), writes=(rEUS,),
           out=EUS[:, 10:12], in0=EUS[:, 10:12], scalar1=1.0 / ntotf, scalar2=None,
           op0=ALU.mult)
        mm(P[2][0:NROWS, 0:1], ONRF[0:1, 0:NROWS], EUS[0:1, 10:11],
           rd=[rONRF, rEUS], wr=[rP[2]], start=True, stop=True)
        E_("act", "activation", reads=(rP[2],), writes=(rSCB,),
           out=SCB[0:NROWS, 0:1], in_=P[2][0:NROWS, 0:1], func=AF.Identity)
        E_("dve", "tensor_scalar", reads=(rGI, rSCB), writes=(rS2,),
           out=S2[0:NROWS, 0:128], in0=GI[0:NROWS, 0:128],
           scalar1=SCB[0:NROWS, 0:1], scalar2=None, op0=ALU.subtract)
        E_("dve", "tensor_tensor", reads=(rS2,), writes=(rS2,),
           out=S2[0:NROWS, 0:128], in0=S2[0:NROWS, 0:128], in1=S2[0:NROWS, 0:128],
           op=ALU.mult)
        mm(PS[0:1, 0:128], ONEF[0:NROWS, :], S2[0:NROWS, 0:128],
           rd=[rONEF, rS2], wr=[rPSp], start=True, stop=True)
        E_("act", "activation", reads=(rPSp,), writes=(rSC1,),
           out=SC1[:, 32:32 + 128], in_=PS[0:1, 0:128], func=AF.Identity)
        E_("dve", "tensor_reduce", reads=(rSC1,), writes=(rEUS,),
           out=EUS[:, 13:14], in_=SC1[:, 32:32 + 128], axis=AX.X, op=ALU.add)
        E_("dve", "tensor_scalar", reads=(rEUS,), writes=(rEUS,),
           out=EUS[:, 13:14], in0=EUS[:, 13:14], scalar1=1.0 / ntotf, scalar2=None,
           op0=ALU.mult)
        E_("act", "activation", reads=(rEUS,), writes=(rEUS,),
           out=EUS[:, 14:15], in_=EUS[:, 13:14], func=AF.Sqrt)
        E_("dve", "tensor_scalar", reads=(rEUS,), writes=(rEUS,),
           out=EUS[:, 15:16], in0=EUS[:, 14:15], scalar1=0.1, scalar2=1.0,
           op0=ALU.mult, op1=ALU.add)
        E_("dve", "reciprocal", reads=(rEUS,), writes=(rEUS,),
           out=EUS[:, 9:10], in_=EUS[:, 15:16])
        E_("dve", "memset", reads=(), writes=(rSC1,), ap=SC1[:, 0:4], constant=-1.0)
        E_("dve", "tensor_reduce", reads=(rGI,), writes=(rSM,),
           out=SM[0:NROWS, 88:89], in_=GI[0:NROWS, 0:128], axis=AX.X, op=ALU.max)
        E_("pe", "transpose", reads=(rSM, rIDF), writes=(rP[0],),
           out=P[0][0:1, 0:NROWS], in_=SM[0:NROWS, 88:89],
           identity=IDF[0:NROWS, 0:NROWS])
        E_("act", "activation", reads=(rP[0],), writes=(rSC1,),
           out=SC1[:, 32:32 + NROWS], in_=P[0][0:1, 0:NROWS], func=AF.Identity)
        E_("dve", "tensor_reduce", reads=(rSC1,), writes=(rSC1,),
           out=SC1[:, 4:5], in_=SC1[:, 32:32 + NROWS], axis=AX.X, op=ALU.max)
        E_("dve", "tensor_scalar", reads=(rSC1,), writes=(rSC1,),
           out=SC1[:, 1:2], in0=SC1[:, 4:5], scalar1=1.0, scalar2=None, op0=ALU.add)
        E_("dve", "tensor_copy", reads=(rSC1,), writes=(rSC1,),
           out=SC1[:, 3:4], in_=SC1[:, 1:2])
        for it in range(64):
            for j in range(2):
                E_("dve", "tensor_tensor", reads=(rSC1,), writes=(rSC1,),
                   out=SC1[:, 5:6], in0=SC1[:, 2 * j:2 * j + 1],
                   in1=SC1[:, 2 * j + 1:2 * j + 2], op=ALU.add)
                E_("dve", "tensor_scalar", reads=(rSC1,), writes=(rSC1,),
                   out=SC1[:, 5:6], in0=SC1[:, 5:6], scalar1=0.5, scalar2=None,
                   op0=ALU.mult)
                mm(P[2][0:NROWS, 0:1], ONRF[0:1, 0:NROWS], SC1[0:1, 5:6],
                   rd=[rONRF, rSC1], wr=[rP[2]], start=True, stop=True)
                E_("act", "activation", reads=(rP[2],), writes=(rSCB,),
                   out=SCB[0:NROWS, 0:1], in_=P[2][0:NROWS, 0:1], func=AF.Identity)
                E_("dve", "tensor_scalar", reads=(rGI, rSCB), writes=(rS2,),
                   out=S2[0:NROWS, 0:128], in0=GI[0:NROWS, 0:128],
                   scalar1=SCB[0:NROWS, 0:1], scalar2=None, op0=ALU.is_le)
                mm(PS[0:1, 0:128], ONEF[0:NROWS, :], S2[0:NROWS, 0:128],
                   rd=[rONEF, rS2], wr=[rPSp], start=True, stop=True)
                E_("act", "activation", reads=(rPSp,), writes=(rSC1,),
                   out=SC1[:, 32:32 + 128], in_=PS[0:1, 0:128], func=AF.Identity)
                E_("dve", "tensor_reduce", reads=(rSC1,), writes=(rSC1,),
                   out=SC1[:, 6:7], in_=SC1[:, 32:32 + 128], axis=AX.X, op=ALU.add)
                E_("dve", "tensor_scalar", reads=(rSC1,), writes=(rSC1,),
                   out=SC1[:, 7:8], in0=SC1[:, 6:7], scalar1=float(KPOS + 1 + j),
                   scalar2=None, op0=ALU.is_ge)
                E_("dve", "tensor_tensor", reads=(rSC1,), writes=(rSC1,),
                   out=SC1[:, 8:9], in0=SC1[:, 5:6],
                   in1=SC1[:, 2 * j + 1:2 * j + 2], op=ALU.subtract)
                E_("dve", "tensor_tensor", reads=(rSC1,), writes=(rSC1,),
                   out=SC1[:, 8:9], in0=SC1[:, 8:9], in1=SC1[:, 7:8], op=ALU.mult)
                E_("dve", "tensor_tensor", reads=(rSC1,), writes=(rSC1,),
                   out=SC1[:, 2 * j + 1:2 * j + 2],
                   in0=SC1[:, 2 * j + 1:2 * j + 2], in1=SC1[:, 8:9], op=ALU.add)
                E_("dve", "tensor_tensor", reads=(rSC1,), writes=(rSC1,),
                   out=SC1[:, 8:9], in0=SC1[:, 2 * j:2 * j + 1],
                   in1=SC1[:, 5:6], op=ALU.subtract)
                E_("dve", "tensor_tensor", reads=(rSC1,), writes=(rSC1,),
                   out=SC1[:, 8:9], in0=SC1[:, 8:9], in1=SC1[:, 7:8], op=ALU.mult)
                E_("dve", "tensor_tensor", reads=(rSC1,), writes=(rSC1,),
                   out=SC1[:, 2 * j:2 * j + 1], in0=SC1[:, 5:6],
                   in1=SC1[:, 8:9], op=ALU.add)
        E_("dve", "tensor_tensor", reads=(rSC1,), writes=(rSC1,),
           out=SC1[:, 5:6], in0=SC1[:, 3:4], in1=SC1[:, 1:2], op=ALU.subtract)
        E_("dve", "tensor_scalar", reads=(rSC1,), writes=(rSC1,),
           out=SC1[:, 5:6], in0=SC1[:, 5:6], scalar1=float(FRAC), scalar2=None,
           op0=ALU.mult)
        E_("dve", "tensor_tensor", reads=(rSC1,), writes=(rSC1,),
           out=SC1[:, 6:7], in0=SC1[:, 1:2], in1=SC1[:, 5:6], op=ALU.add)
        mm(P[2][:, 0:1], ONRF[:, :], SC1[0:1, 6:7], rd=[rONRF, rSC1], wr=[rP[2]],
           start=True, stop=True)
        E_("act", "activation", reads=(rP[2],), writes=(rSCB,),
           out=SCB[:, 1:2], in_=P[2][:, 0:1], func=AF.Identity)
        mm(P[2][:, 0:1], ONRF[:, :], EUS[0:1, 9:10], rd=[rONRF, rEUS], wr=[rP[2]],
           start=True, stop=True)
        E_("act", "activation", reads=(rP[2],), writes=(rSCB,),
           out=SCB[:, 2:3], in_=P[2][:, 0:1], func=AF.Identity)
        E_("dve", "tensor_scalar", reads=(rSM, rSCB), writes=(rSM,),
           out=SM[:, 80:80 + NT], in0=SM[:, 64:64 + NT], scalar1=SCB[:, 1:2],
           scalar2=None, op0=ALU.is_gt)
        E_("dve", "tensor_tensor", reads=(rSM,), writes=(rSM,),
           out=SM[:, 64:64 + NT], in0=SM[:, 80:80 + NT], in1=SM[:, 0:NT], op=ALU.mult)
        E_("dve", "tensor_scalar", reads=(rSM,), writes=(rSM,),
           out=SM[:, 64:64 + NT], in0=SM[:, 64:64 + NT], scalar1=1e-8, scalar2=None,
           op0=ALU.add)
        E_("dve", "reciprocal", reads=(rSM,), writes=(rSM,),
           out=SM[:, 0:NT], in_=SM[:, 64:64 + NT])
        E_("dve", "tensor_tensor", reads=(rSM,), writes=(rSM,),
           out=SM[:, 0:NT], in0=SM[:, 0:NT], in1=SM[:, 80:80 + NT], op=ALU.mult)
        E_("dve", "tensor_scalar", reads=(rSM, rSCB), writes=(rSM,),
           out=SM[:, 0:NT], in0=SM[:, 0:NT], scalar1=SCB[:, 2:3], scalar2=None,
           op0=ALU.mult)
        E_("dve", "tensor_tensor", reads=(rSM,), writes=(rSM,),
           out=SM[:, 16:16 + NT], in0=SM[:, 0:NT], in1=SM[:, 48:48 + NT], op=ALU.mult)
        E_("dve", "tensor_tensor", reads=(rSM,), writes=(rSM,),
           out=SM[:, 0:NT], in0=SM[:, 0:NT], in1=SM[:, 32:32 + NT], op=ALU.mult)

        def accsl(i):
            return AW[:, i * 1024:(i + 1) * 1024]

        def xnTsl(db, t0, t1):
            t_, r_ = (AQ, rAQ) if db < 4 else (AK, rAK)
            return t_[:, (db % 4) * T + t0:(db % 4) * T + t1], r_

        for i in range(NT):
            hb = i % 2
            x1buf = S3[:, hb * 1024:hb * 1024 + 1024]
            E_("sy", "dma_start", reads=(), writes=(S3_HALF[hb],),
               out=x1buf,
               in_=(x1in if mode == "ffn" else x1sp)[i * 128:(i + 1) * 128, :])
            E_("act", "activation", reads=(S3_HALF[hb], rSM), writes=(rS1,),
               out=S1[:, 0:1024], in_=x1buf, func=AF.Identity,
               scale=SM[:, i:i + 1], bias=SM[:, 16 + i:17 + i])
            for db in range(8):
                E_("pe", "transpose", reads=(rS1, rIDB), writes=(rPBp,),
                   out=PB[:, 0:128], in_=S1[:, db * 128:(db + 1) * 128],
                   identity=IDB[:, :])
                dsl, dres = xnTsl(db, i * 128, (i + 1) * 128)
                E_("dve", "tensor_copy", reads=(rPBp,), writes=(dres,),
                   out=dsl, in_=PB[:, 0:128])

        if phases < 6:
            raise _PhaseDone()
        for sl in range(4):
            bi = sl % 2
            E_("gd", "dma_start", reads=(), writes=(rWB[bi],),
               out=WB[:, bi * 2048: bi * 2048 + 2048],
               in_=gw1[:, sl * 256:(sl + 1) * 256].rearrange("(kt p) m -> p kt m", p=128))
            for mt2 in range(2):
                mt = sl * 2 + mt2
                for c in range(NCH):
                    pi = c % 2
                    for kt in range(8):
                        rhs, rr = xnTsl(kt, c * 512, (c + 1) * 512)
                        mm(P[pi][:, :],
                           WB[:, bi * 2048 + kt * 256 + mt2 * 128: bi * 2048 + kt * 256 + (mt2 + 1) * 128],
                           rhs, rd=[rWB[bi], rr], wr=[rP[pi]], start=(kt == 0), stop=(kt == 7))
                    E_("act", "activation", reads=(rP[pi], rCT), writes=(rAV,),
                       out=AV[:, mt * T + c * 512: mt * T + (c + 1) * 512], in_=P[pi][:, :],
                       func=AF.Silu, bias=CT[:, CO["GB1"] + mt:CO["GB1"] + mt + 1])
        for c in range(NCH):
            for mt in range(8):
                mm(PS[:, :], ONEB[:, :], AV[:, mt * T + c * 512: mt * T + (c + 1) * 512],
                   rd=[rONEB, rAV], wr=[rPSp], start=(mt == 0), stop=(mt == 7))
            E_("act", "activation", reads=(rPSp,), writes=(rV1,),
               out=V1[:, c * 512:(c + 1) * 512], in_=PS[:, :], func=AF.Identity)
            for mt in range(8):
                E_("act", "activation", reads=(rAV,), writes=(rS1,),
                   out=S1[:, 0:512], in_=AV[:, mt * T + c * 512: mt * T + (c + 1) * 512],
                   func=AF.Square)
                mm(PS[:, :], ONEB[:, :], S1[:, 0:512], rd=[rONEB, rS1], wr=[rPSp],
                   start=(mt == 0), stop=(mt == 7))
            E_("act", "activation", reads=(rPSp,), writes=(rV2,),
               out=V2[:, c * 512:(c + 1) * 512], in_=PS[:, :], func=AF.Identity)
        V3 = S2[0:1, 0:T]
        E_("dve", "tensor_scalar", reads=(rV1,), writes=(rV1,),
           out=V1[:, :], in0=V1[:, :], scalar1=1.0 / HID, scalar2=None, op0=ALU.mult)
        E_("dve", "tensor_scalar", reads=(rV2,), writes=(rV2,),
           out=V2[:, :], in0=V2[:, :], scalar1=1.0 / HID, scalar2=None, op0=ALU.mult)
        E_("dve", "tensor_tensor", reads=(rV1,), writes=(rS2,),
           out=V3, in0=V1[:, :], in1=V1[:, :], op=ALU.mult)
        E_("dve", "tensor_tensor", reads=(rV2, rS2), writes=(rV2,),
           out=V2[:, :], in0=V2[:, :], in1=V3, op=ALU.subtract)
        E_("dve", "tensor_scalar", reads=(rV2,), writes=(rV2,),
           out=V2[:, :], in0=V2[:, :], scalar1=1e-5, scalar2=None, op0=ALU.add)
        E_("act", "activation", reads=(rV2,), writes=(rV2,),
           out=V2[:, :], in_=V2[:, :], func=AF.Ln)
        E_("act", "activation", reads=(rV2,), writes=(rV2,),
           out=V2[:, :], in_=V2[:, :], func=AF.Exp, scale=-0.5)
        E_("dve", "scalar_tensor_tensor", reads=(rV1, rV2), writes=(rS2,),
           out=V3, in0=V1[:, :], scalar=-1.0, in1=V2[:, :], op0=ALU.mult, op1=ALU.mult)
        for c in range(NCH):
            mm(P[0][:, :], ONRF[:, :], V2[:, c * 512:(c + 1) * 512],
               rd=[rONRF, rV2], wr=[rP[0]], start=True, stop=True)
            mm(P[1][:, :], ONRF[:, :], S2[0:1, c * 512:(c + 1) * 512],
               rd=[rONRF, rS2], wr=[rP[1]], start=True, stop=True)
            for mt in range(8):
                sl_ = AV[:, mt * T + c * 512: mt * T + (c + 1) * 512]
                E_("dve", "tensor_tensor", reads=(rAV, rP[0]), writes=(rS1,),
                   out=S1[:, 0:512], in0=sl_, in1=P[0][:, :], op=ALU.mult)
                E_("dve", "tensor_tensor", reads=(rS1, rP[1]), writes=(rS1,),
                   out=S1[:, 0:512], in0=S1[:, 0:512], in1=P[1][:, :], op=ALU.add)
                E_("act", "activation", reads=(rS1, rCT), writes=(rAV,),
                   out=sl_, in_=S1[:, 0:512], func=AF.Identity,
                   scale=CT[:, CO["LNS"] + mt:CO["LNS"] + mt + 1],
                   bias=CT[:, CO["LNO"] + mt:CO["LNO"] + mt + 1])
        for sl in range(2):
            bi = sl % 2
            E_("gd", "dma_start", reads=(), writes=(rWB[bi],),
               out=WB[:, bi * 2048: bi * 2048 + 2048],
               in_=gw2[:, sl * 256:(sl + 1) * 256].rearrange("(kt p) m -> p kt m", p=128))
            for mt2 in range(2):
                mt = sl * 2 + mt2
                for c in range(NCH):
                    pi = 2 + c % 2
                    for kt in range(8):
                        mm(P[pi][:, :],
                           WB[:, bi * 2048 + kt * 256 + mt2 * 128: bi * 2048 + kt * 256 + (mt2 + 1) * 128],
                           AV[:, kt * T + c * 512: kt * T + (c + 1) * 512],
                           rd=[rWB[bi], rAV], wr=[rP[pi]], start=(kt == 0), stop=(kt == 7))
                    E_("act", "activation", reads=(rP[pi], rCT), writes=(rAG,),
                       out=AG[:, mt * T + c * 512: mt * T + (c + 1) * 512], in_=P[pi][:, :],
                       func=AF.Silu, bias=CT[:, CO["GB2"] + mt:CO["GB2"] + mt + 1])
        E_("sy", "dma_start", reads=(), writes=(rS2,),
           out=S2[0:E, 0:T], in_=noise_d[:, :])
        E_("dve", "tensor_scalar", reads=(rEUS,), writes=(rSC1,),
           out=SC1[:, 16:16 + E], in0=EUS[:, 0:E], scalar1=1e-8, scalar2=None, op0=ALU.add)
        E_("dve", "tensor_reduce", reads=(rSC1,), writes=(rSC1,),
           out=SC1[:, 24:25], in_=SC1[:, 16:16 + E], axis=AX.X, op=ALU.min)
        E_("dve", "tensor_scalar", reads=(rSC1,), writes=(rSC1,),
           out=SC1[:, 25:26], in0=SC1[:, 24:25], scalar1=0.0, scalar2=None, op0=ALU.is_lt)
        E_("dve", "tensor_scalar", reads=(rSC1,), writes=(rSC1,),
           out=SC1[:, 16:16 + E], in0=SC1[:, 16:16 + E], scalar1=1e-30, scalar2=None,
           op0=ALU.max)
        E_("act", "activation", reads=(rSC1,), writes=(rSC1,),
           out=SC1[:, 16:16 + E], in_=SC1[:, 16:16 + E], func=AF.Ln)
        E_("dve", "tensor_tensor", reads=(rSC1, rEUS), writes=(rSC1,),
           out=SC1[:, 16:16 + E], in0=SC1[:, 16:16 + E], in1=EUS[:, 0:E], op=ALU.mult)
        E_("dve", "tensor_reduce", reads=(rSC1,), writes=(rSC1,),
           out=SC1[:, 26:27], in_=SC1[:, 16:16 + E], axis=AX.X, op=ALU.add)
        E_("dve", "tensor_scalar", reads=(rSC1,), writes=(rSC1,),
           out=SC1[:, 26:27], in0=SC1[:, 26:27], scalar1=0.01, scalar2=0.02,
           op0=ALU.mult, op1=ALU.add)
        E_("dve", "tensor_scalar", reads=(rSC1,), writes=(rSC1,),
           out=SC1[:, 27:28], in0=SC1[:, 26:27], scalar1=1e-4, scalar2=None, op0=ALU.max)
        mm(P[2][0:E, 0:1], ONRF[0:1, 0:E], SC1[0:1, 27:28], rd=[rONRF, rSC1],
           wr=[rP[2]], start=True, stop=True)
        E_("act", "activation", reads=(rP[2],), writes=(rSCB,),
           out=SCB[0:E, 3:4], in_=P[2][0:E, 0:1], func=AF.Identity)
        mm(P[2][:, 0:1], ONRF[:, :], SC1[0:1, 25:26], rd=[rONRF, rSC1],
           wr=[rP[2]], start=True, stop=True)
        E_("act", "activation", reads=(rP[2],), writes=(rSCB,),
           out=SCB[:, 4:5], in_=P[2][:, 0:1], func=AF.Identity)
        E_("dve", "tensor_scalar", reads=(rSCB,), writes=(rSCB,),
           out=SCB[:, 5:6], in0=SCB[:, 4:5], scalar1=-1.0, scalar2=1.0,
           op0=ALU.mult, op1=ALU.add)
        for db in range(8):
            dsl, dres = xnTsl(db, 0, T)
            E_("dve", "tensor_reduce", reads=(dres,), writes=(rSCB,),
               out=SCB[:, 6:7], in_=dsl, axis=AX.X, op=ALU.add)
            E_("dve", "tensor_scalar", reads=(rSCB,), writes=(rSCB,),
               out=SCB[:, 7:8], in0=SCB[:, 6:7], scalar1=0.3 / float(T), scalar2=None,
               op0=ALU.mult)
            mm(P[3][0:1, 0:E], SCB[:, 7:8], AFW[:, db * 8:(db + 1) * 8],
               rd=[rSCB, rAFW], wr=[rP[3]], start=(db == 0), stop=(db == 7))
        E_("act", "activation", reads=(rP[3],), writes=(rSC1,),
           out=SC1[:, 40:40 + E], in_=P[3][0:1, 0:E], func=AF.Identity)
        E_("dve", "tensor_tensor", reads=(rSC1, rCT), writes=(rSC1,),
           out=SC1[:, 40:40 + E], in0=SC1[:, 40:40 + E],
           in1=CT[0:1, CO["AFFB"]:CO["AFFB"] + E], op=ALU.add)
        E_("pe", "transpose", reads=(rSC1, rIDF), writes=(rP[2],),
           out=P[2][0:E, 0:1], in_=SC1[0:1, 40:40 + E], identity=IDF[0:1, 0:1])
        E_("act", "activation", reads=(rP[2],), writes=(rSCB,),
           out=SCB[0:E, 0:1], in_=P[2][0:E, 0:1], func=AF.Identity)
        for c in range(NCH):
            for kt in range(4):
                mm(P[5][0:E, :], W3B[:, kt * 8:(kt + 1) * 8],
                   AG[:, kt * T + c * 512: kt * T + (c + 1) * 512],
                   rd=[rW3B, rAG], wr=[rP[5]], start=(kt == 0), stop=(kt == 3))
            E_("act", "activation", reads=(rP[5], rCT), writes=(rLGS,),
               out=LGS[:, :], in_=P[5][0:E, :], func=AF.Identity,
               bias=CT[0:E, CO["GB3"]:CO["GB3"] + 1])
            E_("dve", "tensor_scalar", reads=(rLGS, rSCB), writes=(rLGS,),
               out=LGS[:, :], in0=LGS[:, :], scalar1=SCB[0:E, 0:1], scalar2=None,
               op0=ALU.add)
            E_("dve", "tensor_scalar", reads=(rS2, rSCB), writes=(rS2,),
               out=S2[0:E, c * 512:(c + 1) * 512], in0=S2[0:E, c * 512:(c + 1) * 512],
               scalar1=SCB[0:E, 3:4], scalar2=None, op0=ALU.mult)
            E_("dve", "tensor_tensor", reads=(rLGS, rS2), writes=(rLGS,),
               out=LGS[:, :], in0=LGS[:, :], in1=S2[0:E, c * 512:(c + 1) * 512],
               op=ALU.add)
            E_("dve", "tensor_scalar", reads=(rLGS,), writes=(rLGS,),
               out=LGS[:, :], in0=LGS[:, :], scalar1=1.0 / 1.5, scalar2=None,
               op0=ALU.mult)
            for s4 in range(4):
                i = c * 4 + s4
                E_("pe", "transpose", reads=(rLGS, rIDF), writes=(rP[0],),
                   out=P[0][:, 0:E], in_=LGS[:, s4 * 128:(s4 + 1) * 128],
                   identity=IDF[0:E, 0:E])
                E_("act", "activation", reads=(rP[0],), writes=(rSM,),
                   out=SM[:, 80:88], in_=P[0][:, 0:E], func=AF.Identity)
                E_("dve", "tensor_reduce", reads=(rSM,), writes=(rSM,),
                   out=SM[:, 88:89], in_=SM[:, 80:88], axis=AX.X, op=ALU.max)
                E_("dve", "tensor_scalar", reads=(rSM,), writes=(rS3a,),
                   out=S3[:, 0:8], in0=SM[:, 80:88], scalar1=SM[:, 88:89],
                   scalar2=None, op0=ALU.is_ge)
                E_("dve", "tensor_reduce", reads=(rS3a,), writes=(rSM,),
                   out=SM[:, 89:90], in_=S3[:, 0:8], axis=AX.X, op=ALU.add)
                E_("dve", "tensor_scalar", reads=(rS3a,), writes=(rS3a,),
                   out=S3[:, 8:16], in0=S3[:, 0:8], scalar1=-2e9, scalar2=None,
                   op0=ALU.mult)
                E_("dve", "tensor_tensor", reads=(rSM, rS3a), writes=(rS3a,),
                   out=S3[:, 8:16], in0=SM[:, 80:88], in1=S3[:, 8:16], op=ALU.add)
                E_("dve", "tensor_reduce", reads=(rS3a,), writes=(rSM,),
                   out=SM[:, 91:92], in_=S3[:, 8:16], axis=AX.X, op=ALU.max)
                E_("dve", "tensor_scalar", reads=(rSM,), writes=(rSM,),
                   out=SM[:, 92:93], in0=SM[:, 89:90], scalar1=2.0, scalar2=None,
                   op0=ALU.is_ge)
                E_("dve", "tensor_tensor", reads=(rSM,), writes=(rSM,),
                   out=SM[:, 93:94], in0=SM[:, 88:89], in1=SM[:, 91:92],
                   op=ALU.subtract)
                E_("dve", "tensor_tensor", reads=(rSM,), writes=(rSM,),
                   out=SM[:, 93:94], in0=SM[:, 93:94], in1=SM[:, 92:93], op=ALU.mult)
                E_("dve", "tensor_tensor", reads=(rSM,), writes=(rSM,),
                   out=SM[:, 93:94], in0=SM[:, 91:92], in1=SM[:, 93:94], op=ALU.add)
                E_("dve", "tensor_scalar", reads=(rSM,), writes=(rS3a,),
                   out=S3[:, 0:8], in0=SM[:, 80:88], scalar1=SM[:, 93:94],
                   scalar2=None, op0=ALU.is_ge)
                E_("dve", "tensor_scalar", reads=(rS3a, rSCB), writes=(rS3a,),
                   out=S3[:, 0:8], in0=S3[:, 0:8], scalar1=SCB[:, 5:6], scalar2=None,
                   op0=ALU.mult)
                E_("dve", "tensor_tensor", reads=(rS3a, rSM), writes=(rS3b,),
                   out=S3[:, 16:24], in0=SM[:, 80:88], in1=S3[:, 0:8], op=ALU.mult)
                E_("dve", "tensor_scalar", reads=(rS3a,), writes=(rS3a,),
                   out=S3[:, 24:32], in0=S3[:, 0:8], scalar1=1e9, scalar2=-1e9,
                   op0=ALU.mult, op1=ALU.add)
                E_("dve", "tensor_tensor", reads=(rS3a, rS3b), writes=(rS3b,),
                   out=S3[:, 16:24], in0=S3[:, 16:24], in1=S3[:, 24:32], op=ALU.add)
                E_("dve", "tensor_reduce", reads=(rS3b,), writes=(rSM,),
                   out=SM[:, 94:95], in_=S3[:, 16:24], axis=AX.X, op=ALU.max)
                E_("dve", "tensor_scalar", reads=(rSM,), writes=(rSM,),
                   out=SM[:, 95:96], in0=SM[:, 94:95], scalar1=-1.0, scalar2=None,
                   op0=ALU.mult)
                E_("act", "activation", reads=(rS3b, rSM), writes=(rS3b,),
                   out=S3[:, 16:24], in_=S3[:, 16:24], func=AF.Exp,
                   bias=SM[:, 95:96])
                E_("dve", "tensor_reduce", reads=(rS3b,), writes=(rSM,),
                   out=SM[:, 94:95], in_=S3[:, 16:24], axis=AX.X, op=ALU.add)
                E_("dve", "reciprocal", reads=(rSM,), writes=(rSM,),
                   out=SM[:, 95:96], in_=SM[:, 94:95])
                E_("dve", "tensor_scalar", reads=(rS3b, rSM), writes=(rS3b,),
                   out=S3[:, 16:24], in0=S3[:, 16:24], scalar1=SM[:, 95:96],
                   scalar2=None, op0=ALU.mult)
                E_("dve", "tensor_tensor", reads=(rS3b, rS3a), writes=(rS3a,),
                   out=S3[:, 0:8], in0=S3[:, 16:24], in1=S3[:, 0:8], op=ALU.mult)
                E_("dve", "tensor_scalar", reads=(rNZE, rSCB), writes=(rS3b,),
                   out=S3[:, 24:32], in0=NZE[:, 0:8], scalar1=SCB[:, 4:5],
                   scalar2=None, op0=ALU.mult)
                E_("dve", "tensor_tensor", reads=(rS3a, rS3b), writes=(rPRB,),
                   out=PRB[:, i * E:(i + 1) * E], in0=S3[:, 0:8], in1=S3[:, 24:32],
                   op=ALU.add)
                E_("pe", "transpose", reads=(rPRB, rIDF), writes=(rP[0],),
                   out=P[0][0:E, 0:128], in_=PRB[:, i * E:(i + 1) * E],
                   identity=IDF[:, :])
                E_("act", "activation", reads=(rP[0],), writes=(rPRT,),
                   out=PRT[:, i * 128:(i + 1) * 128], in_=P[0][0:E, 0:128],
                   func=AF.Identity)
        def accd(dt):
            return AW[:, dt * T:(dt + 1) * T]

        for dt_ in range(8):
            xsl, xres = xnTsl(dt_, 0, T)
            E_("act", "activation", reads=(xres,), writes=(rAW,),
               out=accd(dt_), in_=xsl, func=AF.Identity, scale=0.1)
        for dt_ in range(8):
            for c in range(NCH):
                mm(P[0][:, :], BM[:, dt_ * 128:(dt_ + 1) * 128],
                   PRT[:, c * 512:(c + 1) * 512],
                   rd=[rBM, rPRT], wr=[rP[0]], start=True, stop=True)
                E_("dve", "tensor_tensor", reads=(rP[0], rAW), writes=(rAW,),
                   out=accd(dt_)[:, c * 512:(c + 1) * 512],
                   in0=accd(dt_)[:, c * 512:(c + 1) * 512], in1=P[0][:, :], op=ALU.add)

        if phases < 7:
            raise _PhaseDone()
        PO = P

        def hblk(j):
            if j < 8 * T // ETOK:
                return AV[:, j * ETOK:(j + 1) * ETOK], rAV
            jj = j - 8 * T // ETOK
            return AG[:, jj * ETOK:(jj + 1) * ETOK], rAG

        NBH = 8 * T // ETOK

        def h_slab_dma(wdram, e_idx, sl, kh, KH, bi):
            E_("gd", "dma_start", reads=(), writes=(rWB[bi],),
               out=WB[:, bi * 2048: bi * 2048 + 2048],
               in_=wdram[e_idx, kh * 1024:(kh + 1) * 1024, sl * 256:(sl + 1) * 256]
               .rearrange("(kt p) m -> p kt m", p=128))

        def h_stage(wdram, e_idx, KIN, MOUT, actf, biascol, blk0, ec,
                    gelu=False, sigsq=False):
            KH = KIN // 1024
            slabs = MOUT // 256
            seq = [(sl, kh) for sl in range(slabs) for kh in range(KH)]
            h_slab_dma(wdram, e_idx, 0, 0, KH, 0)
            for si, (sl, kh) in enumerate(seq):
                bi = si % 2
                if si + 1 < len(seq):
                    h_slab_dma(wdram, e_idx, seq[si + 1][0], seq[si + 1][1], KH,
                               (si + 1) % 2)
                for mt2 in range(2):
                    mt = sl * 2 + mt2
                    for nc2 in range(ENC):
                        pi = mt2 * ENC + nc2
                        for kt in range(8):
                            gk = kh * 8 + kt
                            if KIN == HID:
                                rhs, rr = xnTsl(gk, ec * ETOK + nc2 * 512,
                                                ec * ETOK + (nc2 + 1) * 512)
                            else:
                                bsl, rr = hblk(gk)
                                rhs = bsl[:, nc2 * 512:(nc2 + 1) * 512]
                            mm(PO[pi][:, :],
                               WB[:, bi * 2048 + kt * 256 + mt2 * 128: bi * 2048 + kt * 256 + (mt2 + 1) * 128],
                               rhs, rd=[rWB[bi], rr], wr=[rP[pi]],
                               start=(kh == 0 and kt == 0), stop=(kh == KH - 1 and kt == 7))
                        if kh != KH - 1:
                            continue
                        bsl, br = hblk(blk0 + mt)
                        dst = bsl[:, nc2 * 512:(nc2 + 1) * 512]
                        if gelu:
                            E_("act", "activation", reads=(rP[pi], rCT), writes=(br,),
                               out=dst, in_=PO[pi][:, :], func=AF.Identity,
                               bias=CT[:, biascol + mt:biascol + mt + 1])
                            E_("act", "activation", reads=(br,), writes=(rS1,),
                               out=S1[:, 0:512], in_=dst, func=AF.Square)
                            E_("dve", "tensor_tensor", reads=(rS1, br), writes=(rS1,),
                               out=S1[:, 0:512], in0=S1[:, 0:512], in1=dst, op=ALU.mult)
                            E_("dve", "tensor_scalar", reads=(rS1,), writes=(rS1,),
                               out=S1[:, 0:512], in0=S1[:, 0:512], scalar1=C2G,
                               scalar2=None, op0=ALU.mult)
                            E_("dve", "tensor_tensor", reads=(rS1, br), writes=(rS1,),
                               out=S1[:, 0:512], in0=S1[:, 0:512], in1=dst, op=ALU.add)
                            E_("act", "activation", reads=(rS1,), writes=(rS1,),
                               out=S1[:, 512:1024], in_=S1[:, 0:512], func=AF.Sigmoid,
                               scale=C1G)
                            E_("dve", "tensor_tensor", reads=(rS1, br), writes=(br,),
                               out=dst, in0=dst, in1=S1[:, 512:1024], op=ALU.mult)
                        elif sigsq:
                            E_("act", "activation", reads=(rP[pi], rCT), writes=(br,),
                               out=dst, in_=PO[pi][:, :], func=AF.Identity,
                               bias=CT[:, biascol + mt:biascol + mt + 1])
                            E_("act", "activation", reads=(br,), writes=(rS1,),
                               out=S1[:, 0:512], in_=dst, func=AF.Sigmoid)
                            E_("dve", "tensor_tensor", reads=(rS1,), writes=(rS1,),
                               out=S1[:, 512:1024], in0=S1[:, 0:512], in1=S1[:, 0:512],
                               op=ALU.mult)
                            E_("dve", "tensor_tensor", reads=(rS1, br), writes=(br,),
                               out=dst, in0=dst, in1=S1[:, 512:1024], op=ALU.mult)
                        else:
                            E_("act", "activation", reads=(rP[pi], rCT), writes=(br,),
                               out=dst, in_=PO[pi][:, :], func=actf,
                               bias=CT[:, biascol + mt:biascol + mt + 1])

        def o_slab_dma(wdram, e_idx, kt, bi):
            E_("gd", "dma_start", reads=(), writes=(rWB[bi],),
               out=WB[:, bi * 2048: bi * 2048 + 1024],
               in_=wdram[e_idx, kt * 128:(kt + 1) * 128, :])

        def out_stage(wdram, e_idx, KIN, gcol, ec, blk0):
            ktn = KIN // 128
            for tt in range(NTT):
                i = ec * NTT + tt
                E_("pe", "transpose", reads=(rPRB, rIDF), writes=(rP[4],),
                   out=P[4][0:1, 0:128], in_=PRB[:, i * E + gcol:i * E + gcol + 1],
                   identity=IDF[:, :])
                E_("act", "activation", reads=(rP[4],), writes=(rV2,),
                   out=V2[:, tt * 128:(tt + 1) * 128], in_=P[4][0:1, 0:128],
                   func=AF.Identity)
            for nc2 in range(ENC):
                mm(P[4 + nc2][:, :], ONRF[:, :], V2[:, nc2 * 512:(nc2 + 1) * 512],
                   rd=[rONRF, rV2], wr=[rP[4 + nc2]], start=True, stop=True)
            khs = [(k0, min(k0 + 16, ktn)) for k0 in range(0, ktn, 16)]
            for dt_ in range(8):
                for hi, (k0, k1) in enumerate(khs):
                    bi = (dt_ * len(khs) + hi) % 2
                    E_("gd", "dma_start", reads=(), writes=(rWB[bi],),
                       out=WB[:, bi * 2048: bi * 2048 + (k1 - k0) * 128],
                       in_=wdram[e_idx, k0 * 128:k1 * 128, dt_ * 128:(dt_ + 1) * 128]
                       .rearrange("(kt p) m -> p kt m", p=128))
                    for nc2 in range(ENC):
                        for kt in range(k0, k1):
                            mm(P[nc2][:, :],
                               WB[:, bi * 2048 + (kt - k0) * 128: bi * 2048 + (kt - k0 + 1) * 128],
                               hblk(blk0 + kt)[0][:, nc2 * 512:(nc2 + 1) * 512],
                               rd=[rWB[bi], hblk(blk0 + kt)[1]], wr=[rP[nc2]],
                               start=(kt == 0), stop=(kt == ktn - 1))
                for nc2 in range(ENC):
                    E_("act", "activation", reads=(rP[nc2],), writes=(rS1,),
                       out=S1[:, nc2 * 512:(nc2 + 1) * 512], in_=P[nc2][:, :],
                       func=AF.Identity)
                    E_("dve", "tensor_tensor", reads=(rS1, rP[4 + nc2]), writes=(rS1,),
                       out=S1[:, nc2 * 512:(nc2 + 1) * 512],
                       in0=S1[:, nc2 * 512:(nc2 + 1) * 512], in1=P[4 + nc2][:, :],
                       op=ALU.mult)
                    E_("dve", "tensor_tensor", reads=(rS1, rAW), writes=(rAW,),
                       out=accd(dt_)[:, ec * ETOK + nc2 * 512: ec * ETOK + (nc2 + 1) * 512],
                       in0=accd(dt_)[:, ec * ETOK + nc2 * 512: ec * ETOK + (nc2 + 1) * 512],
                       in1=S1[:, nc2 * 512:(nc2 + 1) * 512], op=ALU.add)

        for ec in range(NG):
            for e_i in range(3):
                h_stage(e0w1, e_i, HID, 4096, None, CO["E0B1"] + e_i * 32, 0, ec, gelu=True)
                out_stage(e0w2, e_i, 4096, 3 * e_i, ec, 0)
            for e_i in range(2):
                h_stage(e2w1, e_i, HID, 3072, None, CO["E2B1"] + e_i * 24, 0, ec, sigsq=True)
                out_stage(e2w2, e_i, 3072, 3 * e_i + 2, ec, 0)
            for e_i in range(3):
                h_stage(e1w1, e_i, HID, 2048, AF.Silu, CO["E1B1"] + e_i * 16, 0, ec)
                h_stage(e1w2, e_i, 2048, 2048, AF.Silu, CO["E1B2"] + e_i * 16, 16, ec)
                out_stage(e1w3, e_i, 2048, 3 * e_i + 1, ec, 16)

        for c in range(NCH):
            for dt_ in range(8):
                mm(PS[:, :], ONEB[:, :], AW[:, dt_ * T + c * 512: dt_ * T + (c + 1) * 512],
                   rd=[rONEB, rAW], wr=[rPSp], start=(dt_ == 0), stop=(dt_ == 7))
            E_("act", "activation", reads=(rPSp,), writes=(rV1,),
               out=V1[:, c * 512:(c + 1) * 512], in_=PS[:, :], func=AF.Identity)
            for dt_ in range(8):
                E_("act", "activation", reads=(rAW,), writes=(rS1,),
                   out=S1[:, 0:512], in_=AW[:, dt_ * T + c * 512: dt_ * T + (c + 1) * 512],
                   func=AF.Square)
                mm(PS[:, :], ONEB[:, :], S1[:, 0:512], rd=[rONEB, rS1], wr=[rPSp],
                   start=(dt_ == 0), stop=(dt_ == 7))
            E_("act", "activation", reads=(rPSp,), writes=(rV2,),
               out=V2[:, c * 512:(c + 1) * 512], in_=PS[:, :], func=AF.Identity)
        V3f = S2[0:1, 0:T]
        E_("dve", "tensor_scalar", reads=(rV1,), writes=(rV1,),
           out=V1[:, :], in0=V1[:, :], scalar1=1.0 / HID, scalar2=None, op0=ALU.mult)
        E_("dve", "tensor_scalar", reads=(rV2,), writes=(rV2,),
           out=V2[:, :], in0=V2[:, :], scalar1=1.0 / HID, scalar2=None, op0=ALU.mult)
        E_("dve", "tensor_tensor", reads=(rV1,), writes=(rS2,),
           out=V3f, in0=V1[:, :], in1=V1[:, :], op=ALU.mult)
        E_("dve", "tensor_tensor", reads=(rV2, rS2), writes=(rV2,),
           out=V2[:, :], in0=V2[:, :], in1=V3f, op=ALU.subtract)
        E_("dve", "tensor_scalar", reads=(rV2,), writes=(rV2,),
           out=V2[:, :], in0=V2[:, :], scalar1=1e-5, scalar2=None, op0=ALU.add)
        E_("act", "activation", reads=(rV2,), writes=(rV2,),
           out=V2[:, :], in_=V2[:, :], func=AF.Ln)
        E_("act", "activation", reads=(rV2,), writes=(rV2,),
           out=V2[:, :], in_=V2[:, :], func=AF.Exp, scale=-0.5)
        E_("dve", "scalar_tensor_tensor", reads=(rV1, rV2), writes=(rS2,),
           out=V3f, in0=V1[:, :], scalar=-1.0, in1=V2[:, :], op0=ALU.mult, op1=ALU.mult)
        for i in range(NT):
            E_("pe", "transpose", reads=(rV2, rIDF), writes=(rP[0],),
               out=P[0][:, 0:1], in_=V2[:, i * 128:(i + 1) * 128], identity=IDF[0:1, 0:1])
            E_("act", "activation", reads=(rP[0],), writes=(rSM,),
               out=SM[:, 16 + i:17 + i], in_=P[0][:, 0:1], func=AF.Identity)
            E_("pe", "transpose", reads=(rS2, rIDF), writes=(rP[1],),
               out=P[1][:, 0:1], in_=S2[0:1, i * 128:(i + 1) * 128], identity=IDF[0:1, 0:1])
            E_("act", "activation", reads=(rP[1],), writes=(rSM,),
               out=SM[:, 32 + i:33 + i], in_=P[1][:, 0:1], func=AF.Identity)
        for i in range(NT):
            hb = i % 2
            x1buf = S3[:, hb * 1024:hb * 1024 + 1024]
            E_("sy", "dma_start", reads=(), writes=(S3_HALF[hb],),
               out=x1buf,
               in_=(x1in if mode == "ffn" else x1sp)[i * 128:(i + 1) * 128, :])
            for dt_ in range(8):
                E_("pe", "transpose", reads=(rAW, rIDB), writes=(rPBp,),
                   out=PB[:, (dt_ % 2) * 128:(dt_ % 2) * 128 + 128],
                   in_=AW[:, dt_ * T + i * 128: dt_ * T + (i + 1) * 128],
                   identity=IDB[:, :])
                E_("act", "activation", reads=(rPBp, rSM), writes=(rS1,),
                   out=S1[:, dt_ * 128:(dt_ + 1) * 128],
                   in_=PB[:, (dt_ % 2) * 128:(dt_ % 2) * 128 + 128],
                   func=AF.Identity, scale=SM[:, 16 + i:17 + i], bias=SM[:, 32 + i:33 + i])
            E_("dve", "tensor_tensor", reads=(rS1, S3_HALF[hb]), writes=(rS2,),
               out=S2[:, 0:1024], in0=x1buf, in1=S1[:, 0:1024], op=ALU.add)
            E_("sy", "dma_start", reads=(rS2,), writes=(rOUT,),
               out=out_d[i * 128:(i + 1) * 128, :], in_=S2[:, 0:1024])
    except _PhaseDone:
        pass
    for s, v in sd.qdone("sy") + sd.qdone("gd"):
        sd.rawwait("dve", s, v)

    with nc.Block() as block:
        @block.tensor
        def _(e):
            sd.replay("pe", e)

        @block.vector
        def _(e):
            sd.replay("dve", e)

        @block.scalar
        def _(e):
            sd.replay("act", e)

        @block.gpsimd
        def _(e):
            sd.replay("gd", e)

        @block.sync
        def _(e):
            sd.replay("sy", e)

    es.close()
    return nc


def _host_consts(T, NCORES, inputs):
    NT = T // 128
    NDIFF = NT + 3
    CO = ctab_layout(T)
    g = GAMMA
    ctab = np.zeros((128, CO["W"]), dtype=np.float64)
    sl = np.arange(128)
    for h in range(H):
        for di in range(NDIFF):
            d0 = 128 * (di - 3)
            ctab[:, CO["ESC"] + h * NDIFF + di] = \
                np.exp2((d0 - sl) * np.log2(g[h])) * DH ** -0.5
    ctab[:, CO["GNW"]:CO["GNW"] + 8] = inputs["L0_attn_gn_weight"].reshape(8, 128).T
    ctab[:, CO["GNB"]:CO["GNB"] + 8] = inputs["L0_attn_gn_bias"].reshape(8, 128).T
    ctab[:, CO["GB1"]:CO["GB1"] + 8] = inputs["L0_ffn_gate_b1"].reshape(8, 128).T
    ctab[:, CO["GB2"]:CO["GB2"] + 4] = inputs["L0_ffn_gate_b2"].reshape(4, 128).T
    ctab[0:8, CO["GB3"]] = inputs["L0_ffn_gate_b3"]
    ctab[:, CO["E0B1"]:CO["E0B1"] + 96] = inputs["L0_ffn_e0_b1"].reshape(96, 128).T
    ctab[:, CO["E1B1"]:CO["E1B1"] + 48] = inputs["L0_ffn_e1_b1"].reshape(48, 128).T
    ctab[:, CO["E1B2"]:CO["E1B2"] + 48] = inputs["L0_ffn_e1_b2"].reshape(48, 128).T
    ctab[:, CO["E2B1"]:CO["E2B1"] + 48] = inputs["L0_ffn_e2_b1"].reshape(48, 128).T
    ctab[:, CO["LNS"]:CO["LNS"] + 8] = inputs["L0_ffn_ln_scale"].reshape(8, 128).T
    ctab[:, CO["LNO"]:CO["LNO"] + 8] = inputs["L0_ffn_ln_offset"].reshape(8, 128).T
    ctab[0, CO["AFFB"]:CO["AFFB"] + 8] = 0.3 * inputs["L0_ffn_aff_b"]
    ctab = ctab.astype(np.float32)

    inv_freq = 1.0 / 10000.0 ** (np.arange(0, DH, 2, dtype=np.float32) / DH)
    t = np.arange(T, dtype=np.float32)
    emb = np.concatenate([t[:, None] * inv_freq[None, :]] * 2, axis=-1)
    cosv, sinv = np.cos(emb), np.sin(emb)
    trig = np.zeros((64, 2 * T), dtype=np.float32)
    trig[:, 0:T] = cosv.T
    trig[:, T:2 * T] = sinv.T

    masks = np.zeros((128, 4 * 512), dtype=np.float32)
    for k in range(4):
        masks[:, k * 512:(k + 1) * 512] = \
            (np.arange(512)[None, :] >= np.arange(128)[:, None] + 128 * k)

    idn = np.eye(128, dtype=np.float32)

    tmod = np.arange(T) % 512
    epsr = np.stack([1e-5 * np.exp2(-2.0 * tmod * np.log2(g[h])) for h in range(H)]
                    ).astype(np.float32)

    import jax
    import jax.numpy as jnp
    try:
        cpu = jax.devices("cpu")[0]
        with jax.default_device(cpu):
            noise = np.asarray(jax.random.normal(jax.random.key(1),
                                                 (NCORES, T, E), dtype=jnp.float32))
    except Exception:
        noise = np.asarray(jax.random.normal(jax.random.key(1),
                                             (NCORES, T, E), dtype=jnp.float32))

    bm = np.zeros((E, HID), dtype=np.float32)
    for e in range(E):
        fam, inst = e % 3, e // 3
        if fam == 0:
            bm[e] = inputs["L0_ffn_e0_b2"][inst]
        elif fam == 1:
            bm[e] = inputs["L0_ffn_e1_b3"][inst]
        else:
            bm[e] = inputs["L0_ffn_e2_b2"][inst]
    affwT = np.ascontiguousarray(
        inputs["L0_ffn_aff_w"].reshape(8, 128, 8).transpose(1, 0, 2).reshape(128, 64))
    return ctab, trig, masks, idn, epsr, noise, bm, affwT


def kernel(**inputs):
    T = inputs["L0_attn_x"].shape[1]
    NCORES = inputs["L0_attn_x"].shape[0]
    ctab, trig, masks, idn, epsr, noise, bm, affwT = _host_consts(T, NCORES, inputs)
    two = True
    asc = np.ascontiguousarray
    shared = {
        "wqT": asc(inputs["L0_attn_Wq"].T), "wkT": asc(inputs["L0_attn_Wk"].T),
        "wvT": asc(inputs["L0_attn_Wv"].T), "wgT": asc(inputs["L0_attn_Wg"].T),
        "woT": asc(inputs["L0_attn_Wo"].T),
        "gw1": asc(inputs["L0_ffn_gate_w1"]), "gw2": asc(inputs["L0_ffn_gate_w2"]),
        "gw3": asc(inputs["L0_ffn_gate_w3"]), "affwT": affwT,
        "e0w1": asc(inputs["L0_ffn_e0_w1"]), "e0w2": asc(inputs["L0_ffn_e0_w2"]),
        "e1w1": asc(inputs["L0_ffn_e1_w1"]), "e1w2": asc(inputs["L0_ffn_e1_w2"]),
        "e1w3": asc(inputs["L0_ffn_e1_w3"]), "e2w1": asc(inputs["L0_ffn_e2_w1"]),
        "e2w2": asc(inputs["L0_ffn_e2_w2"]), "biasmat": bm, "ctab": ctab,
        "trig": trig, "masks": masks, "idn": idn,
        "eu": asc(inputs["L0_ffn_expert_usage"].reshape(1, E)), "epsrow": epsr,
    }
    if not two:
        nc = build_nc(T, NCORES)
        in_maps = []
        for b in range(NCORES):
            m = dict(shared)
            m["x"] = asc(inputs["L0_attn_x"][b])
            m["noiseT"] = asc(noise[b].T)
            in_maps.append(m)
        res = run_bass_kernel_spmd(nc, in_maps, core_ids=list(range(NCORES)))
        return np.stack([np.asarray(r["out"]) for r in res.results], axis=0)
    nca = build_nc(T, NCORES, mode="attn")
    maps_a = []
    for b in range(NCORES):
        m = dict(shared)
        m["x"] = asc(inputs["L0_attn_x"][b])
        m["noiseT"] = asc(noise[b].T)
        maps_a.append(m)
    ra = run_bass_kernel_spmd(nca, maps_a, core_ids=list(range(NCORES)))
    x1s = [np.asarray(r["x1out"]) for r in ra.results]
    gimp = np.ascontiguousarray(
        np.concatenate([np.asarray(r["impout"]) for r in ra.results], axis=0))
    ncf = build_nc(T, NCORES, mode="ffn")
    maps_f = []
    for b in range(NCORES):
        m = dict(shared)
        m["x"] = asc(inputs["L0_attn_x"][b])
        m["noiseT"] = asc(noise[b].T)
        m["x1in"] = asc(x1s[b])
        m["gimp"] = gimp
        maps_f.append(m)
    rf = run_bass_kernel_spmd(ncf, maps_f, core_ids=list(range(NCORES)))
    return np.stack([np.asarray(r["out"]) for r in rf.results], axis=0)
